# version 08
# baseline (speedup 1.0000x reference)
import math
import numpy as np
import concourse.bass as bass
import concourse.mybir as mybir
from concourse.bass import IndirectOffsetOnAxis
from concourse.bass_utils import run_bass_kernel_spmd
from concourse.collective import flatten_dims_for_collective

F32 = mybir.dt.float32
BF16 = mybir.dt.bfloat16
I32 = mybir.dt.int32
AF = mybir.ActivationFunctionType
ALU = mybir.AluOpType
AX = mybir.AxisListType

ENGINES = ("tensor", "vector", "scalar", "gpsimd", "sync")


class Prog:

    def __init__(self, nc):
        self.nc = nc
        self.steps = {e: [] for e in ENGINES}
        self.sems = {e: nc.alloc_semaphore(f"prog_{e}") for e in ENGINES}
        self.counts = {e: 0 for e in ENGINES}
        self.dma_sems = {}
        self.dma_counts = {}
        self.watermark = {e: {} for e in ENGINES}
        self.writer = {}
        self.readers = {}
        self.n_waits = 0

    def dma_sem(self, name):
        if name not in self.dma_sems:
            self.dma_sems[name] = self.nc.alloc_semaphore(f"dma_{name}")
            self.dma_counts[name] = 0
        return self.dma_sems[name]

    def _sem_by_name(self, name):
        if name in self.dma_sems:
            return self.dma_sems[name]
        return self.sems[name]

    def _edge(self, eng, tok, waits):
        if tok is None:
            return
        sname, val = tok
        if val <= self.watermark[eng].get(sname, 0):
            return
        waits[sname] = max(waits.get(sname, 0), val)

    def step(self, eng, build, reads=(), writes=(), dma=None, n_inc=1, tag=""):
        waits = {}
        for t in reads:
            self._edge(eng, self.writer.get(t), waits)
        for t in writes:
            self._edge(eng, self.writer.get(t), waits)
            for tok in self.readers.get(t, ()):
                self._edge(eng, tok, waits)
        if dma is None:
            self.counts[eng] += 1
            tok = (eng, self.counts[eng])
            sem = self.sems[eng]
            inc_val = 1
        else:
            self.dma_sem(dma)
            self.dma_counts[dma] += 16 * n_inc
            tok = (dma, self.dma_counts[dma])
            sem = self.dma_sems[dma]
            inc_val = 16 * n_inc
        for t in writes:
            self.writer[t] = tok
            self.readers[t] = []
        for t in reads:
            self.readers.setdefault(t, []).append(tok)
        for sname, val in waits.items():
            self.watermark[eng][sname] = val
        self.n_waits += len(waits)
        self.steps[eng].append((dict(waits), build, sem, inc_val, tag))
        return tok

    def seal_dma(self, stream, tiles):
        tok = (stream, self.dma_counts[stream])
        for t in tiles:
            self.writer[t] = tok
            self.readers[t] = []

    def fence(self, new_names, old_names):
        toks = []
        for o in old_names:
            w = self.writer.pop(o, None)
            if w:
                toks.append(w)
            toks.extend(self.readers.pop(o, []))
        for nn in new_names:
            self.readers.setdefault(nn, []).extend(toks)

    def wait_tokens(self, eng, toks):
        waits = {}
        for tok in toks:
            self._edge(eng, tok, waits)
        for sname, val in waits.items():
            self.watermark[eng][sname] = val
        if waits:
            self.steps[eng].append((waits, None, None, 0, "wait"))
        self.n_waits += len(waits)

    def emit(self, block):
        def mk(eng_name):
            def run(eng):
                for waits, build, sem, inc_val, tag in self.steps[eng_name]:
                    for sname, val in waits.items():
                        eng.wait_ge(self._sem_by_name(sname), val)
                    if build is not None:
                        last = build(eng)
                        last.then_inc(sem, inc_val)
            return run
        block.tensor(mk("tensor"))
        block.vector(mk("vector"))
        block.scalar(mk("scalar"))
        block.gpsimd(mk("gpsimd"))
        block.sync(mk("sync"))


B, L, D = 16, 4096, 1024
NCORES = 8
BPC = B // NCORES
TPC = BPC * L
T = 1024
NCHUNK = TPC // T
NFC = D // 128
NTG = T // 128
NG = NCHUNK * NTG
H, DH = 16, 64
E, FF, CAP = 8, 4096, 640
NMF = FF // 128
NSLOT = E * CAP
EPS = 1e-5
BIG = 1.0e6
GC1, GC2 = 0.7978845608028654, 0.044715 * 0.7978845608028654
RG = [list(range(NCORES))]
KB = 1024

A_NAMES = [f"aw{s}{n}" for s in range(2) for n in range(2)]
B_NAMES = [f"bw{s}{n}" for s in range(2) for n in range(2)]
ZCQ_NAMES = ["zc0", "zc1", "zq0", "zq1"]
XFM_NAMES = [f"x_fm{q}" for q in range(4)]
SCR_NAMES = [f"scr{i}" for i in range(4)]
XT_NAMES = ["xt0", "xt1"]
SLAB_NAMES = [f"slab{i}_{n}" for i in range(8) for n in range(2)]
B16_NAMES = [f"b16_{i}_{n}" for i in range(8) for n in range(2)]
HID_NAMES = [f"hid{m}" for m in range(32)]
Y2_NAMES = [f"y2_{t}" for t in range(8)]
HE_NAMES = [f"he{m}" for m in range(32)]
MFM_NAMES = [f"mfm{fc}" for fc in range(8)]
EOFM_NAMES = [f"eo{m}" for m in range(8)]


def _tile4(w):
    O, I = w.shape
    mc, kc = O // 128, I // 128
    t = w.reshape(mc, 128, kc, 128)
    return np.ascontiguousarray(t.transpose(0, 3, 2, 1))


def _rhs_tiles(w):
    I, N = w.shape
    kc = I // 128
    t = w.reshape(kc, 128, N)
    return np.ascontiguousarray(t.transpose(1, 0, 2))


def _cols(v):
    return np.ascontiguousarray(v.reshape(-1, 128).T)


def host_prepare(inputs):
    emb = np.asarray(inputs["L0_attn_emb"], dtype=np.float32)
    att = {k[8:]: np.asarray(v, dtype=np.float32)
           for k, v in inputs.items() if k.startswith("L0_attn_")}
    ffn = {k[7:]: np.asarray(v, dtype=np.float32)
           for k, v in inputs.items() if k.startswith("L0_ffn_")}
    shared = {
        "waT": _tile4(att["Wa_w"]), "wbT": _tile4(att["Wb_w"]),
        "woT": _tile4(att["Wo_w"]), "cT": _tile4(att["C_w"]),
        "f1T": _tile4(att["f1_w"]), "f2T": _tile4(att["f2_w"]),
        "pT": _rhs_tiles(np.ascontiguousarray(att["p_w"].T)),
        "praw": _tile4(np.ascontiguousarray(att["p_w"].T)),
        "gw": None,
        "gb": None,
        "ab": _cols(att["Wa_b"]), "bb": _cols(att["Wb_b"]),
        "cb": _cols(att["C_b"]), "wob": _cols(att["Wo_b"]),
        "lng": _cols(att["ln_g"]), "lnb": _cols(att["ln_b"]),
        "f1b": _cols(att["f1_b"]), "f2b": _cols(att["f2_b"]),
        "pb": att["p_b"].reshape(1, D), "pbc": _cols(att["p_b"]),
        "svec": _cols(np.repeat(att["head_weights"], DH)),
        "ident": np.eye(128, dtype=np.float32),
    }
    shared["gw"] = _rhs_tiles(ffn["gate_w"])
    shared["gb"] = ffn["gate_b"].reshape(1, E)
    per_core = []
    for r in range(NCORES):
        sh = emb[2 * r:2 * r + 2]
        d = dict(shared)
        d["x_tok"] = np.ascontiguousarray(sh)
        d["x_fm"] = np.ascontiguousarray(sh.transpose(0, 2, 1))
        d["w1r"] = _tile4(np.ascontiguousarray(ffn["w1"][r].T))
        d["w2r"] = _tile4(np.ascontiguousarray(ffn["w2"][r].T))
        d["b1r"] = _cols(ffn["b1"][r])
        d["b2r"] = _cols(ffn["b2"][r])
        d["trimask"] = (np.arange(NCORES) < r).astype(np.float32).reshape(NCORES, 1)
        per_core.append(d)
    return per_core


def build_nc(n_chunks=NCHUNK, dbg=False, dumps=()):
    nc = bass.Bass("TRN2", target_bir_lowering=False, debug=False,
                   num_devices=NCORES)

    di = {}
    def dram_in(name, shape):
        di[name] = nc.dram_tensor(name, list(shape), F32, kind="ExternalInput")
    dram_in("x_tok", (BPC, L, D)); dram_in("x_fm", (BPC, D, L))
    for n in ("waT", "wbT", "woT", "cT", "praw"):
        dram_in(n, (NFC, 128, NFC, 128))
    dram_in("f1T", (NMF, 128, NFC, 128)); dram_in("f2T", (NFC, 128, NMF, 128))
    dram_in("w1r", (NMF, 128, NFC, 128)); dram_in("w2r", (NFC, 128, NMF, 128))
    dram_in("pT", (128, NFC, D)); dram_in("gw", (128, NFC, E)); dram_in("gb", (1, E))
    for n in ("ab", "bb", "cb", "wob", "lng", "lnb", "f2b", "svec", "pbc", "b2r"):
        dram_in(n, (128, NFC))
    dram_in("f1b", (128, NMF)); dram_in("b1r", (128, NMF))
    dram_in("pb", (1, D)); dram_in("ident", (128, 128))
    dram_in("trimask", (NCORES, 1))

    out_d = nc.dram_tensor("out", [BPC, L, D], F32, kind="ExternalOutput")
    x1sp = nc.dram_tensor("x1sp", [TPC, D], F32,
                          kind="ExternalOutput" if dbg else "Internal")
    x1spb = nc.dram_tensor("x1spb", [TPC, D], BF16, kind="Internal")
    disp = nc.dram_tensor("disp", [NSLOT + 1, D], BF16, kind="Internal")
    disp_my = nc.dram_tensor("disp_my", [CAP, D], BF16, kind="Internal")
    cnt_in = nc.dram_tensor("cnt_in", [1, E], F32, kind="Internal")
    cnt_out = nc.dram_tensor("cnt_out", [NCORES, E], F32, kind="Internal")
    eo_my = nc.dram_tensor("eo_my", [CAP, D], F32, kind="Internal")
    wcache = {n: nc.dram_tensor("wc_" + n, [128, NFC, D], BF16, kind="Internal")
              for n in ("waT", "wbT", "cT", "woT", "pT")}
    f1c_d = nc.dram_tensor("wc_f1", [NMF, 128, NFC, 128], BF16, kind="Internal")
    f2c_d = nc.dram_tensor("wc_f2", [NFC, 128, NMF, 128], BF16, kind="Internal")
    eo_full = nc.dram_tensor("eo_full", [NSLOT + 1, D], F32, kind="Internal")

    sb = {}
    def S(name, shape, dt=F32):
        sb[name] = nc.alloc_sbuf_tensor("s_" + name, list(shape), dt)
    for n in ("ab", "bb", "cb", "wob", "cwob", "lng", "lnb", "f2b", "svec",
              "b2r", "pbc"):
        S("c_" + n, (128, NFC))
    S("c_f1b", (128, NMF)); S("c_b1r", (128, NMF))
    S("c_f1be", (128, NMF)); S("c_f1bh", (128, NMF))
    S("pb_row", (1, D)); S("gb_row", (1, E))
    S("ones_row", (1, 128)); S("ones_col", (128, 1)); S("ones8", (128, E))
    S("ones_col_b", (128, 1), BF16)
    S("ident_f", (128, 128)); S("ident_b", (128, 128), BF16)
    S("gw_b", (128, NFC, E), BF16); S("G_b", (128, NFC, E), BF16)
    S("gw_f", (128, NFC, E))
    S("pbgw_row", (1, E)); S("gwsum_bc", (128, E)); S("gb_bc", (128, E))
    S("iota8", (128, E)); S("oadj_bc", (128, E))
    S("carry_h", (128, NFC)); S("carry_c", (E, 1))
    S("trimask", (E, 1)); S("cnts", (E, E)); S("orow", (1, E))
    S("ones_row8", (E, T))
    S("pos_all", (128, NG)); S("eid_all", (128, NG)); S("o_all", (128, NG))
    S("gp_all", (128, NG)); S("gpk_all", (128, NG))
    S("slot_f", (128, NG)); S("slot_i", (128, NG), I32)
    S("fm_all", (128, NG, E))
    S("rt_negmu", (128, 2, NTG)); S("lnvar", (128, NTG)); S("lnvar2", (128, NTG))
    S("rt_istd", (128, 2, NTG))
    S("rowA", (1, T)); S("rowB", (1, T))
    S("emaj", (E, T)); S("emaj_cum", (E, T))
    S("logit_all", (128, 2, NTG, E)); S("lmax", (128, 2, NTG)); S("nlmax", (128, 2, NTG))
    S("lsum", (128, 2, NTG))
    S("cum_tok", (128, NTG, E))
    S("scr8a", (128, 2, NTG, E)); S("scr8b", (128, 2, NTG, E)); S("scr8c", (128, 2, NTG, E))

    ARENA_KB = 172
    rng_ = nc.bump_sbuf(ARENA_KB * KB)
    assert rng_ is not None, "SBUF arena allocation failed"
    base = rng_[0]
    def SA(name, shape, dt, off_kb):
        sb[name] = nc.alloc_sbuf_tensor_at("s_" + name, list(shape), dt,
                                           offset=base + off_kb * KB)
    SA("x_fm", (128, NFC, T), BF16, 0)
    SA("w_a", (128, NFC, D), BF16, 16)
    SA("w_b", (128, NFC, D), BF16, 32)
    SA("moe_tok", (128, 5, D), BF16, 0)
    SA("moe_fm", (128, NFC, CAP), BF16, 10)
    SA("eo_fm", (128, NFC, CAP), F32, 20)
    SA("eo_tok", (128, 2, D), F32, 40)
    SA("f1col", (128, 4, NFC, 128), BF16, 48)
    SA("f2col", (128, 2, NMF, 128), BF16, 56)
    SA("slab", (128, NFC, T), F32, 72)
    SA("b16", (128, NFC, T), BF16, 104)
    SA("he", (128, NMF, CAP), BF16, 72)
    SA("y2", (128, NTG, D), F32, 120)
    SA("hid", (128, NMF, 512), BF16, 120)
    SA("a_sl", (128, 2, T), F32, 152)
    SA("zc_sl", (128, 2, T), BF16, 152)
    SA("zq_sl", (128, 2, T), BF16, 156)
    SA("x1_sl", (128, 2, D), F32, 152)
    SA("b_sl", (128, 2, T), BF16, 160)
    SA("sc_b", (128, 2, D), BF16, 160)
    SA("w_p", (128, NFC, D), BF16, 120)
    SA("sc_bB", (128, 2, T), BF16, 152)
    SA("sc_bC", (128, 2, T), BF16, 156)
    SA("x1q2", (128, 4, T), F32, 32)
    SA("f1e", (128, 16, NFC, 128), BF16, 120)
    SA("x1q", (128, 4, T), F32, 152)
    SA("eogq", (128, 4, T), F32, 16)
    SA("scr", (128, 4, 512), F32, 164)
    SA("xt_sl", (128, 2, D), F32, 164)
    SA("eog", (128, 2, D), F32, 164)

    ps = [nc.alloc_psum_tensor(f"ps{i}", [128, 512], F32) for i in range(6)]
    psB = nc.alloc_psum_tensor("psB", [128, 128], BF16)
    psT = nc.alloc_psum_tensor("psT", [128, 128], F32)

    p = Prog(nc)
    dump_d = {}
    def dump(stage, tile, names, cast=False, c=0):
        if stage not in dumps or c != 0:
            return
        shp = [int(x) for x in sb[tile].shape]
        dump_d[stage] = nc.dram_tensor("dump_" + stage, shp, F32, kind="ExternalOutput")
        eng = "gpsimd" if cast else "sync"
        p.step(eng, (lambda stage=stage, tile=tile: lambda e: e.dma_start(
            out=dump_d[stage].ap(), in_=sb[tile][:]))(),
            reads=list(names), dma="dmp")

    small_loads = [("c_ab", "ab"), ("c_bb", "bb"), ("c_cb", "cb"), ("c_wob", "wob"),
                   ("c_lng", "lng"), ("c_lnb", "lnb"), ("c_f2b", "f2b"),
                   ("c_svec", "svec"), ("c_b2r", "b2r"), ("c_pbc", "pbc"),
                   ("c_f1b", "f1b"), ("c_b1r", "b1r"),
                   ("pb_row", "pb"), ("gb_row", "gb"), ("ident_f", "ident"),
                   ("trimask", "trimask")]
    for tn, dn in small_loads:
        p.step("sync", (lambda tn=tn, dn=dn: lambda e: e.dma_start(out=sb[tn][:], in_=di[dn].ap()))(),
               writes=[tn], dma="lds")
    p.step("gpsimd", lambda e: e.dma_start(out=sb["ident_b"][:], in_=di["ident"].ap()),
           writes=["ident_b"], dma="ldg")
    p.step("gpsimd", lambda e: e.dma_start(out=sb["gw_b"][:], in_=di["gw"].ap()),
           writes=["gw_b"], dma="ldg")
    p.step("sync", lambda e: e.dma_start(out=sb["gw_f"][:], in_=di["gw"].ap()),
           writes=["gw_f"], dma="lds")
    p.seal_dma("lds", [tn for tn, _ in small_loads] + ["gw_f"])
    p.seal_dma("ldg", ["ident_b", "gw_b"])

    p.step("vector", lambda v: v.tensor_add(sb["c_cwob"][:], sb["c_cb"][:], sb["c_wob"][:]),
           reads=["c_cb", "c_wob"], writes=["c_cwob"])
    p.step("vector", lambda v: v.tensor_scalar_mul(sb["c_f1be"][:], sb["c_f1b"][:], 1.0 / math.sqrt(2.0)),
           reads=["c_f1b"], writes=["c_f1be"])
    p.step("vector", lambda v: v.tensor_scalar_mul(sb["c_f1bh"][:], sb["c_f1b"][:], 0.5),
           reads=["c_f1b"], writes=["c_f1bh"])
    for tn, val in [("ones_row", 1.0), ("ones_col", 1.0), ("ones_col_b", 1.0), ("ones8", 1.0),
                    ("ones_row8", 1.0), ("carry_h", 0.0), ("carry_c", 0.0)]:
        p.step("vector", (lambda tn=tn, val=val: lambda v: v.memset(sb[tn][:], val))(), writes=[tn])
    p.step("gpsimd", lambda g: g.iota(sb["iota8"][:], pattern=[[1, E]], base=0,
                                      channel_multiplier=0, allow_small_or_imprecise_dtypes=True),
           writes=["iota8"])

    p.step("vector", lambda v: v.memset(sb["sc_b"][:, 0], 0.0), writes=["sc_b0"])
    dzap = disp.ap()[0:NSLOT, :].rearrange("(n p) d -> n p d", p=128)
    for i in range(NSLOT // 128):
        p.step("sync", (lambda i=i: lambda e: e.dma_start(out=dzap[i], in_=sb["sc_b"][:, 0]))(),
               reads=["sc_b0"], writes=[], dma="dz")
    p.seal_dma("dz", ["disp"])
    disp_zeroed = ("dz", p.dma_counts["dz"])

    p.step("gpsimd", lambda g: g.dma_start(
        out=sb["w_a"][:], in_=di["praw"].ap().rearrange("a p b c -> p a (b c)")),
        writes=["w_a"], dma="ldwa")
    for jc in range(NFC):
        def gmm(t, jc=jc):
            for oc in range(NFC):
                last = t.matmul(psT[:, 0:E], sb["w_a"][:, jc, oc * 128:(oc + 1) * 128],
                                sb["gw_b"][:, oc, :], start=(oc == 0), stop=(oc == NFC - 1))
            return last
        p.step("tensor", gmm, reads=["w_a", "gw_b"], writes=["psT"])
        p.step("vector", (lambda jc=jc: lambda v: v.tensor_copy(sb["G_b"][:, jc, :], psT[:, 0:E]))(),
               reads=["psT"], writes=["G_b"])
    def pbgw_mm(t):
        for kc in range(NFC):
            last = t.matmul(psT[0:1, 0:E], sb["c_pbc"][:, kc:kc + 1], sb["gw_f"][:, kc, :],
                            start=(kc == 0), stop=(kc == NFC - 1))
        return last
    p.step("tensor", pbgw_mm, reads=["c_pbc", "gw_f"], writes=["psT"])
    p.step("vector", lambda v: v.tensor_copy(sb["pbgw_row"][:], psT[0:1, 0:E]),
           reads=["psT"], writes=["pbgw_row"])
    def gwsum_mm(t):
        for kc in range(NFC):
            last = t.matmul(psT[0:1, 0:E], sb["ones_col"][:], sb["gw_f"][:, kc, :],
                            start=(kc == 0), stop=(kc == NFC - 1))
        return last
    p.step("tensor", gwsum_mm, reads=["ones_col", "gw_f"], writes=["psT"])
    p.step("vector", lambda v: v.tensor_copy(sb["rowA"][0:1, 0:E], psT[0:1, 0:E]),
           reads=["psT"], writes=["rowA"])
    def bc_mm(t):
        t.matmul(psT[:, 0:E], sb["ones_row"][:], sb["rowA"][0:1, 0:E], start=True, stop=True)
        return t.matmul(psT[:, E:2 * E], sb["ones_row"][:], sb["gb_row"][:], start=True, stop=True)
    p.step("tensor", bc_mm, reads=["ones_row", "rowA", "gb_row"], writes=["psT"])
    p.step("vector", lambda v: v.tensor_copy(sb["gwsum_bc"][:], psT[:, 0:E]),
           reads=["psT"], writes=["gwsum_bc"])
    p.step("vector", lambda v: v.tensor_copy(sb["gb_bc"][:], psT[:, E:2 * E]),
           reads=["psT"], writes=["gb_bc"])

    def load_w(wn, region, c):
        stream = "ldwa" if region == "w_a" else "ldwb"
        if c == 0:
            if wn == "pT":
                p.step("gpsimd", lambda g: g.dma_start(out=sb[region][:], in_=di["pT"].ap()),
                       writes=[region], dma=stream)
            else:
                p.step("gpsimd", (lambda wn=wn, region=region: lambda g: g.dma_start(
                    out=sb[region][:], in_=di[wn].ap().rearrange("a p b c -> p a (b c)")))(),
                    writes=[region], dma=stream)
            p.step("sync", (lambda wn=wn, region=region: lambda e: e.dma_start(
                out=wcache[wn].ap(), in_=sb[region][:]))(),
                reads=[region], writes=[f"wc_{wn}"], dma="wcw")
        else:
            p.step("sync", (lambda wn=wn, region=region: lambda e: e.dma_start(
                out=sb[region][:], in_=wcache[wn].ap()))(),
                reads=[f"wc_{wn}"], writes=[region], dma=stream)

    rot = [0]
    def nxt():
        rot[0] = (rot[0] + 1) % 4
        return rot[0]

    def plan_routing_dve(c):
        par = c % 2
        LA = sb["logit_all"]
        for tg in range(NTG):
            p.step("vector", (lambda tg=tg, par=par: lambda v: v.tensor_scalar(
                LA[:, par, tg, :], LA[:, par, tg, :],
                sb["rt_istd"][:, par, tg:tg + 1], None, ALU.mult))(),
                reads=[f"pre{par}_{tg}", f"st{par}"], writes=[f"pre{par}_{tg}"], tag="lg_ip")
            p.step("vector", (lambda tg=tg, par=par: lambda v: v.tensor_scalar(
                sb["scr8a"][:, par, tg, :], sb["gwsum_bc"][:],
                sb["rt_negmu"][:, par, tg:tg + 1], sb["rt_istd"][:, par, tg:tg + 1],
                ALU.mult, ALU.mult))(),
                reads=["gwsum_bc", f"nm{par}", f"st{par}"], writes=[f"s8a{par}_{tg}"])
            p.step("vector", (lambda tg=tg, par=par: lambda v: v.tensor_tensor(
                LA[:, par, tg, :], LA[:, par, tg, :], sb["scr8a"][:, par, tg, :],
                ALU.add))(),
                reads=[f"pre{par}_{tg}", f"s8a{par}_{tg}"], writes=[f"pre{par}_{tg}"], tag="lg2_ip")
            p.step("vector", (lambda tg=tg, par=par: lambda v: v.tensor_tensor(
                LA[:, par, tg, :], LA[:, par, tg, :], sb["gb_bc"][:],
                ALU.add))(),
                reads=[f"pre{par}_{tg}", "gb_bc"], writes=[f"pre{par}_{tg}"], tag="lg3_ip")
        p.step("vector", (lambda par=par: lambda v: v.tensor_reduce(
            sb["lmax"][:, par, :], LA[:, par], AX.X, ALU.max))(),
            reads=[f"pre{par}_{tg}" for tg in range(NTG)], writes=[f"lmax{par}"])
        p.step("vector", (lambda par=par: lambda v: v.tensor_scalar_mul(
            sb["nlmax"][:, par, :], sb["lmax"][:, par, :], -0.5))(),
            reads=[f"lmax{par}"], writes=[f"nlmax{par}"])
        for tg in range(NTG):
            gidx = c * NTG + tg
            p.step("vector", (lambda tg=tg, par=par: lambda v: v.tensor_scalar(
                sb["scr8a"][:, par, tg, :], LA[:, par, tg, :],
                sb["lmax"][:, par, tg:tg + 1], None, ALU.is_equal))(),
                reads=[f"pre{par}_{tg}", f"lmax{par}"], writes=[f"s8a{par}_{tg}"])
            p.step("vector", (lambda tg=tg, par=par: lambda v: v.tensor_tensor_scan(
                sb["scr8b"][:, par, tg, :], sb["ones8"][:], sb["scr8a"][:, par, tg, :],
                0.0, ALU.mult, ALU.add))(),
                reads=[f"s8a{par}_{tg}", "ones8"], writes=[f"s8b{par}_{tg}"])
            p.step("vector", (lambda tg=tg, par=par: lambda v: v.tensor_scalar(
                sb["scr8c"][:, par, tg, :], sb["scr8b"][:, par, tg, :], 1.0, None, ALU.is_equal))(),
                reads=[f"s8b{par}_{tg}"], writes=[f"s8c{par}_{tg}"])
            p.step("vector", (lambda tg=tg, par=par, gidx=gidx: lambda v: v.tensor_tensor(
                sb["fm_all"][:, gidx, :], sb["scr8a"][:, par, tg, :], sb["scr8c"][:, par, tg, :],
                ALU.mult))(),
                reads=[f"s8a{par}_{tg}", f"s8c{par}_{tg}"], writes=[f"fm{gidx}"])
            p.step("scalar", (lambda tg=tg, par=par: lambda s: s.activation(
                sb["scr8b"][:, par, tg, :], LA[:, par, tg, :], AF.Tanh,
                bias=sb["nlmax"][:, par, tg:tg + 1], scale=0.5))(),
                reads=[f"pre{par}_{tg}", f"nlmax{par}"], writes=[f"s8b{par}_{tg}"])
            p.step("vector", (lambda tg=tg, par=par: lambda v: v.tensor_scalar(
                sb["scr8c"][:, par, tg, :], sb["scr8b"][:, par, tg, :], -1.0, 1.0,
                ALU.mult, ALU.add))(),
                reads=[f"s8b{par}_{tg}"], writes=[f"s8c{par}_{tg}"])
            p.step("vector", (lambda tg=tg, par=par: lambda v: v.reciprocal(
                sb["scr8c"][:, par, tg, :], sb["scr8c"][:, par, tg, :]))(),
                reads=[f"s8c{par}_{tg}"], writes=[f"s8c{par}_{tg}"], tag="rcp_ip")
            p.step("vector", (lambda tg=tg, par=par: lambda v: v.tensor_scalar(
                sb["scr8b"][:, par, tg, :], sb["scr8b"][:, par, tg, :], 1.0, 1.0,
                ALU.mult, ALU.add))(),
                reads=[f"s8b{par}_{tg}"], writes=[f"s8b{par}_{tg}"], tag="onep_ip")
            p.step("vector", (lambda tg=tg, par=par: lambda v: v.tensor_tensor(
                sb["scr8b"][:, par, tg, :], sb["scr8b"][:, par, tg, :],
                sb["scr8c"][:, par, tg, :], ALU.mult))(),
                reads=[f"s8b{par}_{tg}", f"s8c{par}_{tg}"], writes=[f"s8b{par}_{tg}"], tag="expv_ip")
            p.step("vector", (lambda tg=tg, par=par: lambda v: v.tensor_reduce(
                sb["lsum"][:, par, tg:tg + 1], sb["scr8b"][:, par, tg, :], AX.X, ALU.add))(),
                reads=[f"s8b{par}_{tg}"], writes=[f"ls{par}_{tg}"])
            p.step("vector", (lambda tg=tg, par=par, gidx=gidx: lambda v: v.tensor_tensor(
                sb["scr8b"][:, par, tg, :], sb["fm_all"][:, gidx, :], sb["iota8"][:], ALU.mult))(),
                reads=[f"fm{gidx}", "iota8"], writes=[f"s8b{par}_{tg}"])
            p.step("vector", (lambda tg=tg, par=par, gidx=gidx: lambda v: v.tensor_reduce(
                sb["eid_all"][:, gidx:gidx + 1], sb["scr8b"][:, par, tg, :], AX.X, ALU.add))(),
                reads=[f"s8b{par}_{tg}"], writes=[f"eid{gidx}"])
        p.step("vector", (lambda c=c, par=par: lambda v: v.reciprocal(
            sb["gp_all"][:, c * NTG:(c + 1) * NTG], sb["lsum"][:, par, :]))(),
            reads=[f"ls{par}_{tg}" for tg in range(NTG)], writes=["gp_all"])

    def plan_routing_pe(c):
        par = c % 2
        tbufs = [(psT[0:E, 0:128], "psT"), (ps[4][0:E, 0:128], "ps4"),
                 (ps[5][0:E, 0:128], "ps5")]
        tbufs2 = [(psT[:, 0:E], "psT"), (ps[4][:, 0:E], "ps4"), (ps[5][:, 0:E], "ps5")]
        for tg in range(NTG):
            gidx = c * NTG + tg
            buf, bn = tbufs[tg % 3]
            p.step("tensor", (lambda gidx=gidx, buf=buf: lambda t: t.transpose(
                buf, sb["fm_all"][:, gidx, :], sb["ident_f"][:]))(),
                reads=[f"fm{gidx}", "ident_f"], writes=[bn])
            p.step("vector", (lambda tg=tg, buf=buf: lambda v: v.tensor_copy(
                sb["emaj"][:, tg * 128:(tg + 1) * 128], buf))(),
                reads=[bn], writes=[f"em{tg}"])
        p.step("vector", lambda v: v.tensor_tensor_scan(
            sb["emaj_cum"][:], sb["ones_row8"][:], sb["emaj"][:], sb["carry_c"][:],
            ALU.mult, ALU.add),
            reads=[f"em{tg}" for tg in range(NTG)] + ["ones_row8", "carry_c"],
            writes=["emaj_cum"])
        p.step("vector", lambda v: v.tensor_copy(sb["carry_c"][:], sb["emaj_cum"][:, T - 1:T]),
               reads=["emaj_cum"], writes=["carry_c"])
        for tg in range(NTG):
            gidx = c * NTG + tg
            buf, bn = tbufs2[tg % 3]
            p.step("tensor", (lambda tg=tg, buf=buf: lambda t: t.transpose(
                buf, sb["emaj_cum"][:, tg * 128:(tg + 1) * 128],
                sb["ident_f"][0:E, 0:E]))(),
                reads=["emaj_cum", "ident_f"], writes=[bn])
            p.step("vector", (lambda tg=tg, buf=buf: lambda v: v.tensor_copy(
                sb["cum_tok"][:, tg, :], buf))(),
                reads=[bn], writes=[f"ct{tg}"])
            p.step("vector", (lambda tg=tg, par=par, gidx=gidx: lambda v: v.tensor_tensor(
                sb["scr8b"][:, par, tg, :], sb["fm_all"][:, gidx, :], sb["cum_tok"][:, tg, :],
                ALU.mult))(),
                reads=[f"fm{gidx}", f"ct{tg}"], writes=[f"s8b{par}_{tg}"])
            p.step("vector", (lambda tg=tg, par=par, gidx=gidx: lambda v: v.tensor_reduce(
                sb["pos_all"][:, gidx:gidx + 1], sb["scr8b"][:, par, tg, :], AX.X, ALU.add))(),
                reads=[f"s8b{par}_{tg}"], writes=[f"posr{gidx}"])

    for c in range(n_chunks):
        par = c % 2
        bat, l0 = divmod(c * T, L)
        if c > 0:
            plan_routing_dve(c - 1)
        if l0 == 0 and c > 0:
            p.step("vector", lambda v: v.memset(sb["carry_h"][:], 0.0), writes=["carry_h"])
        for q in range(4):
            p.step("gpsimd", (lambda bat=bat, l0=l0, q=q: lambda g: g.dma_start(
                out=sb["x_fm"][:, 2 * q:2 * q + 2, :],
                in_=di["x_fm"].ap()[bat].rearrange("(fc p) l -> p fc l", p=128)
                [:, 2 * q:2 * q + 2, l0:l0 + T]))(),
                writes=[f"x_fm{q}"], dma=f"ldx{q}")
        load_w("waT", "w_a", c); load_w("wbT", "w_b", c)

        p.fence(A_NAMES, ZCQ_NAMES)
        if c == 0:
            p.fence(B_NAMES, ["sc_b0", "sc_b1"])
        for m in range(NFC):
            sl = m % 2
            for which in ("a", "b"):
                wt = "w_a" if which == "a" else "w_b"
                for n in range(2):
                    bank = nxt()
                    def mm(t, m=m, n=n, wt=wt, bank=bank):
                        for kc in range(NFC):
                            last = t.matmul(ps[bank][:], sb[wt][:, m, kc * 128:(kc + 1) * 128],
                                            sb["x_fm"][:, kc, n * 512:(n + 1) * 512],
                                            start=(kc == 0), stop=(kc == NFC - 1))
                        return last
                    p.step("tensor", mm, reads=[wt] + XFM_NAMES, writes=[f"ps{bank}"])
                    if which == "a":
                        p.step("scalar", (lambda n=n, m=m, sl=sl, bank=bank: lambda s: s.activation(
                            sb["a_sl"][:, sl, n * 512:(n + 1) * 512], ps[bank][:], AF.Tanh,
                            bias=sb["c_ab"][:, m:m + 1]))(),
                            reads=[f"ps{bank}", "c_ab"], writes=[f"aw{sl}{n}"])
                    else:
                        p.step("scalar", (lambda n=n, m=m, sl=sl, bank=bank: lambda s: s.activation(
                            sb["b_sl"][:, sl, n * 512:(n + 1) * 512], ps[bank][:], AF.Identity,
                            bias=sb["c_bb"][:, m:m + 1]))(),
                            reads=[f"ps{bank}", "c_bb"], writes=[f"bw{sl}{n}"])
            p.step("vector", (lambda m=m, sl=sl: lambda v: v.tensor_tensor_scan(
                sb["b16"][:, m, :], sb["a_sl"][:, sl, :], sb["b_sl"][:, sl, :],
                sb["carry_h"][:, m:m + 1], ALU.mult, ALU.add))(),
                reads=[f"aw{sl}0", f"aw{sl}1", f"bw{sl}0", f"bw{sl}1", "carry_h"],
                writes=[f"b16_{m}_0", f"b16_{m}_1"], tag="scan")
            p.step("vector", (lambda m=m: lambda v: v.tensor_copy(
                sb["carry_h"][:, m:m + 1], sb["b16"][:, m, T - 1:T]))(),
                reads=[f"b16_{m}_1"], writes=["carry_h"])

        dump("h", "b16", B16_NAMES, cast=True, c=c)
        dump("xfm", "x_fm", XFM_NAMES, cast=True, c=c)
        if c > 0:
            plan_routing_pe(c - 1)
        load_w("cT", "w_a", c); load_w("woT", "w_b", c)

        p.fence(ZCQ_NAMES, A_NAMES)
        if c > 0:
            p.fence(SCR_NAMES, XT_NAMES)
        hreads = [f"b16_{k}_{n}" for k in range(NFC) for n in range(2)]
        smm_q = []
        for m in range(NFC):
            sl = m % 2
            for n in range(2):
                bank = (m * 2 + n) % 2
                def mm(t, m=m, n=n, bank=bank):
                    for kc in range(NFC):
                        t.matmul(ps[bank][:], sb["w_a"][:, m, kc * 128:(kc + 1) * 128],
                                 sb["b16"][:, kc, n * 512:(n + 1) * 512],
                                 start=(kc == 0), stop=False)
                    for kc in range(NFC):
                        last = t.matmul(ps[bank][:], sb["w_b"][:, m, kc * 128:(kc + 1) * 128],
                                        sb["x_fm"][:, kc, n * 512:(n + 1) * 512],
                                        start=False, stop=(kc == NFC - 1))
                    return last
                p.step("tensor", mm, reads=["w_a", "w_b"] + XFM_NAMES
                       + [f"b16_{k}_{n}" for k in range(NFC)],
                       writes=[f"ps{bank}"])
                p.step("vector", (lambda m=m, n=n, bank=bank: lambda v: v.tensor_scalar(
                    sb["slab"][:, m, n * 512:(n + 1) * 512], ps[bank][:],
                    sb["c_cwob"][:, m:m + 1], sb["c_svec"][:, m:m + 1], ALU.add, ALU.mult))(),
                    reads=[f"ps{bank}", "c_cwob", "c_svec"], writes=[f"slab{m}_{n}"])
            p.step("scalar", (lambda m=m, sl=sl: lambda s: s.activation(
                sb["zc_sl"][:, sl, :], sb["slab"][:, m, :], AF.Identity))(),
                reads=[f"slab{m}_0", f"slab{m}_1"], writes=[f"zc{sl}"])
            p.step("scalar", (lambda m=m, sl=sl: lambda s: s.activation(
                sb["zq_sl"][:, sl, :], sb["slab"][:, m, :], AF.Square))(),
                reads=[f"slab{m}_0", f"slab{m}_1"], writes=[f"zq{sl}"])
            def smm(t, m=m, sl=sl):
                t.matmul(ps[2][0:1, :], sb["ones_col_b"][:], sb["zc_sl"][:, sl, 0:512],
                         start=(m == 0), stop=(m == NFC - 1))
                t.matmul(ps[3][0:1, :], sb["ones_col_b"][:], sb["zc_sl"][:, sl, 512:1024],
                         start=(m == 0), stop=(m == NFC - 1))
                t.matmul(ps[4][0:1, :], sb["ones_col_b"][:], sb["zq_sl"][:, sl, 0:512],
                         start=(m == 0), stop=(m == NFC - 1))
                return t.matmul(ps[5][0:1, :], sb["ones_col_b"][:], sb["zq_sl"][:, sl, 512:1024],
                                start=(m == 0), stop=(m == NFC - 1))
            smm_q.append((smm, sl))
            if len(smm_q) == 2:
                smm0, sl0 = smm_q.pop(0)
                p.step("tensor", smm0, reads=[f"zc{sl0}", f"zq{sl0}", "ones_col_b"],
                       writes=["ps2", "ps3", "ps4", "ps5"])
        smm7, sl7 = smm_q.pop(0)
        p.step("tensor", smm7, reads=[f"zc{sl7}", f"zq{sl7}", "ones_col_b"],
               writes=["ps2", "ps3", "ps4", "ps5"])
        dump("z", "slab", SLAB_NAMES, c=c)
        p.step("vector", lambda v: v.tensor_scalar(sb["rowB"][0:1, 0:512], ps[2][0:1, :],
                                                   1.0 / D, None, ALU.mult),
               reads=["ps2"], writes=["rowB"])
        p.step("vector", lambda v: v.tensor_scalar(sb["rowB"][0:1, 512:1024], ps[3][0:1, :],
                                                   1.0 / D, None, ALU.mult),
               reads=["ps3"], writes=["rowB"])
        p.step("vector", lambda v: v.tensor_scalar(sb["rowA"][0:1, 0:512], ps[4][0:1, :],
                                                   1.0 / D, None, ALU.mult),
               reads=["ps4"], writes=["rowA"])
        p.step("vector", lambda v: v.tensor_scalar(sb["rowA"][0:1, 512:1024], ps[5][0:1, :],
                                                   1.0 / D, None, ALU.mult),
               reads=["ps5"], writes=["rowA"])
        p.step("vector", lambda v: v.tensor_mul(ps[2][0:1, 0:512], sb["rowB"][0:1, 0:512],
                                                sb["rowB"][0:1, 0:512]),
               reads=["rowB"], writes=["ps2"])
        p.step("vector", lambda v: v.tensor_mul(ps[3][0:1, 0:512], sb["rowB"][0:1, 512:1024],
                                                sb["rowB"][0:1, 512:1024]),
               reads=["rowB"], writes=["ps3"])
        p.step("vector", lambda v: v.tensor_sub(sb["rowA"][0:1, 0:512], sb["rowA"][0:1, 0:512],
                                                ps[2][0:1, 0:512]),
               reads=["rowA", "ps2"], writes=["rowA"])
        p.step("vector", lambda v: v.tensor_sub(sb["rowA"][0:1, 512:1024], sb["rowA"][0:1, 512:1024],
                                                ps[3][0:1, 0:512]),
               reads=["rowA", "ps3"], writes=["rowA"])
        p.step("vector", lambda v: v.tensor_scalar(sb["rowA"][:], sb["rowA"][:], 1.0, EPS,
                                                   ALU.mult, ALU.add),
               reads=["rowA"], writes=["rowA"], tag="epsadd_ip")
        p.step("scalar", lambda s: s.activation(sb["rowA"][:], sb["rowA"][:], AF.Sqrt),
               reads=["rowA"], writes=["rowA"], tag="sqrt_ip")
        p.step("vector", lambda v: v.reciprocal(sb["rowA"][:], sb["rowA"][:]),
               reads=["rowA"], writes=["rowA"], tag="rcp_ip")
        p.step("vector", lambda v: v.scalar_tensor_tensor(sb["rowB"][:], sb["rowB"][:], -1.0,
                                                          sb["rowA"][:], ALU.mult, ALU.mult),
               reads=["rowB", "rowA"], writes=["rowB"], tag="negmuistd_ip")
        for n in range(2):
            bA, bB = (4, 5) if n == 0 else (2, 3)
            p.step("tensor", (lambda n=n, bA=bA: lambda t: t.matmul(
                ps[bA][:], sb["ones_row"][:], sb["rowA"][0:1, n * 512:(n + 1) * 512],
                start=True, stop=True))(),
                reads=["ones_row", "rowA"], writes=[f"ps{bA}"])
            p.step("tensor", (lambda n=n, bB=bB: lambda t: t.matmul(
                ps[bB][:], sb["ones_row"][:], sb["rowB"][0:1, n * 512:(n + 1) * 512],
                start=True, stop=True))(),
                reads=["ones_row", "rowB"], writes=[f"ps{bB}"])
            for m in range(NFC):
                p.step("vector", (lambda m=m, n=n, bA=bA: lambda v: v.tensor_tensor(
                    sb["scr"][:, 0, :], sb["slab"][:, m, n * 512:(n + 1) * 512],
                    ps[bA][:], ALU.mult))(),
                    reads=[f"slab{m}_{n}", f"ps{bA}"], writes=["scr0"])
                p.step("vector", (lambda m=m, n=n, bB=bB: lambda v: v.tensor_tensor(
                    sb["scr"][:, 1, :], sb["scr"][:, 0, :], ps[bB][:], ALU.add))(),
                    reads=["scr0", f"ps{bB}"], writes=["scr1"])
                p.step("vector", (lambda m=m: lambda v: v.tensor_scalar(
                    sb["scr"][:, 0, :], sb["scr"][:, 1, :], sb["c_lng"][:, m:m + 1],
                    sb["c_lnb"][:, m:m + 1], ALU.mult, ALU.add))(),
                    reads=["scr1", "c_lng", "c_lnb"], writes=["scr0"])
                p.step("vector", (lambda m=m, n=n: lambda v: v.tensor_tensor(
                    sb["slab"][:, m, n * 512:(n + 1) * 512],
                    sb["slab"][:, m, n * 512:(n + 1) * 512], sb["scr"][:, 0, :], ALU.add))(),
                    reads=["scr0", f"slab{m}_{n}"], writes=[f"slab{m}_{n}"], tag="u_ip")
                p.step("vector", (lambda m=m, n=n: lambda v: v.tensor_copy(
                    sb["b16"][:, m, n * 512:(n + 1) * 512],
                    sb["slab"][:, m, n * 512:(n + 1) * 512]))(),
                    reads=[f"slab{m}_{n}"], writes=[f"b16_{m}_{n}"], tag="ub16")

        dump("u", "slab", SLAB_NAMES, c=c)
        dump("ub", "b16", B16_NAMES, cast=True, c=c)
        first = (c == 0)
        def f1load(mi, slot, fresh=first):
            if fresh:
                p.step("gpsimd", (lambda mi=mi, slot=slot: lambda g: g.dma_start(
                    out=sb["f1col"][:, slot], in_=di["f1T"].ap()[mi]))(),
                    writes=[f"f1c{slot}"], dma=f"ldf1_{mi % 8}")
                p.step("sync", (lambda mi=mi, slot=slot: lambda e: e.dma_start(
                    out=f1c_d.ap()[mi], in_=sb["f1col"][:, slot]))(),
                    reads=[f"f1c{slot}"], writes=[f"wcf1_{mi}"], dma="wcw")
            else:
                p.step("sync", (lambda mi=mi, slot=slot: lambda e: e.dma_start(
                    out=sb["f1col"][:, slot], in_=f1c_d.ap()[mi]))(),
                    reads=[f"wcf1_{mi}"], writes=[f"f1c{slot}"], dma=f"ldf1_{mi % 8}")
        def f2load(mi, slot, fresh=first):
            if fresh:
                p.step("gpsimd", (lambda mi=mi, slot=slot: lambda g: g.dma_start(
                    out=sb["f2col"][:, slot], in_=di["f2T"].ap()[mi]))(),
                    writes=[f"f2c{slot}"], dma=f"ldf2_{mi % 4}")
                p.step("sync", (lambda mi=mi, slot=slot: lambda e: e.dma_start(
                    out=f2c_d.ap()[mi], in_=sb["f2col"][:, slot]))(),
                    reads=[f"f2c{slot}"], writes=[f"wcf2_{mi}"], dma="wcw")
            else:
                p.step("sync", (lambda mi=mi, slot=slot: lambda e: e.dma_start(
                    out=sb["f2col"][:, slot], in_=f2c_d.ap()[mi]))(),
                    reads=[f"wcf2_{mi}"], writes=[f"f2c{slot}"], dma=f"ldf2_{slot}")
        for sl2 in range(2):
            fresh = (c == 0 and sl2 == 0)
            to = sl2 * 512
            f1load(0, 0, fresh); f1load(1, 1, fresh)
            for m in range(NMF):
                slot = m % 4
                if m + 2 < NMF:
                    f1load(m + 2, (m + 2) % 4, fresh)
                bank = nxt()
                def mm(t, m=m, to=to, slot=slot, bank=bank):
                    for kc in range(NFC):
                        last = t.matmul(ps[bank][:], sb["f1col"][:, slot, kc, :],
                                        sb["b16"][:, kc, to:to + 512],
                                        start=(kc == 0), stop=(kc == NFC - 1))
                    return last
                p.step("tensor", mm, reads=[f"f1c{slot}"]
                       + [f"b16_{k}_{sl2}" for k in range(NFC)],
                       writes=[f"ps{bank}"])
                sc = 2 * (m % 2)
                p.step("scalar", (lambda m=m, bank=bank, sc=sc: lambda s: s.activation(
                    sb["scr"][:, sc, :], ps[bank][:], AF.Erf,
                    bias=sb["c_f1be"][:, m:m + 1], scale=1.0 / math.sqrt(2.0)))(),
                    reads=[f"ps{bank}", "c_f1be"], writes=[f"scr{sc}"])
                p.step("scalar", (lambda m=m, bank=bank, sc=sc: lambda s: s.activation(
                    sb["scr"][:, sc + 1, :], ps[bank][:], AF.Identity,
                    bias=sb["c_f1bh"][:, m:m + 1], scale=0.5))(),
                    reads=[f"ps{bank}", "c_f1bh"], writes=[f"scr{sc + 1}"])
                p.step("vector", (lambda m=m, sc=sc: lambda v: v.scalar_tensor_tensor(
                    sb["hid"][:, m, :], sb["scr"][:, sc, :], 1.0, sb["scr"][:, sc + 1, :],
                    ALU.add, ALU.mult))(),
                    reads=[f"scr{sc}", f"scr{sc + 1}"], writes=[f"hid{m}"])
            f2load(0, 0, fresh); f2load(1, 1, fresh)
            for m2 in range(NFC):
                slot = m2 % 2
                bank = nxt()
                def mm2(t, m2=m2, to=to, slot=slot, bank=bank):
                    for kc in range(NMF):
                        last = t.matmul(ps[bank][:], sb["f2col"][:, slot, kc, :],
                                        sb["hid"][:, kc, :],
                                        start=(kc == 0), stop=(kc == NMF - 1))
                    return last
                p.step("tensor", mm2, reads=[f"f2c{slot}"] + HID_NAMES, writes=[f"ps{bank}"])
                if m2 + 2 < NFC:
                    f2load(m2 + 2, (m2 + 2) % 2, fresh)
                p.step("vector", (lambda m2=m2, to=to, bank=bank, sl2=sl2: lambda v: v.scalar_tensor_tensor(
                    sb["slab"][:, m2, to:to + 512], ps[bank][:], sb["c_f2b"][:, m2:m2 + 1],
                    sb["slab"][:, m2, to:to + 512], ALU.add, ALU.add))(),
                    reads=[f"ps{bank}", "c_f2b", f"slab{m2}_{sl2}"],
                    writes=[f"slab{m2}_{sl2}"], tag="y_ip")
                p.step("vector", (lambda m2=m2, to=to, sl2=sl2: lambda v: v.tensor_copy(
                    sb["b16"][:, m2, to:to + 512], sb["slab"][:, m2, to:to + 512]))(),
                    reads=[f"slab{m2}_{sl2}"], writes=[f"b16_{m2}_{sl2}"], tag="yb16")

        dump("y", "slab", SLAB_NAMES, c=c)
        dump("hid", "hid", HID_NAMES, cast=True, c=c)
        p.fence(XT_NAMES, SCR_NAMES)
        p.fence(["w_p"], HID_NAMES)
        load_w("pT", "w_p", c)
        xg = di["x_tok"].ap()[bat].rearrange("(g p) d -> g p d", p=128)
        for tg in range(NTG):
            gbank = 4 + (tg % 2)
            def mmg(t, tg=tg, gbank=gbank):
                for kc in range(NFC):
                    t.matmul(ps[gbank][:, 0:E], sb["x_fm"][:, kc, tg * 128:(tg + 1) * 128],
                             sb["gw_b"][:, kc, :], start=(kc == 0), stop=False)
                for kc in range(NFC):
                    t.matmul(ps[gbank][:, 0:E], sb["b16"][:, kc, tg * 128:(tg + 1) * 128],
                             sb["G_b"][:, kc, :], start=False, stop=False)
                return t.matmul(ps[gbank][:, 0:E], sb["ones_row"][:], sb["pbgw_row"][:],
                                start=False, stop=True)
            gbank = 4 + (tg % 2)
            p.step("tensor", (lambda mmg=mmg, gbank=gbank, tg=tg: lambda t: mmg(t))(),
                   reads=[f"b16_{k}_{tg // 4}" for k in range(NFC)] + XFM_NAMES
                   + ["gw_b", "G_b", "ones_row", "pbgw_row"],
                   writes=[f"ps{gbank}"], tag="ggate")
            p.step("vector", (lambda tg=tg, par=par, gbank=gbank: lambda v: v.tensor_copy(
                sb["logit_all"][:, par, tg, :], ps[gbank][:, 0:E]))(),
                reads=[f"ps{gbank}"], writes=[f"pre{par}_{tg}"])
        for tg in range(NTG):
            xs = tg % 2
            p.step("sync", (lambda l0=l0, tg=tg, xs=xs, xg=xg: lambda e: e.dma_start(
                out=sb["xt_sl"][:, xs], in_=xg[l0 // 128 + tg]))(),
                writes=[f"xt{xs}"], dma=f"ldxt{xs}")
            for n in range(2):
                bank = nxt()
                def mmp(t, tg=tg, n=n, bank=bank):
                    for kc in range(NFC):
                        t.matmul(ps[bank][:], sb["b16"][:, kc, tg * 128:(tg + 1) * 128],
                                 sb["w_p"][:, kc, n * 512:(n + 1) * 512],
                                 start=(kc == 0), stop=False)
                    return t.matmul(ps[bank][:], sb["ones_row"][:],
                                    sb["pb_row"][0:1, n * 512:(n + 1) * 512],
                                    start=False, stop=True)
                p.step("tensor", mmp,
                       reads=[f"b16_{k}_{tg // 4}" for k in range(NFC)]
                       + ["w_p", "ones_row", "pb_row"],
                       writes=[f"ps{bank}"])
                p.step("vector", (lambda tg=tg, n=n, xs=xs, bank=bank: lambda v: v.tensor_tensor(
                    sb["slab"][:, tg, n * 512:(n + 1) * 512], ps[bank][:],
                    sb["xt_sl"][:, xs, n * 512:(n + 1) * 512], ALU.add))(),
                    reads=[f"ps{bank}", f"xt{xs}"], writes=[f"slab{tg}_{n}"])
            p.step("vector", (lambda tg=tg, par=par: lambda v: v.tensor_reduce(
                sb["rt_negmu"][:, par, tg:tg + 1], sb["slab"][:, tg, :], AX.X, ALU.add,
                negate=True))(),
                reads=[f"slab{tg}_0", f"slab{tg}_1"], writes=[f"negsum{par}_{tg}"])
        p.step("vector", (lambda par=par: lambda v: v.tensor_scalar(
            sb["rt_negmu"][:, par, :], sb["rt_negmu"][:, par, :],
            1.0 / D, None, ALU.mult))(),
            reads=[f"negsum{par}_{tg}" for tg in range(NTG)], writes=[f"nm{par}"],
            tag="negmu_ip")
        for tg in range(NTG):
            bank = 4 if tg % 2 == 0 else 5
            p.step("scalar", (lambda tg=tg, par=par, bank=bank: lambda s: s.activation(
                ps[bank][:], sb["slab"][:, tg, 0:512], AF.Square,
                bias=sb["rt_negmu"][:, par, tg:tg + 1], accum_out=sb["lnvar"][:, tg:tg + 1]))(),
                reads=[f"slab{tg}_0", f"nm{par}"], writes=[f"ps{bank}", f"lnv{tg}"])
            p.step("scalar", (lambda tg=tg, par=par, bank=bank: lambda s: s.activation(
                ps[bank][:], sb["slab"][:, tg, 512:1024], AF.Square,
                bias=sb["rt_negmu"][:, par, tg:tg + 1], accum_out=sb["lnvar2"][:, tg:tg + 1]))(),
                reads=[f"slab{tg}_1", f"nm{par}"], writes=[f"ps{bank}", f"lnw{tg}"])
        p.step("vector", lambda v: v.tensor_add(sb["lnvar"][:], sb["lnvar"][:], sb["lnvar2"][:]),
               reads=[f"lnv{tg}" for tg in range(NTG)] + [f"lnw{tg}" for tg in range(NTG)],
               writes=["lnvar"], tag="lnvar_ip")
        p.step("vector", lambda v: v.tensor_scalar(sb["lnvar"][:], sb["lnvar"][:],
                                                   1.0 / D, EPS, ALU.mult, ALU.add),
               reads=["lnvar"], writes=["lnvar"], tag="lnvar_ip2")
        p.step("scalar", (lambda par=par: lambda s: s.activation(
            sb["rt_istd"][:, par, :], sb["lnvar"][:], AF.Sqrt))(),
            reads=["lnvar"], writes=[f"st{par}"])
        p.step("vector", (lambda par=par: lambda v: v.reciprocal(
            sb["rt_istd"][:, par, :], sb["rt_istd"][:, par, :]))(),
            reads=[f"st{par}"], writes=[f"st{par}"], tag="istd_ip")
        for tg in range(NTG):
            p.step("vector", (lambda tg=tg, par=par: lambda v: v.tensor_scalar(
                sb["slab"][:, tg, :], sb["slab"][:, tg, :],
                sb["rt_negmu"][:, par, tg:tg + 1], sb["rt_istd"][:, par, tg:tg + 1],
                ALU.add, ALU.mult))(),
                reads=[f"slab{tg}_0", f"slab{tg}_1", f"nm{par}", f"st{par}"],
                writes=[f"slab{tg}_0", f"slab{tg}_1"], tag="x1_ip")
        p.step("sync", (lambda c=c: lambda e: e.dma_start(
            out=x1sp.ap().rearrange("(cc g p) d -> cc p g d", cc=NCHUNK, p=128)[c],
            in_=sb["slab"][:]))(),
            reads=SLAB_NAMES, writes=["x1sp"], dma="spw")
        for m in range(NFC):
            p.step("scalar", (lambda m=m: lambda s: s.activation(
                sb["b16"][:, m, :], sb["slab"][:, m, :], AF.Identity))(),
                reads=[f"slab{m}_0", f"slab{m}_1"],
                writes=[f"b16_{m}_0", f"b16_{m}_1"])
        p.step("sync", (lambda c=c: lambda e: e.dma_start(
            out=x1spb.ap().rearrange("(cc g p) d -> cc p g d", cc=NCHUNK, p=128)[c],
            in_=sb["b16"][:]))(),
            reads=B16_NAMES, writes=["x1spb"], dma="spwb")

    plan_routing_dve(n_chunks - 1)
    plan_routing_pe(n_chunks - 1)
    p.step("vector", lambda v: v.tensor_scalar(sb["pos_all"][:], sb["pos_all"][:],
                                               -1.0, None, ALU.add),
           reads=[f"posr{g}" for g in range(n_chunks * NTG)], writes=["pos_all"],
           tag="pos_ip")

    if dbg:
        dbgs = {}
        for dn, tile in [("dbg_pos", "pos_all"), ("dbg_eid", "eid_all"),
                         ("dbg_gp", "gp_all"), ("dbg_fm", "fm_all")]:
            shp = list(sb[tile].shape)
            dbgs[dn] = nc.dram_tensor(dn, shp, F32, kind="ExternalOutput")
            p.step("sync", (lambda dn=dn, tile=tile: lambda e: e.dma_start(
                out=dbgs[dn].ap(), in_=sb[tile][:]))(),
                reads=[tile], dma="sto")
        dbgs["dbg_cnt"] = nc.dram_tensor("dbg_cnt", [E, 1], F32, kind="ExternalOutput")
        p.step("sync", lambda e: e.dma_start(out=dbgs["dbg_cnt"].ap(), in_=sb["carry_c"][:]),
               reads=["carry_c"], dma="sto")
        if n_chunks < NCHUNK:
            p.wait_tokens("sync", [("spw", p.dma_counts.get("spw", 0)),
                                   ("sto", p.dma_counts.get("sto", 0)),
                                   ("dmp", p.dma_counts.get("dmp", 0))])
            with nc.Block() as block:
                p.emit(block)
            return nc
    elif n_chunks < NCHUNK:
        p.wait_tokens("sync", [("spw", p.dma_counts.get("spw", 0))])
        with nc.Block() as block:
            p.emit(block)
        return nc

    p.step("sync", lambda e: e.dma_start(out=cnt_in.ap().rearrange("a b -> b a"), in_=sb["carry_c"][:]),
           reads=["carry_c"], writes=["cnt_in"], dma="cnt")
    p.step("gpsimd", lambda g: g.collective_compute(
        "AllGather", ALU.bypass, replica_groups=RG,
        ins=[flatten_dims_for_collective(cnt_in.ap())],
        outs=[flatten_dims_for_collective(cnt_out.ap())]),
        reads=["cnt_in"], writes=["cnt_out"], tag="ag_cnt")
    p.step("sync", lambda e: e.dma_start(out=sb["cnts"][:], in_=cnt_out.ap()),
           reads=["cnt_out"], writes=["cnts"], dma="cnt")
    p.step("tensor", lambda t: t.matmul(psT[0:1, 0:E], sb["trimask"][:], sb["cnts"][:],
                                        start=True, stop=True),
           reads=["trimask", "cnts"], writes=["psT"])
    p.step("vector", lambda v: v.tensor_copy(sb["orow"][:], psT[0:1, 0:E]),
           reads=["psT"], writes=["orow"])
    p.step("tensor", lambda t: t.matmul(psT[:, 0:E], sb["ones_row"][:], sb["orow"][:],
                                        start=True, stop=True),
           reads=["ones_row", "orow"], writes=["psT"])
    p.step("vector", lambda v: v.tensor_scalar(sb["oadj_bc"][:], sb["iota8"][:],
                                               float(CAP), None, ALU.mult),
           reads=["iota8"], writes=["oadj_bc"])
    p.step("vector", lambda v: v.tensor_tensor(sb["oadj_bc"][:], sb["oadj_bc"][:],
                                               psT[:, 0:E], ALU.add),
           reads=["psT", "oadj_bc"], writes=["oadj_bc"], tag="oadj_ip")
    for g in range(NG):
        j = g % NTG
        p.step("vector", (lambda g=g, j=j: lambda v: v.tensor_tensor(
            sb["scr8b"][:, 0, j, :], sb["fm_all"][:, g, :], sb["oadj_bc"][:], ALU.mult))(),
            reads=[f"fm{g}", "oadj_bc"], writes=[f"s8b0_{j}"])
        p.step("vector", (lambda g=g, j=j: lambda v: v.tensor_reduce(
            sb["o_all"][:, g:g + 1], sb["scr8b"][:, 0, j, :], AX.X, ALU.add))(),
            reads=[f"s8b0_{j}"], writes=[f"oa{g}"])
    oreads = [f"oa{g}" for g in range(NG)]
    p.step("vector", lambda v: v.tensor_tensor(sb["slot_f"][:], sb["pos_all"][:],
                                               sb["o_all"][:], ALU.add),
           reads=["pos_all"] + oreads, writes=["slot_f"])
    p.step("vector", lambda v: v.tensor_scalar(sb["o_all"][:], sb["eid_all"][:],
                                               float(CAP), None, ALU.mult),
           reads=["eid_all"], writes=["o_all"])
    p.step("vector", lambda v: v.tensor_tensor(sb["o_all"][:], sb["slot_f"][:],
                                               sb["o_all"][:], ALU.subtract),
           reads=["slot_f", "o_all"], writes=["o_all"], tag="posg_ip")
    p.step("vector", lambda v: v.tensor_scalar(sb["o_all"][:], sb["o_all"][:],
                                               float(CAP), None, ALU.is_lt),
           reads=["o_all"], writes=["o_all"], tag="keep_ip")
    p.step("vector", lambda v: v.tensor_tensor(sb["gpk_all"][:], sb["gp_all"][:],
                                               sb["o_all"][:], ALU.mult),
           reads=["gp_all", "o_all"], writes=["gpk_all"])
    p.step("vector", lambda v: v.tensor_scalar(sb["pos_all"][:], sb["o_all"][:],
                                               -float(NSLOT), float(NSLOT), ALU.mult, ALU.add),
           reads=["o_all"], writes=["pos_all"], tag="trash_term")
    p.step("vector", lambda v: v.tensor_tensor(sb["slot_f"][:], sb["slot_f"][:],
                                               sb["o_all"][:], ALU.mult),
           reads=["slot_f", "o_all"], writes=["slot_f"], tag="slot_ip")
    p.step("vector", lambda v: v.tensor_tensor(sb["slot_f"][:], sb["slot_f"][:],
                                               sb["pos_all"][:], ALU.add),
           reads=["slot_f", "pos_all"], writes=["slot_f"], tag="slot_ip2")
    p.step("vector", lambda v: v.tensor_copy(sb["slot_i"][:], sb["slot_f"][:]),
           reads=["slot_f"], writes=["slot_i"])
    x1r = x1sp.ap().rearrange("(g p) d -> g p d", p=128)
    x1rb = x1spb.ap().rearrange("(g p) d -> g p d", p=128)
    p.fence(["sc_b0", "sc_b1"], B_NAMES)
    p.fence(["sc_b2", "sc_b3", "sc_b4", "sc_b5"],
            A_NAMES + ["zc0", "zc1", "zq0", "zq1"])
    p.wait_tokens("gpsimd", [disp_zeroed])
    for g in range(NG):
        slot = g % 6
        sct = (sb["sc_b"][:, slot] if slot < 2 else
               sb["sc_bB"][:, slot - 2] if slot < 4 else sb["sc_bC"][:, slot - 4])
        p.step("sync", (lambda g=g, sct=sct: lambda e: e.dma_start(
            out=sct, in_=x1rb[g]))(),
            reads=["x1spb"], writes=[f"sc_b{slot}"], dma=f"scb{slot}")
        p.step("gpsimd", (lambda g=g, sct=sct: lambda e: e.indirect_dma_start(
            out=disp.ap(), out_offset=IndirectOffsetOnAxis(sb["slot_i"][:, g:g + 1], 0),
            in_=sct, in_offset=None))(),
            reads=[f"sc_b{slot}", "slot_i"], writes=[], dma=f"scbi{slot}")

    p.wait_tokens("gpsimd", [(f"scbi{s}", p.dma_counts.get(f"scbi{s}", 0))
                             for s in range(6)])
    p.step("gpsimd", lambda g: g.collective_compute(
        "ReduceScatter", ALU.add, replica_groups=RG,
        ins=[flatten_dims_for_collective(disp.ap()[0:NSLOT, :])],
        outs=[flatten_dims_for_collective(disp_my.ap())]),
        reads=["disp"], writes=["disp_my"], tag="rs")
    if dbg:
        d_dm = nc.dram_tensor("d_dispmy", [CAP, D], F32, kind="ExternalOutput")
        p.step("gpsimd", lambda g: g.dma_start(out=d_dm.ap(), in_=disp_my.ap()),
               reads=["disp_my"], dma="dmp2")
        d_sf = nc.dram_tensor("d_slotf", [128, NG], F32, kind="ExternalOutput")
        p.step("sync", lambda e: e.dma_start(out=d_sf.ap(), in_=sb["slot_f"][:]),
               reads=["slot_f"], dma="dmp2")
        d_gk = nc.dram_tensor("d_gpk", [128, NG], F32, kind="ExternalOutput")
        p.step("sync", lambda e: e.dma_start(out=d_gk.ap(), in_=sb["gpk_all"][:]),
               reads=["gpk_all"], dma="dmp2")
        d_or = nc.dram_tensor("d_orow", [1, E], F32, kind="ExternalOutput")
        p.step("sync", lambda e: e.dma_start(out=d_or.ap(), in_=sb["orow"][:]),
               reads=["orow"], dma="dmp2")
    p.fence(MFM_NAMES + EOFM_NAMES + ["eot0", "eot1"],
            XFM_NAMES + ["w_a", "w_b"])
    p.fence(HE_NAMES, SLAB_NAMES + B16_NAMES)
    for fc in range(NFC):
        p.step("sync", (lambda fc=fc: lambda e: e.dma_start(
            out=sb["moe_fm"][:, fc, :], in_=disp_my.ap()[:, fc * 128:(fc + 1) * 128],
            transpose=True))(),
            reads=["disp_my"], writes=[f"mfm{fc}"], dma=f"mtk{fc}")
    def w1load(mi, slot):
        p.step("gpsimd", (lambda mi=mi, slot=slot: lambda g: g.dma_start(
            out=sb["f1e"][:, slot], in_=di["w1r"].ap()[mi]))(),
            writes=[f"f1e{slot}"], dma=f"lde{slot % 4}")
    def w2load(mi, slot):
        p.step("gpsimd", (lambda mi=mi, slot=slot: lambda g: g.dma_start(
            out=sb["f2col"][:, slot], in_=di["w2r"].ap()[mi]))(),
            writes=[f"f2c{slot}"], dma=f"ldf2_{slot}")
    p.fence([f"f1e{s}" for s in range(16)], HID_NAMES + ["w_p"])
    for mi in range(16):
        w1load(mi, mi)
    for m in range(NMF):
        slot = m % 16
        for n0, nw in ((0, 512), (512, 128)):
            bank = nxt()
            def mm(t, m=m, slot=slot, n0=n0, nw=nw, bank=bank):
                for kc in range(NFC):
                    last = t.matmul(ps[bank][:, 0:nw], sb["f1e"][:, slot, kc, :],
                                    sb["moe_fm"][:, kc, n0:n0 + nw],
                                    start=(kc == 0), stop=(kc == NFC - 1))
                return last
            p.step("tensor", mm, reads=[f"f1e{slot}"] + MFM_NAMES, writes=[f"ps{bank}"])
            p.step("scalar", (lambda m=m, bank=bank, nw=nw: lambda s: s.activation(
                sb["scr"][:, 0, 0:nw], ps[bank][:, 0:nw], AF.Square,
                bias=sb["c_b1r"][:, m:m + 1]))(),
                reads=[f"ps{bank}", "c_b1r"], writes=["scr0"])
            p.step("scalar", (lambda m=m, bank=bank, nw=nw: lambda s: s.activation(
                sb["scr"][:, 1, 0:nw], ps[bank][:, 0:nw], AF.Identity,
                bias=sb["c_b1r"][:, m:m + 1]))(),
                reads=[f"ps{bank}", "c_b1r"], writes=["scr1"])
            p.step("vector", (lambda nw=nw: lambda v: v.tensor_scalar(
                sb["scr"][:, 2, 0:nw], sb["scr"][:, 0, 0:nw], GC2, GC1, ALU.mult, ALU.add))(),
                reads=["scr0"], writes=["scr2"])
            p.step("vector", (lambda nw=nw: lambda v: v.tensor_tensor(
                sb["scr"][:, 0, 0:nw], sb["scr"][:, 2, 0:nw], sb["scr"][:, 1, 0:nw], ALU.mult))(),
                reads=["scr1", "scr2"], writes=["scr0"])
            p.step("scalar", (lambda nw=nw: lambda s: s.activation(
                sb["scr"][:, 2, 0:nw], sb["scr"][:, 0, 0:nw], AF.Tanh))(),
                reads=["scr0"], writes=["scr2"])
            p.step("vector", (lambda nw=nw: lambda v: v.tensor_scalar(
                sb["scr"][:, 3, 0:nw], sb["scr"][:, 2, 0:nw], 0.5, 0.5, ALU.mult, ALU.add))(),
                reads=["scr2"], writes=["scr3"])
            p.step("vector", (lambda m=m, n0=n0, nw=nw: lambda v: v.tensor_tensor(
                sb["he"][:, m, n0:n0 + nw], sb["scr"][:, 3, 0:nw], sb["scr"][:, 1, 0:nw],
                ALU.mult))(),
                reads=["scr1", "scr3"], writes=[f"he{m}"])
        if m + 16 < NMF:
            w1load(m + 16, slot)
    w2load(0, 0); w2load(1, 1)
    for m2 in range(NFC):
        slot = m2 % 2
        for n0, nw in ((0, 512), (512, 128)):
            bank = nxt()
            def mm2(t, m2=m2, slot=slot, n0=n0, nw=nw, bank=bank):
                for kc in range(NMF):
                    last = t.matmul(ps[bank][:, 0:nw], sb["f2col"][:, slot, kc, :],
                                    sb["he"][:, kc, n0:n0 + nw],
                                    start=(kc == 0), stop=(kc == NMF - 1))
                return last
            p.step("tensor", mm2, reads=[f"f2c{slot}"] + HE_NAMES, writes=[f"ps{bank}"])
            p.step("vector", (lambda m2=m2, n0=n0, nw=nw, bank=bank: lambda v: v.tensor_scalar(
                sb["eo_fm"][:, m2, n0:n0 + nw], ps[bank][:, 0:nw],
                sb["c_b2r"][:, m2:m2 + 1], None, ALU.add))(),
                reads=[f"ps{bank}", "c_b2r"], writes=[f"eo{m2}"])
        if m2 + 2 < NFC:
            w2load(m2 + 2, (m2 + 2) % 2)
    etb = [(psT[:, 0:128], "psT"), (ps[0][:, 0:128], "ps0"), (ps[1][:, 0:128], "ps1")]
    for s5 in range(5):
        slot = s5 % 2
        for fc in range(NFC):
            buf, bn = etb[(s5 * NFC + fc) % 3]
            p.step("tensor", (lambda s5=s5, fc=fc, buf=buf: lambda t: t.transpose(
                buf, sb["eo_fm"][:, fc, s5 * 128:(s5 + 1) * 128], sb["ident_f"][:]))(),
                reads=[f"eo{fc}", "ident_f"], writes=[bn])
            p.step("vector", (lambda fc=fc, slot=slot, buf=buf: lambda v: v.tensor_copy(
                sb["eo_tok"][:, slot, fc * 128:(fc + 1) * 128], buf))(),
                reads=[bn], writes=[f"eot{slot}"])
        p.step("sync", (lambda s5=s5, slot=slot: lambda e: e.dma_start(
            out=eo_my.ap().rearrange("(s p) d -> s p d", p=128)[s5], in_=sb["eo_tok"][:, slot]))(),
            reads=[f"eot{slot}"], writes=[], dma=f"eow{slot}")

    p.wait_tokens("gpsimd", [("eow0", p.dma_counts.get("eow0", 0)),
                             ("eow1", p.dma_counts.get("eow1", 0))])
    p.step("gpsimd", lambda g: g.collective_compute(
        "AllGather", ALU.bypass, replica_groups=RG,
        ins=[flatten_dims_for_collective(eo_my.ap())],
        outs=[flatten_dims_for_collective(eo_full.ap()[0:NSLOT, :])]),
        writes=["eo_full"], tag="ag_eo")

    if dbg:
        p.wait_tokens("sync", [("eow0", p.dma_counts.get("eow0", 0)),
                               ("eow1", p.dma_counts.get("eow1", 0))])
        d_eo = nc.dram_tensor("d_eomy", [CAP, D], F32, kind="ExternalOutput")
        p.step("sync", lambda e: e.dma_start(out=d_eo.ap(), in_=eo_my.ap()),
               reads=[], dma="dmp2")
        d_ef = nc.dram_tensor("d_eofull", [NSLOT, D], F32, kind="ExternalOutput")
        p.step("sync", lambda e: e.dma_start(out=d_ef.ap(), in_=eo_full.ap()[0:NSLOT, :]),
               reads=["eo_full"], dma="dmp2")
    p.fence(Y2_NAMES, [f"f1e{s}" for s in range(16)])
    p.fence([f"x1q{s}" for s in range(8)],
            A_NAMES + ZCQ_NAMES + B_NAMES + SCR_NAMES + XT_NAMES + ["w_b"]
            + [f"sc_b{s}" for s in range(6)])
    p.fence([f"egq{s}" for s in range(4)], ["w_a"] + MFM_NAMES + EOFM_NAMES)
    p.step("vector", lambda v: v.memset(sb["eogq"][:, 0], 0.0), writes=["egq0"])
    p.step("sync", lambda e: e.dma_start(out=eo_full.ap()[NSLOT:NSLOT + 1, :],
                                         in_=sb["eogq"][0:1, 0, :]),
           reads=["egq0"], writes=["eo_trash"], dma="etz")
    p.wait_tokens("gpsimd", [("etz", p.dma_counts["etz"])])
    outr = out_d.ap().rearrange("b (g p) d -> (b g) p d", p=128)
    for c in range(NCHUNK):
        for tg in range(NTG):
            g = c * NTG + tg
            slot = g % 8
            x1t = sb["x1q"][:, slot] if slot < 4 else sb["x1q2"][:, slot - 4]
            p.step("sync", (lambda g=g, x1t=x1t: lambda e: e.dma_start(
                out=x1t, in_=x1r[g]))(),
                reads=["x1sp"], writes=[f"x1q{slot}"], dma=f"x1r{slot % 4}")
            eslot = g % 4
            p.step("gpsimd", (lambda g=g, eslot=eslot: lambda e: e.indirect_dma_start(
                out=sb["eogq"][:, eslot], out_offset=None,
                in_=eo_full.ap(), in_offset=IndirectOffsetOnAxis(sb["slot_i"][:, g:g + 1], 0)))(),
                reads=["eo_full", "slot_i", f"egq{eslot}"], writes=[f"egq{eslot}"],
                dma=f"eg{eslot}")
            p.step("vector", (lambda g=g, tg=tg, eslot=eslot, x1t=x1t: lambda v: v.scalar_tensor_tensor(
                sb["y2"][:, tg, :], sb["eogq"][:, eslot, :], sb["gpk_all"][:, g:g + 1],
                x1t, ALU.mult, ALU.add))(),
                reads=[f"egq{eslot}", f"x1q{slot}", "gpk_all"], writes=[f"y2_{tg}"])
            p.step("vector", (lambda tg=tg: lambda v: v.tensor_reduce(
                sb["rt_negmu"][:, 0, tg:tg + 1], sb["y2"][:, tg, :], AX.X, ALU.add,
                negate=True))(),
                reads=[f"y2_{tg}"], writes=[f"ns2{tg}"])
        p.step("vector", lambda v: v.tensor_scalar(sb["rt_negmu"][:, 0, :], sb["rt_negmu"][:, 0, :],
                                                   1.0 / D, None, ALU.mult),
               reads=[f"ns2{tg}" for tg in range(NTG)], writes=["nm0"], tag="nm2_ip")
        for tg in range(NTG):
            bank = 4 if tg % 2 == 0 else 5
            p.step("scalar", (lambda tg=tg, bank=bank: lambda s: s.activation(
                ps[bank][:], sb["y2"][:, tg, 0:512], AF.Square,
                bias=sb["rt_negmu"][:, 0, tg:tg + 1], accum_out=sb["lnvar"][:, tg:tg + 1]))(),
                reads=[f"y2_{tg}", "nm0"], writes=[f"ps{bank}", f"lv2{tg}"])
            p.step("scalar", (lambda tg=tg, bank=bank: lambda s: s.activation(
                ps[bank][:], sb["y2"][:, tg, 512:1024], AF.Square,
                bias=sb["rt_negmu"][:, 0, tg:tg + 1], accum_out=sb["lnvar2"][:, tg:tg + 1]))(),
                reads=[f"y2_{tg}", "nm0"], writes=[f"ps{bank}", f"lw2{tg}"])
        p.step("vector", lambda v: v.tensor_add(sb["lnvar"][:], sb["lnvar"][:], sb["lnvar2"][:]),
               reads=[f"lv2{tg}" for tg in range(NTG)] + [f"lw2{tg}" for tg in range(NTG)],
               writes=["lnvar"], tag="lv2_ip")
        p.step("vector", lambda v: v.tensor_scalar(sb["lnvar"][:], sb["lnvar"][:],
                                                   1.0 / D, EPS, ALU.mult, ALU.add),
               reads=["lnvar"], writes=["lnvar"], tag="lv2_ip2")
        p.step("scalar", lambda s: s.activation(sb["rt_istd"][:, 0, :], sb["lnvar"][:], AF.Sqrt),
               reads=["lnvar"], writes=["st0"])
        p.step("vector", lambda v: v.reciprocal(sb["rt_istd"][:, 0, :], sb["rt_istd"][:, 0, :]),
               reads=["st0"], writes=["st0"], tag="istd2_ip")
        for tg in range(NTG):
            g = c * NTG + tg
            p.step("vector", (lambda tg=tg: lambda v: v.tensor_scalar(
                sb["y2"][:, tg, :], sb["y2"][:, tg, :],
                sb["rt_negmu"][:, 0, tg:tg + 1], sb["rt_istd"][:, 0, tg:tg + 1],
                ALU.add, ALU.mult))(),
                reads=[f"y2_{tg}", "nm0", "st0"], writes=[f"y2_{tg}"], tag="ln2_ap")
            p.step("sync", (lambda g=g, tg=tg: lambda e: e.dma_start(
                out=outr[g], in_=sb["y2"][:, tg, :]))(),
                reads=[f"y2_{tg}"], writes=[], dma=f"sto{tg % 2}")

    toks = [(f"sto{s}", p.dma_counts[f"sto{s}"]) for s in range(2)]
    if "dmp2" in p.dma_counts:
        toks.append(("dmp2", p.dma_counts["dmp2"]))
    p.wait_tokens("sync", toks)
    with nc.Block() as block:
        p.emit(block)
    return nc


def kernel(**inputs):
    per_core = host_prepare(inputs)
    nc = build_nc()
    res = run_bass_kernel_spmd(nc, per_core, core_ids=list(range(NCORES)))
    out = np.concatenate([np.asarray(r["out"]) for r in res.results], axis=0)
    return out.reshape(B, L, D)


# version 09
# speedup vs baseline: 1.0154x; 1.0154x over previous
import math
import numpy as np
import concourse.bass as bass
import concourse.mybir as mybir
from concourse.bass import IndirectOffsetOnAxis
from concourse.bass_utils import run_bass_kernel_spmd
from concourse.collective import flatten_dims_for_collective

F32 = mybir.dt.float32
BF16 = mybir.dt.bfloat16
I32 = mybir.dt.int32
AF = mybir.ActivationFunctionType
ALU = mybir.AluOpType
AX = mybir.AxisListType

ENGINES = ("tensor", "vector", "scalar", "gpsimd", "sync")


class Prog:

    def __init__(self, nc):
        self.nc = nc
        self.steps = {e: [] for e in ENGINES}
        self.sems = {e: nc.alloc_semaphore(f"prog_{e}") for e in ENGINES}
        self.counts = {e: 0 for e in ENGINES}
        self.dma_sems = {}
        self.dma_counts = {}
        self.watermark = {e: {} for e in ENGINES}
        self.writer = {}
        self.readers = {}
        self.n_waits = 0

    def dma_sem(self, name):
        if name not in self.dma_sems:
            self.dma_sems[name] = self.nc.alloc_semaphore(f"dma_{name}")
            self.dma_counts[name] = 0
        return self.dma_sems[name]

    def _sem_by_name(self, name):
        if name in self.dma_sems:
            return self.dma_sems[name]
        return self.sems[name]

    def _edge(self, eng, tok, waits):
        if tok is None:
            return
        sname, val = tok
        if val <= self.watermark[eng].get(sname, 0):
            return
        waits[sname] = max(waits.get(sname, 0), val)

    def step(self, eng, build, reads=(), writes=(), dma=None, n_inc=1, tag=""):
        waits = {}
        for t in reads:
            self._edge(eng, self.writer.get(t), waits)
        for t in writes:
            self._edge(eng, self.writer.get(t), waits)
            for tok in self.readers.get(t, ()):
                self._edge(eng, tok, waits)
        if dma is None:
            self.counts[eng] += 1
            tok = (eng, self.counts[eng])
            sem = self.sems[eng]
            inc_val = 1
        else:
            self.dma_sem(dma)
            self.dma_counts[dma] += 16 * n_inc
            tok = (dma, self.dma_counts[dma])
            sem = self.dma_sems[dma]
            inc_val = 16 * n_inc
        for t in writes:
            self.writer[t] = tok
            self.readers[t] = []
        for t in reads:
            self.readers.setdefault(t, []).append(tok)
        for sname, val in waits.items():
            self.watermark[eng][sname] = val
        self.n_waits += len(waits)
        self.steps[eng].append((dict(waits), build, sem, inc_val, tag))
        return tok

    def seal_dma(self, stream, tiles):
        tok = (stream, self.dma_counts[stream])
        for t in tiles:
            self.writer[t] = tok
            self.readers[t] = []

    def fence(self, new_names, old_names):
        toks = []
        for o in old_names:
            w = self.writer.pop(o, None)
            if w:
                toks.append(w)
            toks.extend(self.readers.pop(o, []))
        for nn in new_names:
            self.readers.setdefault(nn, []).extend(toks)

    def wait_tokens(self, eng, toks):
        waits = {}
        for tok in toks:
            self._edge(eng, tok, waits)
        for sname, val in waits.items():
            self.watermark[eng][sname] = val
        if waits:
            self.steps[eng].append((waits, None, None, 0, "wait"))
        self.n_waits += len(waits)

    def emit(self, block):
        def mk(eng_name):
            def run(eng):
                for waits, build, sem, inc_val, tag in self.steps[eng_name]:
                    for sname, val in waits.items():
                        eng.wait_ge(self._sem_by_name(sname), val)
                    if build is not None:
                        last = build(eng)
                        last.then_inc(sem, inc_val)
            return run
        block.tensor(mk("tensor"))
        block.vector(mk("vector"))
        block.scalar(mk("scalar"))
        block.gpsimd(mk("gpsimd"))
        block.sync(mk("sync"))


B, L, D = 16, 4096, 1024
NCORES = 8
BPC = B // NCORES
TPC = BPC * L
T = 1024
NCHUNK = TPC // T
NFC = D // 128
NTG = T // 128
NG = NCHUNK * NTG
H, DH = 16, 64
E, FF, CAP = 8, 4096, 640
NMF = FF // 128
NSLOT = E * CAP
EPS = 1e-5
BIG = 1.0e6
GC1, GC2 = 0.7978845608028654, 0.044715 * 0.7978845608028654
RG = [list(range(NCORES))]
KB = 1024

A_NAMES = [f"aw{s}{n}" for s in range(2) for n in range(2)]
B_NAMES = [f"bw{s}{n}" for s in range(2) for n in range(2)]
ZCQ_NAMES = ["zc0", "zc1", "zq0", "zq1"]
XFM_NAMES = [f"x_fm{q}" for q in range(4)]
SCR_NAMES = [f"scr{i}" for i in range(4)]
XT_NAMES = ["xt0", "xt1"]
SLAB_NAMES = [f"slab{i}_{n}" for i in range(8) for n in range(2)]
B16_NAMES = [f"b16_{i}_{n}" for i in range(8) for n in range(2)]
HID_NAMES = [f"hid{m}" for m in range(32)]
Y2_NAMES = [f"y2_{t}" for t in range(8)]
HE_NAMES = [f"he{m}" for m in range(32)]
MFM_NAMES = [f"mfm{fc}" for fc in range(8)]
EOFM_NAMES = [f"eo{m}" for m in range(8)]


def _tile4(w):
    O, I = w.shape
    mc, kc = O // 128, I // 128
    t = w.reshape(mc, 128, kc, 128)
    return np.ascontiguousarray(t.transpose(0, 3, 2, 1))


def _rhs_tiles(w):
    I, N = w.shape
    kc = I // 128
    t = w.reshape(kc, 128, N)
    return np.ascontiguousarray(t.transpose(1, 0, 2))


def _cols(v):
    return np.ascontiguousarray(v.reshape(-1, 128).T)


def host_prepare(inputs):
    emb = np.asarray(inputs["L0_attn_emb"], dtype=np.float32)
    att = {k[8:]: np.asarray(v, dtype=np.float32)
           for k, v in inputs.items() if k.startswith("L0_attn_")}
    ffn = {k[7:]: np.asarray(v, dtype=np.float32)
           for k, v in inputs.items() if k.startswith("L0_ffn_")}
    shared = {
        "waT": _tile4(att["Wa_w"]), "wbT": _tile4(att["Wb_w"]),
        "woT": _tile4(att["Wo_w"]), "cT": _tile4(att["C_w"]),
        "f1T": _tile4(att["f1_w"]), "f2T": _tile4(att["f2_w"]),
        "pT": _rhs_tiles(np.ascontiguousarray(att["p_w"].T)),
        "praw": _tile4(np.ascontiguousarray(att["p_w"].T)),
        "gw": None,
        "gb": None,
        "ab": _cols(att["Wa_b"]), "bb": _cols(att["Wb_b"]),
        "cb": _cols(att["C_b"]), "wob": _cols(att["Wo_b"]),
        "lng": _cols(att["ln_g"]), "lnb": _cols(att["ln_b"]),
        "f1b": _cols(att["f1_b"]), "f2b": _cols(att["f2_b"]),
        "pb": att["p_b"].reshape(1, D), "pbc": _cols(att["p_b"]),
        "svec": _cols(np.repeat(att["head_weights"], DH)),
        "ident": np.eye(128, dtype=np.float32),
    }
    shared["gw"] = _rhs_tiles(ffn["gate_w"])
    shared["gb"] = ffn["gate_b"].reshape(1, E)
    per_core = []
    for r in range(NCORES):
        sh = emb[2 * r:2 * r + 2]
        d = dict(shared)
        d["x_tok"] = np.ascontiguousarray(sh)
        d["x_fm"] = np.ascontiguousarray(sh.transpose(0, 2, 1))
        d["w1r"] = _tile4(np.ascontiguousarray(ffn["w1"][r].T))
        d["w2r"] = _tile4(np.ascontiguousarray(ffn["w2"][r].T))
        d["b1r"] = _cols(ffn["b1"][r])
        d["b2r"] = _cols(ffn["b2"][r])
        d["trimask"] = (np.arange(NCORES) < r).astype(np.float32).reshape(NCORES, 1)
        per_core.append(d)
    return per_core


def build_nc(n_chunks=NCHUNK, dbg=False, dumps=()):
    nc = bass.Bass("TRN2", target_bir_lowering=False, debug=False,
                   num_devices=NCORES)

    di = {}
    def dram_in(name, shape):
        di[name] = nc.dram_tensor(name, list(shape), F32, kind="ExternalInput")
    dram_in("x_tok", (BPC, L, D)); dram_in("x_fm", (BPC, D, L))
    for n in ("waT", "wbT", "woT", "cT", "praw"):
        dram_in(n, (NFC, 128, NFC, 128))
    dram_in("f1T", (NMF, 128, NFC, 128)); dram_in("f2T", (NFC, 128, NMF, 128))
    dram_in("w1r", (NMF, 128, NFC, 128)); dram_in("w2r", (NFC, 128, NMF, 128))
    dram_in("pT", (128, NFC, D)); dram_in("gw", (128, NFC, E)); dram_in("gb", (1, E))
    for n in ("ab", "bb", "cb", "wob", "lng", "lnb", "f2b", "svec", "pbc", "b2r"):
        dram_in(n, (128, NFC))
    dram_in("f1b", (128, NMF)); dram_in("b1r", (128, NMF))
    dram_in("pb", (1, D)); dram_in("ident", (128, 128))
    dram_in("trimask", (NCORES, 1))

    out_d = nc.dram_tensor("out", [BPC, L, D], F32, kind="ExternalOutput")
    x1sp = nc.dram_tensor("x1sp", [TPC, D], F32,
                          kind="ExternalOutput" if dbg else "Internal")
    x1spb = nc.dram_tensor("x1spb", [TPC, D], BF16, kind="Internal")
    disp = nc.dram_tensor("disp", [NSLOT + 1, D], BF16, kind="Internal")
    disp_my = nc.dram_tensor("disp_my", [CAP, D], BF16, kind="Internal")
    cnt_in = nc.dram_tensor("cnt_in", [1, E], F32, kind="Internal")
    cnt_out = nc.dram_tensor("cnt_out", [NCORES, E], F32, kind="Internal")
    eo_my = nc.dram_tensor("eo_my", [CAP, D], F32, kind="Internal")
    wcache = {n: nc.dram_tensor("wc_" + n, [128, NFC, D], BF16, kind="Internal")
              for n in ("waT", "wbT", "cT", "woT", "pT")}
    f1c_d = nc.dram_tensor("wc_f1", [NMF, 128, NFC, 128], BF16, kind="Internal")
    f2c_d = nc.dram_tensor("wc_f2", [NFC, 128, NMF, 128], BF16, kind="Internal")
    eo_full = nc.dram_tensor("eo_full", [NSLOT + 1, D], F32, kind="Internal")

    sb = {}
    def S(name, shape, dt=F32):
        sb[name] = nc.alloc_sbuf_tensor("s_" + name, list(shape), dt)
    for n in ("ab", "bb", "cb", "wob", "cwob", "lng", "lnb", "f2b", "svec",
              "b2r", "pbc"):
        S("c_" + n, (128, NFC))
    S("c_f1b", (128, NMF)); S("c_b1r", (128, NMF))
    S("c_f1be", (128, NMF)); S("c_f1bh", (128, NMF))
    S("pb_row", (1, D)); S("gb_row", (1, E))
    S("ones_row", (1, 128)); S("ones_col", (128, 1)); S("ones8", (128, E))
    S("ones_col_b", (128, 1), BF16)
    S("ident_f", (128, 128)); S("ident_b", (128, 128), BF16)
    S("gw_b", (128, NFC, E), BF16); S("G_b", (128, NFC, E), BF16)
    S("gw_f", (128, NFC, E))
    S("pbgw_row", (1, E)); S("gwsum_bc", (128, E)); S("gb_bc", (128, E))
    S("iota8", (128, E)); S("oadj_bc", (128, E))
    S("carry_h", (128, NFC)); S("carry_c", (E, 1))
    S("trimask", (E, 1)); S("cnts", (E, E)); S("orow", (1, E))
    S("ones_row8", (E, T))
    S("pos_all", (128, NG)); S("eid_all", (128, NG)); S("o_all", (128, NG))
    S("gp_all", (128, NG)); S("gpk_all", (128, NG))
    S("slot_f", (128, NG)); S("slot_i", (128, NG), I32)
    S("fm_all", (128, NG, E))
    S("rt_negmu", (128, 2, NTG)); S("lnvar", (128, NTG)); S("lnvar2", (128, NTG))
    S("rt_istd", (128, 2, NTG))
    S("rowA", (1, T)); S("rowB", (1, T))
    S("emaj", (E, T)); S("emaj_cum", (E, T))
    S("logit_all", (128, 2, NTG, E)); S("lmax", (128, 2, NTG)); S("nlmax", (128, 2, NTG))
    S("lsum", (128, 2, NTG))
    S("cum_tok", (128, NTG, E))
    S("scr8a", (128, 2, NTG, E)); S("scr8b", (128, 2, NTG, E)); S("scr8c", (128, 2, NTG, E))

    ARENA_KB = 172
    rng_ = nc.bump_sbuf(ARENA_KB * KB)
    assert rng_ is not None, "SBUF arena allocation failed"
    base = rng_[0]
    def SA(name, shape, dt, off_kb):
        sb[name] = nc.alloc_sbuf_tensor_at("s_" + name, list(shape), dt,
                                           offset=base + off_kb * KB)
    SA("x_fm", (128, NFC, T), BF16, 0)
    SA("w_a", (128, NFC, D), BF16, 16)
    SA("w_b", (128, NFC, D), BF16, 32)
    SA("moe_tok", (128, 5, D), BF16, 0)
    SA("moe_fm", (128, NFC, CAP), BF16, 10)
    SA("eo_fm", (128, NFC, CAP), F32, 20)
    SA("eo_tok", (128, 2, D), F32, 40)
    SA("f1col", (128, 4, NFC, 128), BF16, 48)
    SA("f2col", (128, 2, NMF, 128), BF16, 56)
    SA("slab", (128, NFC, T), F32, 72)
    SA("b16", (128, NFC, T), BF16, 104)
    SA("he", (128, NMF, CAP), BF16, 72)
    SA("y2", (128, NTG, D), F32, 120)
    SA("hid", (128, NMF, 512), BF16, 120)
    SA("a_sl", (128, 2, T), F32, 152)
    SA("zc_sl", (128, 2, T), BF16, 152)
    SA("zq_sl", (128, 2, T), BF16, 156)
    SA("x1_sl", (128, 2, D), F32, 152)
    SA("b_sl", (128, 2, T), BF16, 160)
    SA("sc_b", (128, 2, D), BF16, 160)
    SA("w_p", (128, NFC, D), BF16, 120)
    SA("sc_bB", (128, 2, T), BF16, 152)
    SA("sc_bC", (128, 2, T), BF16, 156)
    SA("x1q2", (128, 4, T), F32, 32)
    SA("f1e", (128, 16, NFC, 128), BF16, 120)
    SA("x1q", (128, 4, T), F32, 152)
    SA("eogq", (128, 4, T), F32, 16)
    SA("scr", (128, 4, 512), F32, 164)
    SA("xt_sl", (128, 2, D), F32, 164)
    SA("eog", (128, 2, D), F32, 164)

    ps = [nc.alloc_psum_tensor(f"ps{i}", [128, 512], F32) for i in range(6)]
    psB = nc.alloc_psum_tensor("psB", [128, 128], BF16)
    psT = nc.alloc_psum_tensor("psT", [128, 128], F32)

    p = Prog(nc)
    dump_d = {}
    def dump(stage, tile, names, cast=False, c=0):
        if stage not in dumps or c != 0:
            return
        shp = [int(x) for x in sb[tile].shape]
        dump_d[stage] = nc.dram_tensor("dump_" + stage, shp, F32, kind="ExternalOutput")
        eng = "gpsimd" if cast else "sync"
        p.step(eng, (lambda stage=stage, tile=tile: lambda e: e.dma_start(
            out=dump_d[stage].ap(), in_=sb[tile][:]))(),
            reads=list(names), dma="dmp")

    small_loads = [("c_ab", "ab"), ("c_bb", "bb"), ("c_cb", "cb"), ("c_wob", "wob"),
                   ("c_lng", "lng"), ("c_lnb", "lnb"), ("c_f2b", "f2b"),
                   ("c_svec", "svec"), ("c_b2r", "b2r"), ("c_pbc", "pbc"),
                   ("c_f1b", "f1b"), ("c_b1r", "b1r"),
                   ("pb_row", "pb"), ("gb_row", "gb"), ("ident_f", "ident"),
                   ("trimask", "trimask")]
    for tn, dn in small_loads:
        p.step("sync", (lambda tn=tn, dn=dn: lambda e: e.dma_start(out=sb[tn][:], in_=di[dn].ap()))(),
               writes=[tn], dma="lds")
    p.step("gpsimd", lambda e: e.dma_start(out=sb["ident_b"][:], in_=di["ident"].ap()),
           writes=["ident_b"], dma="ldg")
    p.step("gpsimd", lambda e: e.dma_start(out=sb["gw_b"][:], in_=di["gw"].ap()),
           writes=["gw_b"], dma="ldg")
    p.step("sync", lambda e: e.dma_start(out=sb["gw_f"][:], in_=di["gw"].ap()),
           writes=["gw_f"], dma="lds")
    p.seal_dma("lds", [tn for tn, _ in small_loads] + ["gw_f"])
    p.seal_dma("ldg", ["ident_b", "gw_b"])

    p.step("vector", lambda v: v.tensor_add(sb["c_cwob"][:], sb["c_cb"][:], sb["c_wob"][:]),
           reads=["c_cb", "c_wob"], writes=["c_cwob"])
    p.step("vector", lambda v: v.tensor_scalar_mul(sb["c_f1be"][:], sb["c_f1b"][:], 1.0 / math.sqrt(2.0)),
           reads=["c_f1b"], writes=["c_f1be"])
    p.step("vector", lambda v: v.tensor_scalar_mul(sb["c_f1bh"][:], sb["c_f1b"][:], 0.5),
           reads=["c_f1b"], writes=["c_f1bh"])
    for tn, val in [("ones_row", 1.0), ("ones_col", 1.0), ("ones_col_b", 1.0), ("ones8", 1.0),
                    ("ones_row8", 1.0), ("carry_h", 0.0), ("carry_c", 0.0)]:
        p.step("vector", (lambda tn=tn, val=val: lambda v: v.memset(sb[tn][:], val))(), writes=[tn])
    p.step("gpsimd", lambda g: g.iota(sb["iota8"][:], pattern=[[1, E]], base=0,
                                      channel_multiplier=0, allow_small_or_imprecise_dtypes=True),
           writes=["iota8"])

    p.step("vector", lambda v: v.memset(sb["sc_b"][:, 0], 0.0), writes=["sc_b0"])
    dzap = disp.ap()[0:NSLOT, :].rearrange("(n p) d -> n p d", p=128)
    for i in range(NSLOT // 128):
        p.step("sync", (lambda i=i: lambda e: e.dma_start(out=dzap[i], in_=sb["sc_b"][:, 0]))(),
               reads=["sc_b0"], writes=[], dma="dz")
    p.seal_dma("dz", ["disp"])
    disp_zeroed = ("dz", p.dma_counts["dz"])

    p.step("gpsimd", lambda g: g.dma_start(
        out=sb["w_a"][:], in_=di["praw"].ap().rearrange("a p b c -> p a (b c)")),
        writes=["w_a"], dma="ldwa")
    for jc in range(NFC):
        def gmm(t, jc=jc):
            for oc in range(NFC):
                last = t.matmul(psT[:, 0:E], sb["w_a"][:, jc, oc * 128:(oc + 1) * 128],
                                sb["gw_b"][:, oc, :], start=(oc == 0), stop=(oc == NFC - 1))
            return last
        p.step("tensor", gmm, reads=["w_a", "gw_b"], writes=["psT"])
        p.step("vector", (lambda jc=jc: lambda v: v.tensor_copy(sb["G_b"][:, jc, :], psT[:, 0:E]))(),
               reads=["psT"], writes=["G_b"])
    def pbgw_mm(t):
        for kc in range(NFC):
            last = t.matmul(psT[0:1, 0:E], sb["c_pbc"][:, kc:kc + 1], sb["gw_f"][:, kc, :],
                            start=(kc == 0), stop=(kc == NFC - 1))
        return last
    p.step("tensor", pbgw_mm, reads=["c_pbc", "gw_f"], writes=["psT"])
    p.step("vector", lambda v: v.tensor_copy(sb["pbgw_row"][:], psT[0:1, 0:E]),
           reads=["psT"], writes=["pbgw_row"])
    def gwsum_mm(t):
        for kc in range(NFC):
            last = t.matmul(psT[0:1, 0:E], sb["ones_col"][:], sb["gw_f"][:, kc, :],
                            start=(kc == 0), stop=(kc == NFC - 1))
        return last
    p.step("tensor", gwsum_mm, reads=["ones_col", "gw_f"], writes=["psT"])
    p.step("vector", lambda v: v.tensor_copy(sb["rowA"][0:1, 0:E], psT[0:1, 0:E]),
           reads=["psT"], writes=["rowA"])
    def bc_mm(t):
        t.matmul(psT[:, 0:E], sb["ones_row"][:], sb["rowA"][0:1, 0:E], start=True, stop=True)
        return t.matmul(psT[:, E:2 * E], sb["ones_row"][:], sb["gb_row"][:], start=True, stop=True)
    p.step("tensor", bc_mm, reads=["ones_row", "rowA", "gb_row"], writes=["psT"])
    p.step("vector", lambda v: v.tensor_copy(sb["gwsum_bc"][:], psT[:, 0:E]),
           reads=["psT"], writes=["gwsum_bc"])
    p.step("vector", lambda v: v.tensor_copy(sb["gb_bc"][:], psT[:, E:2 * E]),
           reads=["psT"], writes=["gb_bc"])

    def load_w(wn, region, c):
        stream = "ldwa" if region == "w_a" else "ldwb"
        if c == 0:
            if wn == "pT":
                p.step("gpsimd", lambda g: g.dma_start(out=sb[region][:], in_=di["pT"].ap()),
                       writes=[region], dma=stream)
            else:
                p.step("gpsimd", (lambda wn=wn, region=region: lambda g: g.dma_start(
                    out=sb[region][:], in_=di[wn].ap().rearrange("a p b c -> p a (b c)")))(),
                    writes=[region], dma=stream)
            p.step("sync", (lambda wn=wn, region=region: lambda e: e.dma_start(
                out=wcache[wn].ap(), in_=sb[region][:]))(),
                reads=[region], writes=[f"wc_{wn}"], dma="wcw")
        else:
            p.step("sync", (lambda wn=wn, region=region: lambda e: e.dma_start(
                out=sb[region][:], in_=wcache[wn].ap()))(),
                reads=[f"wc_{wn}"], writes=[region], dma=stream)

    rot = [0]
    def nxt():
        rot[0] = (rot[0] + 1) % 4
        return rot[0]

    def plan_routing_dve(c):
        par = c % 2
        LA = sb["logit_all"]
        for tg in range(NTG):
            p.step("vector", (lambda tg=tg, par=par: lambda v: v.tensor_scalar(
                LA[:, par, tg, :], LA[:, par, tg, :],
                sb["rt_istd"][:, par, tg:tg + 1], None, ALU.mult))(),
                reads=[f"pre{par}_{tg}", f"st{par}"], writes=[f"pre{par}_{tg}"], tag="lg_ip")
            p.step("vector", (lambda tg=tg, par=par: lambda v: v.tensor_scalar(
                sb["scr8a"][:, par, tg, :], sb["gwsum_bc"][:],
                sb["rt_negmu"][:, par, tg:tg + 1], sb["rt_istd"][:, par, tg:tg + 1],
                ALU.mult, ALU.mult))(),
                reads=["gwsum_bc", f"nm{par}", f"st{par}"], writes=[f"s8a{par}_{tg}"])
            p.step("vector", (lambda tg=tg, par=par: lambda v: v.tensor_tensor(
                LA[:, par, tg, :], LA[:, par, tg, :], sb["scr8a"][:, par, tg, :],
                ALU.add))(),
                reads=[f"pre{par}_{tg}", f"s8a{par}_{tg}"], writes=[f"pre{par}_{tg}"], tag="lg2_ip")
            p.step("vector", (lambda tg=tg, par=par: lambda v: v.tensor_tensor(
                LA[:, par, tg, :], LA[:, par, tg, :], sb["gb_bc"][:],
                ALU.add))(),
                reads=[f"pre{par}_{tg}", "gb_bc"], writes=[f"pre{par}_{tg}"], tag="lg3_ip")
        p.step("vector", (lambda par=par: lambda v: v.tensor_reduce(
            sb["lmax"][:, par, :], LA[:, par], AX.X, ALU.max))(),
            reads=[f"pre{par}_{tg}" for tg in range(NTG)], writes=[f"lmax{par}"])
        p.step("vector", (lambda par=par: lambda v: v.tensor_scalar_mul(
            sb["nlmax"][:, par, :], sb["lmax"][:, par, :], -0.5))(),
            reads=[f"lmax{par}"], writes=[f"nlmax{par}"])
        for tg in range(NTG):
            gidx = c * NTG + tg
            p.step("vector", (lambda tg=tg, par=par: lambda v: v.tensor_scalar(
                sb["scr8a"][:, par, tg, :], LA[:, par, tg, :],
                sb["lmax"][:, par, tg:tg + 1], None, ALU.is_equal))(),
                reads=[f"pre{par}_{tg}", f"lmax{par}"], writes=[f"s8a{par}_{tg}"])
            p.step("vector", (lambda tg=tg, par=par: lambda v: v.tensor_tensor_scan(
                sb["scr8b"][:, par, tg, :], sb["ones8"][:], sb["scr8a"][:, par, tg, :],
                0.0, ALU.mult, ALU.add))(),
                reads=[f"s8a{par}_{tg}", "ones8"], writes=[f"s8b{par}_{tg}"])
            p.step("vector", (lambda tg=tg, par=par: lambda v: v.tensor_scalar(
                sb["scr8c"][:, par, tg, :], sb["scr8b"][:, par, tg, :], 1.0, None, ALU.is_equal))(),
                reads=[f"s8b{par}_{tg}"], writes=[f"s8c{par}_{tg}"])
            p.step("vector", (lambda tg=tg, par=par, gidx=gidx: lambda v: v.tensor_tensor(
                sb["fm_all"][:, gidx, :], sb["scr8a"][:, par, tg, :], sb["scr8c"][:, par, tg, :],
                ALU.mult))(),
                reads=[f"s8a{par}_{tg}", f"s8c{par}_{tg}"], writes=[f"fm{gidx}"])
            p.step("vector", (lambda tg=tg, par=par, gidx=gidx: lambda v: v.tensor_tensor(
                sb["scr8b"][:, par, tg, :], sb["fm_all"][:, gidx, :], sb["iota8"][:], ALU.mult))(),
                reads=[f"fm{gidx}", "iota8"], writes=[f"s8b{par}_{tg}"])
            p.step("vector", (lambda tg=tg, par=par, gidx=gidx: lambda v: v.tensor_reduce(
                sb["eid_all"][:, gidx:gidx + 1], sb["scr8b"][:, par, tg, :], AX.X, ALU.add))(),
                reads=[f"s8b{par}_{tg}"], writes=[f"eid{gidx}"])
    def plan_routing_pe(c):
        par = c % 2
        LA = sb["logit_all"]
        tbufs = [(psT[0:E, 0:128], "psT"), (ps[4][0:E, 0:128], "ps4"),
                 (ps[5][0:E, 0:128], "ps5")]
        tbufs2 = [(psT[:, 0:E], "psT"), (ps[4][:, 0:E], "ps4"), (ps[5][:, 0:E], "ps5")]
        for tg in range(NTG):
            p.step("scalar", (lambda tg=tg, par=par: lambda s: s.activation(
                sb["scr8b"][:, par, tg, :], LA[:, par, tg, :], AF.Tanh,
                bias=sb["nlmax"][:, par, tg:tg + 1], scale=0.5))(),
                reads=[f"pre{par}_{tg}", f"nlmax{par}"], writes=[f"s8b{par}_{tg}"])
            p.step("vector", (lambda tg=tg, par=par: lambda v: v.tensor_scalar(
                sb["scr8c"][:, par, tg, :], sb["scr8b"][:, par, tg, :], -1.0, 1.0,
                ALU.mult, ALU.add))(),
                reads=[f"s8b{par}_{tg}"], writes=[f"s8c{par}_{tg}"])
            p.step("vector", (lambda tg=tg, par=par: lambda v: v.reciprocal(
                sb["scr8c"][:, par, tg, :], sb["scr8c"][:, par, tg, :]))(),
                reads=[f"s8c{par}_{tg}"], writes=[f"s8c{par}_{tg}"], tag="rcp_ip")
            p.step("vector", (lambda tg=tg, par=par: lambda v: v.tensor_scalar(
                sb["scr8b"][:, par, tg, :], sb["scr8b"][:, par, tg, :], 1.0, 1.0,
                ALU.mult, ALU.add))(),
                reads=[f"s8b{par}_{tg}"], writes=[f"s8b{par}_{tg}"], tag="onep_ip")
            p.step("vector", (lambda tg=tg, par=par: lambda v: v.tensor_tensor(
                sb["scr8b"][:, par, tg, :], sb["scr8b"][:, par, tg, :],
                sb["scr8c"][:, par, tg, :], ALU.mult))(),
                reads=[f"s8b{par}_{tg}", f"s8c{par}_{tg}"], writes=[f"s8b{par}_{tg}"], tag="expv_ip")
            p.step("vector", (lambda tg=tg, par=par: lambda v: v.tensor_reduce(
                sb["lsum"][:, par, tg:tg + 1], sb["scr8b"][:, par, tg, :], AX.X, ALU.add))(),
                reads=[f"s8b{par}_{tg}"], writes=[f"ls{par}_{tg}"])
        p.step("vector", (lambda c=c, par=par: lambda v: v.reciprocal(
            sb["gp_all"][:, c * NTG:(c + 1) * NTG], sb["lsum"][:, par, :]))(),
            reads=[f"ls{par}_{tg}" for tg in range(NTG)], writes=["gp_all"])
        for tg in range(NTG):
            gidx = c * NTG + tg
            buf, bn = tbufs[tg % 3]
            p.step("tensor", (lambda gidx=gidx, buf=buf: lambda t: t.transpose(
                buf, sb["fm_all"][:, gidx, :], sb["ident_f"][:]))(),
                reads=[f"fm{gidx}", "ident_f"], writes=[bn])
            p.step("vector", (lambda tg=tg, buf=buf: lambda v: v.tensor_copy(
                sb["emaj"][:, tg * 128:(tg + 1) * 128], buf))(),
                reads=[bn], writes=[f"em{tg}"])
        p.step("vector", lambda v: v.tensor_tensor_scan(
            sb["emaj_cum"][:], sb["ones_row8"][:], sb["emaj"][:], sb["carry_c"][:],
            ALU.mult, ALU.add),
            reads=[f"em{tg}" for tg in range(NTG)] + ["ones_row8", "carry_c"],
            writes=["emaj_cum"])
        p.step("vector", lambda v: v.tensor_copy(sb["carry_c"][:], sb["emaj_cum"][:, T - 1:T]),
               reads=["emaj_cum"], writes=["carry_c"])
        for tg in range(NTG):
            gidx = c * NTG + tg
            buf, bn = tbufs2[tg % 3]
            p.step("tensor", (lambda tg=tg, buf=buf: lambda t: t.transpose(
                buf, sb["emaj_cum"][:, tg * 128:(tg + 1) * 128],
                sb["ident_f"][0:E, 0:E]))(),
                reads=["emaj_cum", "ident_f"], writes=[bn])
            p.step("vector", (lambda tg=tg, buf=buf: lambda v: v.tensor_copy(
                sb["cum_tok"][:, tg, :], buf))(),
                reads=[bn], writes=[f"ct{tg}"])
            p.step("vector", (lambda tg=tg, par=par, gidx=gidx: lambda v: v.tensor_tensor(
                sb["scr8b"][:, par, tg, :], sb["fm_all"][:, gidx, :], sb["cum_tok"][:, tg, :],
                ALU.mult))(),
                reads=[f"fm{gidx}", f"ct{tg}"], writes=[f"s8b{par}_{tg}"])
            p.step("vector", (lambda tg=tg, par=par, gidx=gidx: lambda v: v.tensor_reduce(
                sb["pos_all"][:, gidx:gidx + 1], sb["scr8b"][:, par, tg, :], AX.X, ALU.add))(),
                reads=[f"s8b{par}_{tg}"], writes=[f"posr{gidx}"])

    for c in range(n_chunks):
        par = c % 2
        bat, l0 = divmod(c * T, L)
        if c > 0:
            plan_routing_dve(c - 1)
        if l0 == 0 and c > 0:
            p.step("vector", lambda v: v.memset(sb["carry_h"][:], 0.0), writes=["carry_h"])
        for q in range(4):
            p.step("gpsimd", (lambda bat=bat, l0=l0, q=q: lambda g: g.dma_start(
                out=sb["x_fm"][:, 2 * q:2 * q + 2, :],
                in_=di["x_fm"].ap()[bat].rearrange("(fc p) l -> p fc l", p=128)
                [:, 2 * q:2 * q + 2, l0:l0 + T]))(),
                writes=[f"x_fm{q}"], dma=f"ldx{q}")
        load_w("waT", "w_a", c); load_w("wbT", "w_b", c)

        p.fence(A_NAMES, ZCQ_NAMES)
        if c == 0:
            p.fence(B_NAMES, ["sc_b0", "sc_b1"])
        for m in range(NFC):
            sl = m % 2
            for which in ("a", "b"):
                wt = "w_a" if which == "a" else "w_b"
                for n in range(2):
                    bank = nxt()
                    def mm(t, m=m, n=n, wt=wt, bank=bank):
                        for kc in range(NFC):
                            last = t.matmul(ps[bank][:], sb[wt][:, m, kc * 128:(kc + 1) * 128],
                                            sb["x_fm"][:, kc, n * 512:(n + 1) * 512],
                                            start=(kc == 0), stop=(kc == NFC - 1))
                        return last
                    p.step("tensor", mm, reads=[wt] + XFM_NAMES, writes=[f"ps{bank}"])
                    if which == "a":
                        p.step("scalar", (lambda n=n, m=m, sl=sl, bank=bank: lambda s: s.activation(
                            sb["a_sl"][:, sl, n * 512:(n + 1) * 512], ps[bank][:], AF.Tanh,
                            bias=sb["c_ab"][:, m:m + 1]))(),
                            reads=[f"ps{bank}", "c_ab"], writes=[f"aw{sl}{n}"])
                    else:
                        p.step("scalar", (lambda n=n, m=m, sl=sl, bank=bank: lambda s: s.activation(
                            sb["b_sl"][:, sl, n * 512:(n + 1) * 512], ps[bank][:], AF.Identity,
                            bias=sb["c_bb"][:, m:m + 1]))(),
                            reads=[f"ps{bank}", "c_bb"], writes=[f"bw{sl}{n}"])
            p.step("vector", (lambda m=m, sl=sl: lambda v: v.tensor_tensor_scan(
                sb["b16"][:, m, :], sb["a_sl"][:, sl, :], sb["b_sl"][:, sl, :],
                sb["carry_h"][:, m:m + 1], ALU.mult, ALU.add))(),
                reads=[f"aw{sl}0", f"aw{sl}1", f"bw{sl}0", f"bw{sl}1", "carry_h"],
                writes=[f"b16_{m}_0", f"b16_{m}_1"], tag="scan")
            p.step("vector", (lambda m=m: lambda v: v.tensor_copy(
                sb["carry_h"][:, m:m + 1], sb["b16"][:, m, T - 1:T]))(),
                reads=[f"b16_{m}_1"], writes=["carry_h"])

        dump("h", "b16", B16_NAMES, cast=True, c=c)
        dump("xfm", "x_fm", XFM_NAMES, cast=True, c=c)
        if c > 0:
            plan_routing_pe(c - 1)
        load_w("cT", "w_a", c); load_w("woT", "w_b", c)

        p.fence(ZCQ_NAMES, A_NAMES)
        if c > 0:
            p.fence(SCR_NAMES, XT_NAMES)
        hreads = [f"b16_{k}_{n}" for k in range(NFC) for n in range(2)]
        smm_q = []
        for m in range(NFC):
            sl = m % 2
            for n in range(2):
                bank = (m * 2 + n) % 2
                def mm(t, m=m, n=n, bank=bank):
                    for kc in range(NFC):
                        t.matmul(ps[bank][:], sb["w_a"][:, m, kc * 128:(kc + 1) * 128],
                                 sb["b16"][:, kc, n * 512:(n + 1) * 512],
                                 start=(kc == 0), stop=False)
                    for kc in range(NFC):
                        last = t.matmul(ps[bank][:], sb["w_b"][:, m, kc * 128:(kc + 1) * 128],
                                        sb["x_fm"][:, kc, n * 512:(n + 1) * 512],
                                        start=False, stop=(kc == NFC - 1))
                    return last
                p.step("tensor", mm, reads=["w_a", "w_b"] + XFM_NAMES
                       + [f"b16_{k}_{n}" for k in range(NFC)],
                       writes=[f"ps{bank}"])
                p.step("vector", (lambda m=m, n=n, bank=bank: lambda v: v.tensor_scalar(
                    sb["slab"][:, m, n * 512:(n + 1) * 512], ps[bank][:],
                    sb["c_cwob"][:, m:m + 1], sb["c_svec"][:, m:m + 1], ALU.add, ALU.mult))(),
                    reads=[f"ps{bank}", "c_cwob", "c_svec"], writes=[f"slab{m}_{n}"])
            p.step("scalar", (lambda m=m, sl=sl: lambda s: s.activation(
                sb["zc_sl"][:, sl, :], sb["slab"][:, m, :], AF.Identity))(),
                reads=[f"slab{m}_0", f"slab{m}_1"], writes=[f"zc{sl}"])
            p.step("scalar", (lambda m=m, sl=sl: lambda s: s.activation(
                sb["zq_sl"][:, sl, :], sb["slab"][:, m, :], AF.Square))(),
                reads=[f"slab{m}_0", f"slab{m}_1"], writes=[f"zq{sl}"])
            def smm(t, m=m, sl=sl):
                t.matmul(ps[2][0:1, :], sb["ones_col_b"][:], sb["zc_sl"][:, sl, 0:512],
                         start=(m == 0), stop=(m == NFC - 1))
                t.matmul(ps[3][0:1, :], sb["ones_col_b"][:], sb["zc_sl"][:, sl, 512:1024],
                         start=(m == 0), stop=(m == NFC - 1))
                t.matmul(ps[4][0:1, :], sb["ones_col_b"][:], sb["zq_sl"][:, sl, 0:512],
                         start=(m == 0), stop=(m == NFC - 1))
                return t.matmul(ps[5][0:1, :], sb["ones_col_b"][:], sb["zq_sl"][:, sl, 512:1024],
                                start=(m == 0), stop=(m == NFC - 1))
            smm_q.append((smm, sl))
            if len(smm_q) == 2:
                smm0, sl0 = smm_q.pop(0)
                p.step("tensor", smm0, reads=[f"zc{sl0}", f"zq{sl0}", "ones_col_b"],
                       writes=["ps2", "ps3", "ps4", "ps5"])
        smm7, sl7 = smm_q.pop(0)
        p.step("tensor", smm7, reads=[f"zc{sl7}", f"zq{sl7}", "ones_col_b"],
               writes=["ps2", "ps3", "ps4", "ps5"])
        dump("z", "slab", SLAB_NAMES, c=c)
        p.step("vector", lambda v: v.tensor_scalar(sb["rowB"][0:1, 0:512], ps[2][0:1, :],
                                                   1.0 / D, None, ALU.mult),
               reads=["ps2"], writes=["rowB"])
        p.step("vector", lambda v: v.tensor_scalar(sb["rowB"][0:1, 512:1024], ps[3][0:1, :],
                                                   1.0 / D, None, ALU.mult),
               reads=["ps3"], writes=["rowB"])
        p.step("vector", lambda v: v.tensor_scalar(sb["rowA"][0:1, 0:512], ps[4][0:1, :],
                                                   1.0 / D, None, ALU.mult),
               reads=["ps4"], writes=["rowA"])
        p.step("vector", lambda v: v.tensor_scalar(sb["rowA"][0:1, 512:1024], ps[5][0:1, :],
                                                   1.0 / D, None, ALU.mult),
               reads=["ps5"], writes=["rowA"])
        p.step("vector", lambda v: v.tensor_mul(ps[2][0:1, 0:512], sb["rowB"][0:1, 0:512],
                                                sb["rowB"][0:1, 0:512]),
               reads=["rowB"], writes=["ps2"])
        p.step("vector", lambda v: v.tensor_mul(ps[3][0:1, 0:512], sb["rowB"][0:1, 512:1024],
                                                sb["rowB"][0:1, 512:1024]),
               reads=["rowB"], writes=["ps3"])
        p.step("vector", lambda v: v.tensor_sub(sb["rowA"][0:1, 0:512], sb["rowA"][0:1, 0:512],
                                                ps[2][0:1, 0:512]),
               reads=["rowA", "ps2"], writes=["rowA"])
        p.step("vector", lambda v: v.tensor_sub(sb["rowA"][0:1, 512:1024], sb["rowA"][0:1, 512:1024],
                                                ps[3][0:1, 0:512]),
               reads=["rowA", "ps3"], writes=["rowA"])
        p.step("vector", lambda v: v.tensor_scalar(sb["rowA"][:], sb["rowA"][:], 1.0, EPS,
                                                   ALU.mult, ALU.add),
               reads=["rowA"], writes=["rowA"], tag="epsadd_ip")
        p.step("scalar", lambda s: s.activation(sb["rowA"][:], sb["rowA"][:], AF.Sqrt),
               reads=["rowA"], writes=["rowA"], tag="sqrt_ip")
        p.step("vector", lambda v: v.reciprocal(sb["rowA"][:], sb["rowA"][:]),
               reads=["rowA"], writes=["rowA"], tag="rcp_ip")
        p.step("vector", lambda v: v.scalar_tensor_tensor(sb["rowB"][:], sb["rowB"][:], -1.0,
                                                          sb["rowA"][:], ALU.mult, ALU.mult),
               reads=["rowB", "rowA"], writes=["rowB"], tag="negmuistd_ip")
        for n in range(2):
            bA, bB = (4, 5) if n == 0 else (2, 3)
            p.step("tensor", (lambda n=n, bA=bA: lambda t: t.matmul(
                ps[bA][:], sb["ones_row"][:], sb["rowA"][0:1, n * 512:(n + 1) * 512],
                start=True, stop=True))(),
                reads=["ones_row", "rowA"], writes=[f"ps{bA}"])
            p.step("tensor", (lambda n=n, bB=bB: lambda t: t.matmul(
                ps[bB][:], sb["ones_row"][:], sb["rowB"][0:1, n * 512:(n + 1) * 512],
                start=True, stop=True))(),
                reads=["ones_row", "rowB"], writes=[f"ps{bB}"])
            for m in range(NFC):
                s0, s1 = (0, 1) if m % 2 == 0 else (2, 3)
                p.step("vector", (lambda m=m, n=n, bA=bA, s0=s0: lambda v: v.tensor_tensor(
                    sb["scr"][:, s0, :], sb["slab"][:, m, n * 512:(n + 1) * 512],
                    ps[bA][:], ALU.mult))(),
                    reads=[f"slab{m}_{n}", f"ps{bA}"], writes=[f"scr{s0}"])
                p.step("vector", (lambda m=m, n=n, bB=bB, s0=s0, s1=s1: lambda v: v.tensor_tensor(
                    sb["scr"][:, s1, :], sb["scr"][:, s0, :], ps[bB][:], ALU.add))(),
                    reads=[f"scr{s0}", f"ps{bB}"], writes=[f"scr{s1}"])
                p.step("gpsimd", (lambda m=m, s0=s0, s1=s1: lambda g: g.tensor_scalar(
                    sb["scr"][:, s0, :], sb["scr"][:, s1, :], sb["c_lng"][:, m:m + 1],
                    sb["c_lnb"][:, m:m + 1], ALU.mult, ALU.add))(),
                    reads=[f"scr{s1}", "c_lng", "c_lnb"], writes=[f"scr{s0}"])
                p.step("gpsimd", (lambda m=m, n=n, s0=s0: lambda g: g.tensor_tensor(
                    sb["slab"][:, m, n * 512:(n + 1) * 512],
                    sb["slab"][:, m, n * 512:(n + 1) * 512], sb["scr"][:, s0, :], ALU.add))(),
                    reads=[f"scr{s0}", f"slab{m}_{n}"], writes=[f"slab{m}_{n}"], tag="u_ip")
                p.step("scalar", (lambda m=m, n=n: lambda s: s.activation(
                    sb["b16"][:, m, n * 512:(n + 1) * 512],
                    sb["slab"][:, m, n * 512:(n + 1) * 512], AF.Identity))(),
                    reads=[f"slab{m}_{n}"], writes=[f"b16_{m}_{n}"], tag="ub16")

        dump("u", "slab", SLAB_NAMES, c=c)
        dump("ub", "b16", B16_NAMES, cast=True, c=c)
        first = (c == 0)
        def f1load(mi, slot, fresh=first):
            if fresh:
                p.step("gpsimd", (lambda mi=mi, slot=slot: lambda g: g.dma_start(
                    out=sb["f1col"][:, slot], in_=di["f1T"].ap()[mi]))(),
                    writes=[f"f1c{slot}"], dma=f"ldf1_{mi % 8}")
                p.step("sync", (lambda mi=mi, slot=slot: lambda e: e.dma_start(
                    out=f1c_d.ap()[mi], in_=sb["f1col"][:, slot]))(),
                    reads=[f"f1c{slot}"], writes=[f"wcf1_{mi}"], dma="wcw")
            else:
                p.step("sync", (lambda mi=mi, slot=slot: lambda e: e.dma_start(
                    out=sb["f1col"][:, slot], in_=f1c_d.ap()[mi]))(),
                    reads=[f"wcf1_{mi}"], writes=[f"f1c{slot}"], dma=f"ldf1_{mi % 8}")
        def f2load(mi, slot, fresh=first):
            if fresh:
                p.step("gpsimd", (lambda mi=mi, slot=slot: lambda g: g.dma_start(
                    out=sb["f2col"][:, slot], in_=di["f2T"].ap()[mi]))(),
                    writes=[f"f2c{slot}"], dma=f"ldf2_{mi % 4}")
                p.step("sync", (lambda mi=mi, slot=slot: lambda e: e.dma_start(
                    out=f2c_d.ap()[mi], in_=sb["f2col"][:, slot]))(),
                    reads=[f"f2c{slot}"], writes=[f"wcf2_{mi}"], dma="wcw")
            else:
                p.step("sync", (lambda mi=mi, slot=slot: lambda e: e.dma_start(
                    out=sb["f2col"][:, slot], in_=f2c_d.ap()[mi]))(),
                    reads=[f"wcf2_{mi}"], writes=[f"f2c{slot}"], dma=f"ldf2_{slot}")
        for sl2 in range(2):
            fresh = (c == 0 and sl2 == 0)
            to = sl2 * 512
            f1load(0, 0, fresh); f1load(1, 1, fresh)
            for m in range(NMF):
                slot = m % 4
                if m + 2 < NMF:
                    f1load(m + 2, (m + 2) % 4, fresh)
                bank = nxt()
                def mm(t, m=m, to=to, slot=slot, bank=bank):
                    for kc in range(NFC):
                        last = t.matmul(ps[bank][:], sb["f1col"][:, slot, kc, :],
                                        sb["b16"][:, kc, to:to + 512],
                                        start=(kc == 0), stop=(kc == NFC - 1))
                    return last
                p.step("tensor", mm, reads=[f"f1c{slot}"]
                       + [f"b16_{k}_{sl2}" for k in range(NFC)],
                       writes=[f"ps{bank}"])
                sc = 2 * (m % 2)
                p.step("scalar", (lambda m=m, bank=bank, sc=sc: lambda s: s.activation(
                    sb["scr"][:, sc, :], ps[bank][:], AF.Erf,
                    bias=sb["c_f1be"][:, m:m + 1], scale=1.0 / math.sqrt(2.0)))(),
                    reads=[f"ps{bank}", "c_f1be"], writes=[f"scr{sc}"])
                p.step("scalar", (lambda m=m, bank=bank, sc=sc: lambda s: s.activation(
                    sb["scr"][:, sc + 1, :], ps[bank][:], AF.Identity,
                    bias=sb["c_f1bh"][:, m:m + 1], scale=0.5))(),
                    reads=[f"ps{bank}", "c_f1bh"], writes=[f"scr{sc + 1}"])
                p.step("vector", (lambda m=m, sc=sc: lambda v: v.scalar_tensor_tensor(
                    sb["hid"][:, m, :], sb["scr"][:, sc, :], 1.0, sb["scr"][:, sc + 1, :],
                    ALU.add, ALU.mult))(),
                    reads=[f"scr{sc}", f"scr{sc + 1}"], writes=[f"hid{m}"])
            f2load(0, 0, fresh); f2load(1, 1, fresh)
            for m2 in range(NFC):
                slot = m2 % 2
                bank = nxt()
                def mm2(t, m2=m2, to=to, slot=slot, bank=bank):
                    for kc in range(NMF):
                        last = t.matmul(ps[bank][:], sb["f2col"][:, slot, kc, :],
                                        sb["hid"][:, kc, :],
                                        start=(kc == 0), stop=(kc == NMF - 1))
                    return last
                p.step("tensor", mm2, reads=[f"f2c{slot}"] + HID_NAMES, writes=[f"ps{bank}"])
                if m2 + 2 < NFC:
                    f2load(m2 + 2, (m2 + 2) % 2, fresh)
                p.step("vector", (lambda m2=m2, to=to, bank=bank, sl2=sl2: lambda v: v.scalar_tensor_tensor(
                    sb["slab"][:, m2, to:to + 512], ps[bank][:], sb["c_f2b"][:, m2:m2 + 1],
                    sb["slab"][:, m2, to:to + 512], ALU.add, ALU.add))(),
                    reads=[f"ps{bank}", "c_f2b", f"slab{m2}_{sl2}"],
                    writes=[f"slab{m2}_{sl2}"], tag="y_ip")
                p.step("vector", (lambda m2=m2, to=to, sl2=sl2: lambda v: v.tensor_copy(
                    sb["b16"][:, m2, to:to + 512], sb["slab"][:, m2, to:to + 512]))(),
                    reads=[f"slab{m2}_{sl2}"], writes=[f"b16_{m2}_{sl2}"], tag="yb16")

        dump("y", "slab", SLAB_NAMES, c=c)
        dump("hid", "hid", HID_NAMES, cast=True, c=c)
        p.fence(XT_NAMES, SCR_NAMES)
        p.fence(["w_p"], HID_NAMES)
        load_w("pT", "w_p", c)
        xg = di["x_tok"].ap()[bat].rearrange("(g p) d -> g p d", p=128)
        for tg in range(NTG):
            gbank = 4 + (tg % 2)
            def mmg(t, tg=tg, gbank=gbank):
                for kc in range(NFC):
                    t.matmul(ps[gbank][:, 0:E], sb["x_fm"][:, kc, tg * 128:(tg + 1) * 128],
                             sb["gw_b"][:, kc, :], start=(kc == 0), stop=False)
                for kc in range(NFC):
                    t.matmul(ps[gbank][:, 0:E], sb["b16"][:, kc, tg * 128:(tg + 1) * 128],
                             sb["G_b"][:, kc, :], start=False, stop=False)
                return t.matmul(ps[gbank][:, 0:E], sb["ones_row"][:], sb["pbgw_row"][:],
                                start=False, stop=True)
            gbank = 4 + (tg % 2)
            p.step("tensor", (lambda mmg=mmg, gbank=gbank, tg=tg: lambda t: mmg(t))(),
                   reads=[f"b16_{k}_{tg // 4}" for k in range(NFC)] + XFM_NAMES
                   + ["gw_b", "G_b", "ones_row", "pbgw_row"],
                   writes=[f"ps{gbank}"], tag="ggate")
            p.step("vector", (lambda tg=tg, par=par, gbank=gbank: lambda v: v.tensor_copy(
                sb["logit_all"][:, par, tg, :], ps[gbank][:, 0:E]))(),
                reads=[f"ps{gbank}"], writes=[f"pre{par}_{tg}"])
        for tg in range(NTG):
            xs = tg % 2
            p.step("sync", (lambda l0=l0, tg=tg, xs=xs, xg=xg: lambda e: e.dma_start(
                out=sb["xt_sl"][:, xs], in_=xg[l0 // 128 + tg]))(),
                writes=[f"xt{xs}"], dma=f"ldxt{xs}")
            for n in range(2):
                bank = nxt()
                def mmp(t, tg=tg, n=n, bank=bank):
                    for kc in range(NFC):
                        t.matmul(ps[bank][:], sb["b16"][:, kc, tg * 128:(tg + 1) * 128],
                                 sb["w_p"][:, kc, n * 512:(n + 1) * 512],
                                 start=(kc == 0), stop=False)
                    return t.matmul(ps[bank][:], sb["ones_row"][:],
                                    sb["pb_row"][0:1, n * 512:(n + 1) * 512],
                                    start=False, stop=True)
                p.step("tensor", mmp,
                       reads=[f"b16_{k}_{tg // 4}" for k in range(NFC)]
                       + ["w_p", "ones_row", "pb_row"],
                       writes=[f"ps{bank}"])
                p.step("vector", (lambda tg=tg, n=n, xs=xs, bank=bank: lambda v: v.tensor_tensor(
                    sb["slab"][:, tg, n * 512:(n + 1) * 512], ps[bank][:],
                    sb["xt_sl"][:, xs, n * 512:(n + 1) * 512], ALU.add))(),
                    reads=[f"ps{bank}", f"xt{xs}"], writes=[f"slab{tg}_{n}"])
            p.step("vector", (lambda tg=tg, par=par: lambda v: v.tensor_reduce(
                sb["rt_negmu"][:, par, tg:tg + 1], sb["slab"][:, tg, :], AX.X, ALU.add,
                negate=True))(),
                reads=[f"slab{tg}_0", f"slab{tg}_1"], writes=[f"negsum{par}_{tg}"])
        p.step("vector", (lambda par=par: lambda v: v.tensor_scalar(
            sb["rt_negmu"][:, par, :], sb["rt_negmu"][:, par, :],
            1.0 / D, None, ALU.mult))(),
            reads=[f"negsum{par}_{tg}" for tg in range(NTG)], writes=[f"nm{par}"],
            tag="negmu_ip")
        for tg in range(NTG):
            bank = 4 if tg % 2 == 0 else 5
            p.step("scalar", (lambda tg=tg, par=par, bank=bank: lambda s: s.activation(
                ps[bank][:], sb["slab"][:, tg, 0:512], AF.Square,
                bias=sb["rt_negmu"][:, par, tg:tg + 1], accum_out=sb["lnvar"][:, tg:tg + 1]))(),
                reads=[f"slab{tg}_0", f"nm{par}"], writes=[f"ps{bank}", f"lnv{tg}"])
            p.step("scalar", (lambda tg=tg, par=par, bank=bank: lambda s: s.activation(
                ps[bank][:], sb["slab"][:, tg, 512:1024], AF.Square,
                bias=sb["rt_negmu"][:, par, tg:tg + 1], accum_out=sb["lnvar2"][:, tg:tg + 1]))(),
                reads=[f"slab{tg}_1", f"nm{par}"], writes=[f"ps{bank}", f"lnw{tg}"])
        p.step("vector", lambda v: v.tensor_add(sb["lnvar"][:], sb["lnvar"][:], sb["lnvar2"][:]),
               reads=[f"lnv{tg}" for tg in range(NTG)] + [f"lnw{tg}" for tg in range(NTG)],
               writes=["lnvar"], tag="lnvar_ip")
        p.step("vector", lambda v: v.tensor_scalar(sb["lnvar"][:], sb["lnvar"][:],
                                                   1.0 / D, EPS, ALU.mult, ALU.add),
               reads=["lnvar"], writes=["lnvar"], tag="lnvar_ip2")
        p.step("scalar", (lambda par=par: lambda s: s.activation(
            sb["rt_istd"][:, par, :], sb["lnvar"][:], AF.Sqrt))(),
            reads=["lnvar"], writes=[f"st{par}"])
        p.step("vector", (lambda par=par: lambda v: v.reciprocal(
            sb["rt_istd"][:, par, :], sb["rt_istd"][:, par, :]))(),
            reads=[f"st{par}"], writes=[f"st{par}"], tag="istd_ip")
        for tg in range(NTG):
            p.step("vector", (lambda tg=tg, par=par: lambda v: v.tensor_scalar(
                sb["slab"][:, tg, :], sb["slab"][:, tg, :],
                sb["rt_negmu"][:, par, tg:tg + 1], sb["rt_istd"][:, par, tg:tg + 1],
                ALU.add, ALU.mult))(),
                reads=[f"slab{tg}_0", f"slab{tg}_1", f"nm{par}", f"st{par}"],
                writes=[f"slab{tg}_0", f"slab{tg}_1"], tag="x1_ip")
        p.step("sync", (lambda c=c: lambda e: e.dma_start(
            out=x1sp.ap().rearrange("(cc g p) d -> cc p g d", cc=NCHUNK, p=128)[c],
            in_=sb["slab"][:]))(),
            reads=SLAB_NAMES, writes=["x1sp"], dma="spw")
        for m in range(NFC):
            p.step("scalar", (lambda m=m: lambda s: s.activation(
                sb["b16"][:, m, :], sb["slab"][:, m, :], AF.Identity))(),
                reads=[f"slab{m}_0", f"slab{m}_1"],
                writes=[f"b16_{m}_0", f"b16_{m}_1"])
        p.step("sync", (lambda c=c: lambda e: e.dma_start(
            out=x1spb.ap().rearrange("(cc g p) d -> cc p g d", cc=NCHUNK, p=128)[c],
            in_=sb["b16"][:]))(),
            reads=B16_NAMES, writes=["x1spb"], dma="spwb")

    plan_routing_dve(n_chunks - 1)
    plan_routing_pe(n_chunks - 1)
    p.step("vector", lambda v: v.tensor_scalar(sb["pos_all"][:], sb["pos_all"][:],
                                               -1.0, None, ALU.add),
           reads=[f"posr{g}" for g in range(n_chunks * NTG)], writes=["pos_all"],
           tag="pos_ip")

    if dbg:
        dbgs = {}
        for dn, tile in [("dbg_pos", "pos_all"), ("dbg_eid", "eid_all"),
                         ("dbg_gp", "gp_all"), ("dbg_fm", "fm_all")]:
            shp = list(sb[tile].shape)
            dbgs[dn] = nc.dram_tensor(dn, shp, F32, kind="ExternalOutput")
            p.step("sync", (lambda dn=dn, tile=tile: lambda e: e.dma_start(
                out=dbgs[dn].ap(), in_=sb[tile][:]))(),
                reads=[tile], dma="sto")
        dbgs["dbg_cnt"] = nc.dram_tensor("dbg_cnt", [E, 1], F32, kind="ExternalOutput")
        p.step("sync", lambda e: e.dma_start(out=dbgs["dbg_cnt"].ap(), in_=sb["carry_c"][:]),
               reads=["carry_c"], dma="sto")
        if n_chunks < NCHUNK:
            p.wait_tokens("sync", [("spw", p.dma_counts.get("spw", 0)),
                                   ("sto", p.dma_counts.get("sto", 0)),
                                   ("dmp", p.dma_counts.get("dmp", 0))])
            with nc.Block() as block:
                p.emit(block)
            return nc
    elif n_chunks < NCHUNK:
        p.wait_tokens("sync", [("spw", p.dma_counts.get("spw", 0))])
        with nc.Block() as block:
            p.emit(block)
        return nc

    p.step("sync", lambda e: e.dma_start(out=cnt_in.ap().rearrange("a b -> b a"), in_=sb["carry_c"][:]),
           reads=["carry_c"], writes=["cnt_in"], dma="cnt")
    p.step("gpsimd", lambda g: g.collective_compute(
        "AllGather", ALU.bypass, replica_groups=RG,
        ins=[flatten_dims_for_collective(cnt_in.ap())],
        outs=[flatten_dims_for_collective(cnt_out.ap())]),
        reads=["cnt_in"], writes=["cnt_out"], tag="ag_cnt")
    p.step("sync", lambda e: e.dma_start(out=sb["cnts"][:], in_=cnt_out.ap()),
           reads=["cnt_out"], writes=["cnts"], dma="cnt")
    p.step("tensor", lambda t: t.matmul(psT[0:1, 0:E], sb["trimask"][:], sb["cnts"][:],
                                        start=True, stop=True),
           reads=["trimask", "cnts"], writes=["psT"])
    p.step("vector", lambda v: v.tensor_copy(sb["orow"][:], psT[0:1, 0:E]),
           reads=["psT"], writes=["orow"])
    p.step("tensor", lambda t: t.matmul(psT[:, 0:E], sb["ones_row"][:], sb["orow"][:],
                                        start=True, stop=True),
           reads=["ones_row", "orow"], writes=["psT"])
    p.step("vector", lambda v: v.tensor_scalar(sb["oadj_bc"][:], sb["iota8"][:],
                                               float(CAP), None, ALU.mult),
           reads=["iota8"], writes=["oadj_bc"])
    p.step("vector", lambda v: v.tensor_tensor(sb["oadj_bc"][:], sb["oadj_bc"][:],
                                               psT[:, 0:E], ALU.add),
           reads=["psT", "oadj_bc"], writes=["oadj_bc"], tag="oadj_ip")
    for g in range(NG):
        j = g % NTG
        p.step("vector", (lambda g=g, j=j: lambda v: v.tensor_tensor(
            sb["scr8b"][:, 0, j, :], sb["fm_all"][:, g, :], sb["oadj_bc"][:], ALU.mult))(),
            reads=[f"fm{g}", "oadj_bc"], writes=[f"s8b0_{j}"])
        p.step("vector", (lambda g=g, j=j: lambda v: v.tensor_reduce(
            sb["o_all"][:, g:g + 1], sb["scr8b"][:, 0, j, :], AX.X, ALU.add))(),
            reads=[f"s8b0_{j}"], writes=[f"oa{g}"])
    oreads = [f"oa{g}" for g in range(NG)]
    p.step("vector", lambda v: v.tensor_tensor(sb["slot_f"][:], sb["pos_all"][:],
                                               sb["o_all"][:], ALU.add),
           reads=["pos_all"] + oreads, writes=["slot_f"])
    p.step("vector", lambda v: v.tensor_scalar(sb["o_all"][:], sb["eid_all"][:],
                                               float(CAP), None, ALU.mult),
           reads=["eid_all"], writes=["o_all"])
    p.step("vector", lambda v: v.tensor_tensor(sb["o_all"][:], sb["slot_f"][:],
                                               sb["o_all"][:], ALU.subtract),
           reads=["slot_f", "o_all"], writes=["o_all"], tag="posg_ip")
    p.step("vector", lambda v: v.tensor_scalar(sb["o_all"][:], sb["o_all"][:],
                                               float(CAP), None, ALU.is_lt),
           reads=["o_all"], writes=["o_all"], tag="keep_ip")
    p.step("vector", lambda v: v.tensor_tensor(sb["gpk_all"][:], sb["gp_all"][:],
                                               sb["o_all"][:], ALU.mult),
           reads=["gp_all", "o_all"], writes=["gpk_all"])
    p.step("vector", lambda v: v.tensor_scalar(sb["pos_all"][:], sb["o_all"][:],
                                               -float(NSLOT), float(NSLOT), ALU.mult, ALU.add),
           reads=["o_all"], writes=["pos_all"], tag="trash_term")
    p.step("vector", lambda v: v.tensor_tensor(sb["slot_f"][:], sb["slot_f"][:],
                                               sb["o_all"][:], ALU.mult),
           reads=["slot_f", "o_all"], writes=["slot_f"], tag="slot_ip")
    p.step("vector", lambda v: v.tensor_tensor(sb["slot_f"][:], sb["slot_f"][:],
                                               sb["pos_all"][:], ALU.add),
           reads=["slot_f", "pos_all"], writes=["slot_f"], tag="slot_ip2")
    p.step("vector", lambda v: v.tensor_copy(sb["slot_i"][:], sb["slot_f"][:]),
           reads=["slot_f"], writes=["slot_i"])
    x1r = x1sp.ap().rearrange("(g p) d -> g p d", p=128)
    x1rb = x1spb.ap().rearrange("(g p) d -> g p d", p=128)
    p.fence(["sc_b0", "sc_b1"], B_NAMES)
    p.fence(["sc_b2", "sc_b3", "sc_b4", "sc_b5"],
            A_NAMES + ["zc0", "zc1", "zq0", "zq1"])
    p.wait_tokens("gpsimd", [disp_zeroed])
    for g in range(NG):
        slot = g % 6
        sct = (sb["sc_b"][:, slot] if slot < 2 else
               sb["sc_bB"][:, slot - 2] if slot < 4 else sb["sc_bC"][:, slot - 4])
        p.step("sync", (lambda g=g, sct=sct: lambda e: e.dma_start(
            out=sct, in_=x1rb[g]))(),
            reads=["x1spb"], writes=[f"sc_b{slot}"], dma=f"scb{slot}")
        p.step("gpsimd", (lambda g=g, sct=sct: lambda e: e.indirect_dma_start(
            out=disp.ap(), out_offset=IndirectOffsetOnAxis(sb["slot_i"][:, g:g + 1], 0),
            in_=sct, in_offset=None))(),
            reads=[f"sc_b{slot}", "slot_i"], writes=[], dma=f"scbi{slot}")

    p.wait_tokens("gpsimd", [(f"scbi{s}", p.dma_counts.get(f"scbi{s}", 0))
                             for s in range(6)])
    p.step("gpsimd", lambda g: g.collective_compute(
        "ReduceScatter", ALU.add, replica_groups=RG,
        ins=[flatten_dims_for_collective(disp.ap()[0:NSLOT, :])],
        outs=[flatten_dims_for_collective(disp_my.ap())]),
        reads=["disp"], writes=["disp_my"], tag="rs")
    if dbg:
        d_dm = nc.dram_tensor("d_dispmy", [CAP, D], F32, kind="ExternalOutput")
        p.step("gpsimd", lambda g: g.dma_start(out=d_dm.ap(), in_=disp_my.ap()),
               reads=["disp_my"], dma="dmp2")
        d_sf = nc.dram_tensor("d_slotf", [128, NG], F32, kind="ExternalOutput")
        p.step("sync", lambda e: e.dma_start(out=d_sf.ap(), in_=sb["slot_f"][:]),
               reads=["slot_f"], dma="dmp2")
        d_gk = nc.dram_tensor("d_gpk", [128, NG], F32, kind="ExternalOutput")
        p.step("sync", lambda e: e.dma_start(out=d_gk.ap(), in_=sb["gpk_all"][:]),
               reads=["gpk_all"], dma="dmp2")
        d_or = nc.dram_tensor("d_orow", [1, E], F32, kind="ExternalOutput")
        p.step("sync", lambda e: e.dma_start(out=d_or.ap(), in_=sb["orow"][:]),
               reads=["orow"], dma="dmp2")
    p.fence(MFM_NAMES + EOFM_NAMES + ["eot0", "eot1"],
            XFM_NAMES + ["w_a", "w_b"])
    p.fence(HE_NAMES, SLAB_NAMES + B16_NAMES)
    for fc in range(NFC):
        p.step("sync", (lambda fc=fc: lambda e: e.dma_start(
            out=sb["moe_fm"][:, fc, :], in_=disp_my.ap()[:, fc * 128:(fc + 1) * 128],
            transpose=True))(),
            reads=["disp_my"], writes=[f"mfm{fc}"], dma=f"mtk{fc}")
    def w1load(mi, slot):
        p.step("gpsimd", (lambda mi=mi, slot=slot: lambda g: g.dma_start(
            out=sb["f1e"][:, slot], in_=di["w1r"].ap()[mi]))(),
            writes=[f"f1e{slot}"], dma=f"lde{slot % 4}")
    def w2load(mi, slot):
        p.step("gpsimd", (lambda mi=mi, slot=slot: lambda g: g.dma_start(
            out=sb["f2col"][:, slot], in_=di["w2r"].ap()[mi]))(),
            writes=[f"f2c{slot}"], dma=f"ldf2_{slot}")
    p.fence([f"f1e{s}" for s in range(16)], HID_NAMES + ["w_p"])
    for mi in range(16):
        w1load(mi, mi)
    for m in range(NMF):
        slot = m % 16
        for n0, nw in ((0, 512), (512, 128)):
            bank = nxt()
            def mm(t, m=m, slot=slot, n0=n0, nw=nw, bank=bank):
                for kc in range(NFC):
                    last = t.matmul(ps[bank][:, 0:nw], sb["f1e"][:, slot, kc, :],
                                    sb["moe_fm"][:, kc, n0:n0 + nw],
                                    start=(kc == 0), stop=(kc == NFC - 1))
                return last
            p.step("tensor", mm, reads=[f"f1e{slot}"] + MFM_NAMES, writes=[f"ps{bank}"])
            p.step("scalar", (lambda m=m, bank=bank, nw=nw: lambda s: s.activation(
                sb["scr"][:, 0, 0:nw], ps[bank][:, 0:nw], AF.Square,
                bias=sb["c_b1r"][:, m:m + 1]))(),
                reads=[f"ps{bank}", "c_b1r"], writes=["scr0"])
            p.step("scalar", (lambda m=m, bank=bank, nw=nw: lambda s: s.activation(
                sb["scr"][:, 1, 0:nw], ps[bank][:, 0:nw], AF.Identity,
                bias=sb["c_b1r"][:, m:m + 1]))(),
                reads=[f"ps{bank}", "c_b1r"], writes=["scr1"])
            p.step("vector", (lambda nw=nw: lambda v: v.tensor_scalar(
                sb["scr"][:, 2, 0:nw], sb["scr"][:, 0, 0:nw], GC2, GC1, ALU.mult, ALU.add))(),
                reads=["scr0"], writes=["scr2"])
            p.step("vector", (lambda nw=nw: lambda v: v.tensor_tensor(
                sb["scr"][:, 0, 0:nw], sb["scr"][:, 2, 0:nw], sb["scr"][:, 1, 0:nw], ALU.mult))(),
                reads=["scr1", "scr2"], writes=["scr0"])
            p.step("scalar", (lambda nw=nw: lambda s: s.activation(
                sb["scr"][:, 2, 0:nw], sb["scr"][:, 0, 0:nw], AF.Tanh))(),
                reads=["scr0"], writes=["scr2"])
            p.step("vector", (lambda nw=nw: lambda v: v.tensor_scalar(
                sb["scr"][:, 3, 0:nw], sb["scr"][:, 2, 0:nw], 0.5, 0.5, ALU.mult, ALU.add))(),
                reads=["scr2"], writes=["scr3"])
            p.step("vector", (lambda m=m, n0=n0, nw=nw: lambda v: v.tensor_tensor(
                sb["he"][:, m, n0:n0 + nw], sb["scr"][:, 3, 0:nw], sb["scr"][:, 1, 0:nw],
                ALU.mult))(),
                reads=["scr1", "scr3"], writes=[f"he{m}"])
        if m + 16 < NMF:
            w1load(m + 16, slot)
    w2load(0, 0); w2load(1, 1)
    for m2 in range(NFC):
        slot = m2 % 2
        for n0, nw in ((0, 512), (512, 128)):
            bank = nxt()
            def mm2(t, m2=m2, slot=slot, n0=n0, nw=nw, bank=bank):
                for kc in range(NMF):
                    last = t.matmul(ps[bank][:, 0:nw], sb["f2col"][:, slot, kc, :],
                                    sb["he"][:, kc, n0:n0 + nw],
                                    start=(kc == 0), stop=(kc == NMF - 1))
                return last
            p.step("tensor", mm2, reads=[f"f2c{slot}"] + HE_NAMES, writes=[f"ps{bank}"])
            p.step("vector", (lambda m2=m2, n0=n0, nw=nw, bank=bank: lambda v: v.tensor_scalar(
                sb["eo_fm"][:, m2, n0:n0 + nw], ps[bank][:, 0:nw],
                sb["c_b2r"][:, m2:m2 + 1], None, ALU.add))(),
                reads=[f"ps{bank}", "c_b2r"], writes=[f"eo{m2}"])
        if m2 + 2 < NFC:
            w2load(m2 + 2, (m2 + 2) % 2)
    etb = [(psT[:, 0:128], "psT"), (ps[0][:, 0:128], "ps0"), (ps[1][:, 0:128], "ps1")]
    for s5 in range(5):
        slot = s5 % 2
        for fc in range(NFC):
            buf, bn = etb[(s5 * NFC + fc) % 3]
            p.step("tensor", (lambda s5=s5, fc=fc, buf=buf: lambda t: t.transpose(
                buf, sb["eo_fm"][:, fc, s5 * 128:(s5 + 1) * 128], sb["ident_f"][:]))(),
                reads=[f"eo{fc}", "ident_f"], writes=[bn])
            p.step("vector", (lambda fc=fc, slot=slot, buf=buf: lambda v: v.tensor_copy(
                sb["eo_tok"][:, slot, fc * 128:(fc + 1) * 128], buf))(),
                reads=[bn], writes=[f"eot{slot}"])
        p.step("sync", (lambda s5=s5, slot=slot: lambda e: e.dma_start(
            out=eo_my.ap().rearrange("(s p) d -> s p d", p=128)[s5], in_=sb["eo_tok"][:, slot]))(),
            reads=[f"eot{slot}"], writes=[], dma=f"eow{slot}")

    p.wait_tokens("gpsimd", [("eow0", p.dma_counts.get("eow0", 0)),
                             ("eow1", p.dma_counts.get("eow1", 0))])
    p.step("gpsimd", lambda g: g.collective_compute(
        "AllGather", ALU.bypass, replica_groups=RG,
        ins=[flatten_dims_for_collective(eo_my.ap())],
        outs=[flatten_dims_for_collective(eo_full.ap()[0:NSLOT, :])]),
        writes=["eo_full"], tag="ag_eo")

    if dbg:
        p.wait_tokens("sync", [("eow0", p.dma_counts.get("eow0", 0)),
                               ("eow1", p.dma_counts.get("eow1", 0))])
        d_eo = nc.dram_tensor("d_eomy", [CAP, D], F32, kind="ExternalOutput")
        p.step("sync", lambda e: e.dma_start(out=d_eo.ap(), in_=eo_my.ap()),
               reads=[], dma="dmp2")
        d_ef = nc.dram_tensor("d_eofull", [NSLOT, D], F32, kind="ExternalOutput")
        p.step("sync", lambda e: e.dma_start(out=d_ef.ap(), in_=eo_full.ap()[0:NSLOT, :]),
               reads=["eo_full"], dma="dmp2")
    p.fence(Y2_NAMES, [f"f1e{s}" for s in range(16)])
    p.fence([f"x1q{s}" for s in range(8)],
            A_NAMES + ZCQ_NAMES + B_NAMES + SCR_NAMES + XT_NAMES + ["w_b"]
            + [f"sc_b{s}" for s in range(6)])
    p.fence([f"egq{s}" for s in range(4)], ["w_a"] + MFM_NAMES + EOFM_NAMES)
    p.step("vector", lambda v: v.memset(sb["eogq"][:, 0], 0.0), writes=["egq0"])
    p.step("sync", lambda e: e.dma_start(out=eo_full.ap()[NSLOT:NSLOT + 1, :],
                                         in_=sb["eogq"][0:1, 0, :]),
           reads=["egq0"], writes=["eo_trash"], dma="etz")
    p.wait_tokens("gpsimd", [("etz", p.dma_counts["etz"])])
    outr = out_d.ap().rearrange("b (g p) d -> (b g) p d", p=128)
    for c in range(NCHUNK):
        for tg in range(NTG):
            g = c * NTG + tg
            slot = g % 8
            x1t = sb["x1q"][:, slot] if slot < 4 else sb["x1q2"][:, slot - 4]
            p.step("sync", (lambda g=g, x1t=x1t: lambda e: e.dma_start(
                out=x1t, in_=x1r[g]))(),
                reads=["x1sp"], writes=[f"x1q{slot}"], dma=f"x1r{slot % 4}")
            eslot = g % 4
            p.step("gpsimd", (lambda g=g, eslot=eslot: lambda e: e.indirect_dma_start(
                out=sb["eogq"][:, eslot], out_offset=None,
                in_=eo_full.ap(), in_offset=IndirectOffsetOnAxis(sb["slot_i"][:, g:g + 1], 0)))(),
                reads=["eo_full", "slot_i", f"egq{eslot}"], writes=[f"egq{eslot}"],
                dma=f"eg{eslot}")
            p.step("vector", (lambda g=g, tg=tg, eslot=eslot, x1t=x1t: lambda v: v.scalar_tensor_tensor(
                sb["y2"][:, tg, :], sb["eogq"][:, eslot, :], sb["gpk_all"][:, g:g + 1],
                x1t, ALU.mult, ALU.add))(),
                reads=[f"egq{eslot}", f"x1q{slot}", "gpk_all"], writes=[f"y2_{tg}"])
            p.step("vector", (lambda tg=tg: lambda v: v.tensor_reduce(
                sb["rt_negmu"][:, 0, tg:tg + 1], sb["y2"][:, tg, :], AX.X, ALU.add,
                negate=True))(),
                reads=[f"y2_{tg}"], writes=[f"ns2{tg}"])
        p.step("vector", lambda v: v.tensor_scalar(sb["rt_negmu"][:, 0, :], sb["rt_negmu"][:, 0, :],
                                                   1.0 / D, None, ALU.mult),
               reads=[f"ns2{tg}" for tg in range(NTG)], writes=["nm0"], tag="nm2_ip")
        for tg in range(NTG):
            bank = 4 if tg % 2 == 0 else 5
            p.step("scalar", (lambda tg=tg, bank=bank: lambda s: s.activation(
                ps[bank][:], sb["y2"][:, tg, 0:512], AF.Square,
                bias=sb["rt_negmu"][:, 0, tg:tg + 1], accum_out=sb["lnvar"][:, tg:tg + 1]))(),
                reads=[f"y2_{tg}", "nm0"], writes=[f"ps{bank}", f"lv2{tg}"])
            p.step("scalar", (lambda tg=tg, bank=bank: lambda s: s.activation(
                ps[bank][:], sb["y2"][:, tg, 512:1024], AF.Square,
                bias=sb["rt_negmu"][:, 0, tg:tg + 1], accum_out=sb["lnvar2"][:, tg:tg + 1]))(),
                reads=[f"y2_{tg}", "nm0"], writes=[f"ps{bank}", f"lw2{tg}"])
        p.step("vector", lambda v: v.tensor_add(sb["lnvar"][:], sb["lnvar"][:], sb["lnvar2"][:]),
               reads=[f"lv2{tg}" for tg in range(NTG)] + [f"lw2{tg}" for tg in range(NTG)],
               writes=["lnvar"], tag="lv2_ip")
        p.step("vector", lambda v: v.tensor_scalar(sb["lnvar"][:], sb["lnvar"][:],
                                                   1.0 / D, EPS, ALU.mult, ALU.add),
               reads=["lnvar"], writes=["lnvar"], tag="lv2_ip2")
        p.step("scalar", lambda s: s.activation(sb["rt_istd"][:, 0, :], sb["lnvar"][:], AF.Sqrt),
               reads=["lnvar"], writes=["st0"])
        p.step("vector", lambda v: v.reciprocal(sb["rt_istd"][:, 0, :], sb["rt_istd"][:, 0, :]),
               reads=["st0"], writes=["st0"], tag="istd2_ip")
        for tg in range(NTG):
            g = c * NTG + tg
            p.step("vector", (lambda tg=tg: lambda v: v.tensor_scalar(
                sb["y2"][:, tg, :], sb["y2"][:, tg, :],
                sb["rt_negmu"][:, 0, tg:tg + 1], sb["rt_istd"][:, 0, tg:tg + 1],
                ALU.add, ALU.mult))(),
                reads=[f"y2_{tg}", "nm0", "st0"], writes=[f"y2_{tg}"], tag="ln2_ap")
            p.step("sync", (lambda g=g, tg=tg: lambda e: e.dma_start(
                out=outr[g], in_=sb["y2"][:, tg, :]))(),
                reads=[f"y2_{tg}"], writes=[], dma=f"sto{tg % 2}")

    toks = [(f"sto{s}", p.dma_counts[f"sto{s}"]) for s in range(2)]
    if "dmp2" in p.dma_counts:
        toks.append(("dmp2", p.dma_counts["dmp2"]))
    p.wait_tokens("sync", toks)
    with nc.Block() as block:
        p.emit(block)
    return nc


def kernel(**inputs):
    per_core = host_prepare(inputs)
    nc = build_nc()
    res = run_bass_kernel_spmd(nc, per_core, core_ids=list(range(NCORES)))
    out = np.concatenate([np.asarray(r["out"]) for r in res.results], axis=0)
    return out.reshape(B, L, D)


# version 10
# speedup vs baseline: 1.0214x; 1.0059x over previous
import math
import numpy as np
import concourse.bass as bass
import concourse.mybir as mybir
from concourse.bass import IndirectOffsetOnAxis
from concourse.bass_utils import run_bass_kernel_spmd
from concourse.collective import flatten_dims_for_collective

F32 = mybir.dt.float32
BF16 = mybir.dt.bfloat16
I32 = mybir.dt.int32
AF = mybir.ActivationFunctionType
ALU = mybir.AluOpType
AX = mybir.AxisListType

ENGINES = ("tensor", "vector", "scalar", "gpsimd", "sync")


class Prog:

    def __init__(self, nc):
        self.nc = nc
        self.steps = {e: [] for e in ENGINES}
        self.sems = {e: nc.alloc_semaphore(f"prog_{e}") for e in ENGINES}
        self.counts = {e: 0 for e in ENGINES}
        self.dma_sems = {}
        self.dma_counts = {}
        self.watermark = {e: {} for e in ENGINES}
        self.writer = {}
        self.readers = {}
        self.n_waits = 0

    def dma_sem(self, name):
        if name not in self.dma_sems:
            self.dma_sems[name] = self.nc.alloc_semaphore(f"dma_{name}")
            self.dma_counts[name] = 0
        return self.dma_sems[name]

    def _sem_by_name(self, name):
        if name in self.dma_sems:
            return self.dma_sems[name]
        return self.sems[name]

    def _edge(self, eng, tok, waits):
        if tok is None:
            return
        sname, val = tok
        if val <= self.watermark[eng].get(sname, 0):
            return
        waits[sname] = max(waits.get(sname, 0), val)

    def step(self, eng, build, reads=(), writes=(), dma=None, n_inc=1, tag=""):
        waits = {}
        for t in reads:
            self._edge(eng, self.writer.get(t), waits)
        for t in writes:
            self._edge(eng, self.writer.get(t), waits)
            for tok in self.readers.get(t, ()):
                self._edge(eng, tok, waits)
        if dma is None:
            self.counts[eng] += 1
            tok = (eng, self.counts[eng])
            sem = self.sems[eng]
            inc_val = 1
        else:
            self.dma_sem(dma)
            self.dma_counts[dma] += 16 * n_inc
            tok = (dma, self.dma_counts[dma])
            sem = self.dma_sems[dma]
            inc_val = 16 * n_inc
        for t in writes:
            self.writer[t] = tok
            self.readers[t] = []
        for t in reads:
            self.readers.setdefault(t, []).append(tok)
        for sname, val in waits.items():
            self.watermark[eng][sname] = val
        self.n_waits += len(waits)
        self.steps[eng].append((dict(waits), build, sem, inc_val, tag))
        return tok

    def seal_dma(self, stream, tiles):
        tok = (stream, self.dma_counts[stream])
        for t in tiles:
            self.writer[t] = tok
            self.readers[t] = []

    def fence(self, new_names, old_names):
        toks = []
        for o in old_names:
            w = self.writer.pop(o, None)
            if w:
                toks.append(w)
            toks.extend(self.readers.pop(o, []))
        for nn in new_names:
            self.readers.setdefault(nn, []).extend(toks)

    def wait_tokens(self, eng, toks):
        waits = {}
        for tok in toks:
            self._edge(eng, tok, waits)
        for sname, val in waits.items():
            self.watermark[eng][sname] = val
        if waits:
            self.steps[eng].append((waits, None, None, 0, "wait"))
        self.n_waits += len(waits)

    def emit(self, block):
        def mk(eng_name):
            def run(eng):
                for waits, build, sem, inc_val, tag in self.steps[eng_name]:
                    for sname, val in waits.items():
                        eng.wait_ge(self._sem_by_name(sname), val)
                    if build is not None:
                        last = build(eng)
                        last.then_inc(sem, inc_val)
            return run
        block.tensor(mk("tensor"))
        block.vector(mk("vector"))
        block.scalar(mk("scalar"))
        block.gpsimd(mk("gpsimd"))
        block.sync(mk("sync"))


B, L, D = 16, 4096, 1024
NCORES = 8
BPC = B // NCORES
TPC = BPC * L
T = 1024
NCHUNK = TPC // T
NFC = D // 128
NTG = T // 128
NG = NCHUNK * NTG
H, DH = 16, 64
E, FF, CAP = 8, 4096, 640
NMF = FF // 128
NSLOT = E * CAP
EPS = 1e-5
BIG = 1.0e6
GC1, GC2 = 0.7978845608028654, 0.044715 * 0.7978845608028654
RG = [list(range(NCORES))]
KB = 1024

A_NAMES = [f"aw{s}{n}" for s in range(2) for n in range(2)]
B_NAMES = [f"bw{s}{n}" for s in range(2) for n in range(2)]
ZCQ_NAMES = ["zc0", "zc1", "zq0", "zq1"]
XFM_NAMES = [f"x_fm{q}" for q in range(4)]
SCR_NAMES = [f"scr{i}" for i in range(4)]
XT_NAMES = ["xt0", "xt1"]
SLAB_NAMES = [f"slab{i}_{n}" for i in range(8) for n in range(2)]
B16_NAMES = [f"b16_{i}_{n}" for i in range(8) for n in range(2)]
HID_NAMES = [f"hid{m}" for m in range(32)]
Y2_NAMES = [f"y2_{t}" for t in range(8)]
HE_NAMES = [f"he{m}" for m in range(32)]
MFM_NAMES = [f"mfm{fc}" for fc in range(8)]
EOFM_NAMES = [f"eo{m}" for m in range(8)]


def _tile4(w):
    O, I = w.shape
    mc, kc = O // 128, I // 128
    t = w.reshape(mc, 128, kc, 128)
    return np.ascontiguousarray(t.transpose(0, 3, 2, 1))


def _rhs_tiles(w):
    I, N = w.shape
    kc = I // 128
    t = w.reshape(kc, 128, N)
    return np.ascontiguousarray(t.transpose(1, 0, 2))


def _cols(v):
    return np.ascontiguousarray(v.reshape(-1, 128).T)


def host_prepare(inputs):
    emb = np.asarray(inputs["L0_attn_emb"], dtype=np.float32)
    att = {k[8:]: np.asarray(v, dtype=np.float32)
           for k, v in inputs.items() if k.startswith("L0_attn_")}
    ffn = {k[7:]: np.asarray(v, dtype=np.float32)
           for k, v in inputs.items() if k.startswith("L0_ffn_")}
    shared = {
        "waT": _tile4(att["Wa_w"]), "wbT": _tile4(att["Wb_w"]),
        "woT": _tile4(att["Wo_w"]), "cT": _tile4(att["C_w"]),
        "f1T": _tile4(att["f1_w"]), "f2T": _tile4(att["f2_w"]),
        "pT": _rhs_tiles(np.ascontiguousarray(att["p_w"].T)),
        "praw": _tile4(np.ascontiguousarray(att["p_w"].T)),
        "gw": None,
        "gb": None,
        "ab": _cols(att["Wa_b"]), "bb": _cols(att["Wb_b"]),
        "cb": _cols(att["C_b"]), "wob": _cols(att["Wo_b"]),
        "lng": _cols(att["ln_g"]), "lnb": _cols(att["ln_b"]),
        "f1b": _cols(att["f1_b"]), "f2b": _cols(att["f2_b"]),
        "pb": att["p_b"].reshape(1, D), "pbc": _cols(att["p_b"]),
        "svec": _cols(np.repeat(att["head_weights"], DH)),
        "ident": np.eye(128, dtype=np.float32),
    }
    shared["gw"] = _rhs_tiles(ffn["gate_w"])
    shared["gb"] = ffn["gate_b"].reshape(1, E)
    per_core = []
    for r in range(NCORES):
        sh = emb[2 * r:2 * r + 2]
        d = dict(shared)
        d["x_tok"] = np.ascontiguousarray(sh)
        d["x_fm"] = np.ascontiguousarray(sh.transpose(0, 2, 1))
        d["w1r"] = _tile4(np.ascontiguousarray(ffn["w1"][r].T))
        d["w2r"] = _tile4(np.ascontiguousarray(ffn["w2"][r].T))
        d["b1r"] = _cols(ffn["b1"][r])
        d["b2r"] = _cols(ffn["b2"][r])
        d["trimask"] = (np.arange(NCORES) < r).astype(np.float32).reshape(NCORES, 1)
        per_core.append(d)
    return per_core


def build_nc(n_chunks=NCHUNK, dbg=False, dumps=()):
    nc = bass.Bass("TRN2", target_bir_lowering=False, debug=False,
                   num_devices=NCORES)

    di = {}
    def dram_in(name, shape):
        di[name] = nc.dram_tensor(name, list(shape), F32, kind="ExternalInput")
    dram_in("x_tok", (BPC, L, D)); dram_in("x_fm", (BPC, D, L))
    for n in ("waT", "wbT", "woT", "cT", "praw"):
        dram_in(n, (NFC, 128, NFC, 128))
    dram_in("f1T", (NMF, 128, NFC, 128)); dram_in("f2T", (NFC, 128, NMF, 128))
    dram_in("w1r", (NMF, 128, NFC, 128)); dram_in("w2r", (NFC, 128, NMF, 128))
    dram_in("pT", (128, NFC, D)); dram_in("gw", (128, NFC, E)); dram_in("gb", (1, E))
    for n in ("ab", "bb", "cb", "wob", "lng", "lnb", "f2b", "svec", "pbc", "b2r"):
        dram_in(n, (128, NFC))
    dram_in("f1b", (128, NMF)); dram_in("b1r", (128, NMF))
    dram_in("pb", (1, D)); dram_in("ident", (128, 128))
    dram_in("trimask", (NCORES, 1))

    out_d = nc.dram_tensor("out", [BPC, L, D], F32, kind="ExternalOutput")
    x1sp = nc.dram_tensor("x1sp", [TPC, D], F32,
                          kind="ExternalOutput" if dbg else "Internal")
    x1spb = nc.dram_tensor("x1spb", [TPC, D], BF16, kind="Internal")
    disp = nc.dram_tensor("disp", [NSLOT + 1, D], BF16, kind="Internal")
    disp_my = nc.dram_tensor("disp_my", [CAP, D], BF16, kind="Internal")
    cnt_in = nc.dram_tensor("cnt_in", [1, E], F32, kind="Internal")
    cnt_out = nc.dram_tensor("cnt_out", [NCORES, E], F32, kind="Internal")
    eo_my = nc.dram_tensor("eo_my", [CAP, D], F32, kind="Internal")
    wcache = {n: nc.dram_tensor("wc_" + n, [128, NFC, D], BF16, kind="Internal")
              for n in ("waT", "wbT", "cT", "woT", "pT")}
    f1c_d = nc.dram_tensor("wc_f1", [NMF, 128, NFC, 128], BF16, kind="Internal")
    f2c_d = nc.dram_tensor("wc_f2", [NFC, 128, NMF, 128], BF16, kind="Internal")
    eo_full = nc.dram_tensor("eo_full", [NSLOT + 1, D], F32, kind="Internal")

    sb = {}
    def S(name, shape, dt=F32):
        sb[name] = nc.alloc_sbuf_tensor("s_" + name, list(shape), dt)
    for n in ("ab", "bb", "cb", "wob", "cwob", "lng", "lnb", "f2b", "svec",
              "b2r", "pbc"):
        S("c_" + n, (128, NFC))
    S("c_f1b", (128, NMF)); S("c_b1r", (128, NMF))
    S("c_f1be", (128, NMF)); S("c_f1bh", (128, NMF))
    S("pb_row", (1, D)); S("gb_row", (1, E))
    S("ones_row", (1, 128)); S("ones_col", (128, 1)); S("ones8", (128, E))
    S("ones_col_b", (128, 1), BF16)
    S("ident_f", (128, 128)); S("ident_b", (128, 128), BF16)
    S("gw_b", (128, NFC, E), BF16); S("G_b", (128, NFC, E), BF16)
    S("gw_f", (128, NFC, E))
    S("pbgw_row", (1, E)); S("gwsum_bc", (128, E)); S("gb_bc", (128, E))
    S("iota8", (128, E)); S("oadj_bc", (128, E))
    S("carry_h", (128, NFC)); S("carry_c", (E, 1))
    S("trimask", (E, 1)); S("cnts", (E, E)); S("orow", (1, E))
    S("ones_row8", (E, T))
    S("pos_all", (128, NG)); S("eid_all", (128, NG)); S("o_all", (128, NG))
    S("gp_all", (128, NG)); S("gpk_all", (128, NG))
    S("slot_f", (128, NG)); S("slot_i", (128, NG), I32)
    S("fm_all", (128, NG, E))
    S("rt_negmu", (128, 2, NTG)); S("lnvar", (128, NTG)); S("lnvar2", (128, NTG))
    S("rt_istd", (128, 2, NTG))
    S("rowA", (1, T)); S("rowB", (1, T))
    S("emaj", (E, T)); S("emaj_cum", (E, T))
    S("logit_all", (128, 2, NTG, E)); S("lmax", (128, 2, NTG)); S("nlmax", (128, 2, NTG))
    S("lsum", (128, 2, NTG))
    S("cum_tok", (128, NTG, E))
    S("scr8a", (128, 2, NTG, E)); S("scr8b", (128, 2, NTG, E)); S("scr8c", (128, 2, NTG, E))

    ARENA_KB = 172
    rng_ = nc.bump_sbuf(ARENA_KB * KB)
    assert rng_ is not None, "SBUF arena allocation failed"
    base = rng_[0]
    def SA(name, shape, dt, off_kb):
        sb[name] = nc.alloc_sbuf_tensor_at("s_" + name, list(shape), dt,
                                           offset=base + off_kb * KB)
    SA("x_fm", (128, NFC, T), BF16, 0)
    SA("w_a", (128, NFC, D), BF16, 16)
    SA("w_b", (128, NFC, D), BF16, 32)
    SA("moe_tok", (128, 5, D), BF16, 0)
    SA("moe_fm", (128, NFC, CAP), BF16, 10)
    SA("eo_fm", (128, NFC, CAP), F32, 20)
    SA("eo_tok", (128, 2, D), F32, 40)
    SA("f1col", (128, 4, NFC, 128), BF16, 48)
    SA("f2col", (128, 2, NMF, 128), BF16, 56)
    SA("slab", (128, NFC, T), F32, 72)
    SA("b16", (128, NFC, T), BF16, 104)
    SA("he", (128, NMF, CAP), BF16, 72)
    SA("y2", (128, NTG, D), F32, 120)
    SA("hid", (128, NMF, 512), BF16, 120)
    SA("a_sl", (128, 2, T), F32, 152)
    SA("zc_sl", (128, 2, T), BF16, 152)
    SA("zq_sl", (128, 2, T), BF16, 156)
    SA("x1_sl", (128, 2, D), F32, 152)
    SA("b_sl", (128, 2, T), BF16, 160)
    SA("sc_b", (128, 2, D), BF16, 160)
    SA("w_p", (128, NFC, D), BF16, 120)
    SA("sc_bB", (128, 2, T), BF16, 152)
    SA("sc_bC", (128, 2, T), BF16, 156)
    SA("x1q2", (128, 4, T), F32, 32)
    SA("f1e", (128, 16, NFC, 128), BF16, 120)
    SA("x1q", (128, 4, T), F32, 152)
    SA("eogq", (128, 4, T), F32, 16)
    SA("scr", (128, 4, 512), F32, 164)
    SA("xt_sl", (128, 2, D), F32, 164)
    SA("eog", (128, 2, D), F32, 164)

    ps = [nc.alloc_psum_tensor(f"ps{i}", [128, 512], F32) for i in range(6)]
    psB = nc.alloc_psum_tensor("psB", [128, 128], BF16)
    psT = nc.alloc_psum_tensor("psT", [128, 128], F32)

    p = Prog(nc)
    dump_d = {}
    def dump(stage, tile, names, cast=False, c=0):
        if stage not in dumps or c != 0:
            return
        shp = [int(x) for x in sb[tile].shape]
        dump_d[stage] = nc.dram_tensor("dump_" + stage, shp, F32, kind="ExternalOutput")
        eng = "gpsimd" if cast else "sync"
        p.step(eng, (lambda stage=stage, tile=tile: lambda e: e.dma_start(
            out=dump_d[stage].ap(), in_=sb[tile][:]))(),
            reads=list(names), dma="dmp")

    small_loads = [("c_ab", "ab"), ("c_bb", "bb"), ("c_cb", "cb"), ("c_wob", "wob"),
                   ("c_lng", "lng"), ("c_lnb", "lnb"), ("c_f2b", "f2b"),
                   ("c_svec", "svec"), ("c_b2r", "b2r"), ("c_pbc", "pbc"),
                   ("c_f1b", "f1b"), ("c_b1r", "b1r"),
                   ("pb_row", "pb"), ("gb_row", "gb"), ("ident_f", "ident"),
                   ("trimask", "trimask")]
    for tn, dn in small_loads:
        p.step("sync", (lambda tn=tn, dn=dn: lambda e: e.dma_start(out=sb[tn][:], in_=di[dn].ap()))(),
               writes=[tn], dma="lds")
    p.step("gpsimd", lambda e: e.dma_start(out=sb["ident_b"][:], in_=di["ident"].ap()),
           writes=["ident_b"], dma="ldg")
    p.step("gpsimd", lambda e: e.dma_start(out=sb["gw_b"][:], in_=di["gw"].ap()),
           writes=["gw_b"], dma="ldg")
    p.step("sync", lambda e: e.dma_start(out=sb["gw_f"][:], in_=di["gw"].ap()),
           writes=["gw_f"], dma="lds")
    p.seal_dma("lds", [tn for tn, _ in small_loads] + ["gw_f"])
    p.seal_dma("ldg", ["ident_b", "gw_b"])

    p.step("vector", lambda v: v.tensor_add(sb["c_cwob"][:], sb["c_cb"][:], sb["c_wob"][:]),
           reads=["c_cb", "c_wob"], writes=["c_cwob"])
    p.step("vector", lambda v: v.tensor_scalar_mul(sb["c_f1be"][:], sb["c_f1b"][:], 1.0 / math.sqrt(2.0)),
           reads=["c_f1b"], writes=["c_f1be"])
    p.step("vector", lambda v: v.tensor_scalar_mul(sb["c_f1bh"][:], sb["c_f1b"][:], 0.5),
           reads=["c_f1b"], writes=["c_f1bh"])
    for tn, val in [("ones_row", 1.0), ("ones_col", 1.0), ("ones_col_b", 1.0), ("ones8", 1.0),
                    ("ones_row8", 1.0), ("carry_h", 0.0), ("carry_c", 0.0)]:
        p.step("vector", (lambda tn=tn, val=val: lambda v: v.memset(sb[tn][:], val))(), writes=[tn])
    p.step("gpsimd", lambda g: g.iota(sb["iota8"][:], pattern=[[1, E]], base=0,
                                      channel_multiplier=0, allow_small_or_imprecise_dtypes=True),
           writes=["iota8"])

    p.step("vector", lambda v: v.memset(sb["sc_b"][:, 0], 0.0), writes=["sc_b0"])
    dzap = disp.ap()[0:NSLOT, :].rearrange("(n p) d -> n p d", p=128)
    for i in range(NSLOT // 128):
        p.step("sync", (lambda i=i: lambda e: e.dma_start(out=dzap[i], in_=sb["sc_b"][:, 0]))(),
               reads=["sc_b0"], writes=[], dma="dz")
    p.seal_dma("dz", ["disp"])
    disp_zeroed = ("dz", p.dma_counts["dz"])

    p.step("gpsimd", lambda g: g.dma_start(
        out=sb["w_a"][:], in_=di["praw"].ap().rearrange("a p b c -> p a (b c)")),
        writes=["w_a"], dma="ldwa")
    for jc in range(NFC):
        def gmm(t, jc=jc):
            for oc in range(NFC):
                last = t.matmul(psT[:, 0:E], sb["w_a"][:, jc, oc * 128:(oc + 1) * 128],
                                sb["gw_b"][:, oc, :], start=(oc == 0), stop=(oc == NFC - 1))
            return last
        p.step("tensor", gmm, reads=["w_a", "gw_b"], writes=["psT"])
        p.step("vector", (lambda jc=jc: lambda v: v.tensor_copy(sb["G_b"][:, jc, :], psT[:, 0:E]))(),
               reads=["psT"], writes=["G_b"])
    def pbgw_mm(t):
        for kc in range(NFC):
            last = t.matmul(psT[0:1, 0:E], sb["c_pbc"][:, kc:kc + 1], sb["gw_f"][:, kc, :],
                            start=(kc == 0), stop=(kc == NFC - 1))
        return last
    p.step("tensor", pbgw_mm, reads=["c_pbc", "gw_f"], writes=["psT"])
    p.step("vector", lambda v: v.tensor_copy(sb["pbgw_row"][:], psT[0:1, 0:E]),
           reads=["psT"], writes=["pbgw_row"])
    def gwsum_mm(t):
        for kc in range(NFC):
            last = t.matmul(psT[0:1, 0:E], sb["ones_col"][:], sb["gw_f"][:, kc, :],
                            start=(kc == 0), stop=(kc == NFC - 1))
        return last
    p.step("tensor", gwsum_mm, reads=["ones_col", "gw_f"], writes=["psT"])
    p.step("vector", lambda v: v.tensor_copy(sb["rowA"][0:1, 0:E], psT[0:1, 0:E]),
           reads=["psT"], writes=["rowA"])
    def bc_mm(t):
        t.matmul(psT[:, 0:E], sb["ones_row"][:], sb["rowA"][0:1, 0:E], start=True, stop=True)
        return t.matmul(psT[:, E:2 * E], sb["ones_row"][:], sb["gb_row"][:], start=True, stop=True)
    p.step("tensor", bc_mm, reads=["ones_row", "rowA", "gb_row"], writes=["psT"])
    p.step("vector", lambda v: v.tensor_copy(sb["gwsum_bc"][:], psT[:, 0:E]),
           reads=["psT"], writes=["gwsum_bc"])
    p.step("vector", lambda v: v.tensor_copy(sb["gb_bc"][:], psT[:, E:2 * E]),
           reads=["psT"], writes=["gb_bc"])

    def load_w(wn, region, c):
        stream = "ldwa" if region == "w_a" else "ldwb"
        if c == 0:
            if wn == "pT":
                p.step("gpsimd", lambda g: g.dma_start(out=sb[region][:], in_=di["pT"].ap()),
                       writes=[region], dma=stream)
            else:
                p.step("gpsimd", (lambda wn=wn, region=region: lambda g: g.dma_start(
                    out=sb[region][:], in_=di[wn].ap().rearrange("a p b c -> p a (b c)")))(),
                    writes=[region], dma=stream)
            p.step("sync", (lambda wn=wn, region=region: lambda e: e.dma_start(
                out=wcache[wn].ap(), in_=sb[region][:]))(),
                reads=[region], writes=[f"wc_{wn}"], dma="wcw")
        else:
            p.step("sync", (lambda wn=wn, region=region: lambda e: e.dma_start(
                out=sb[region][:], in_=wcache[wn].ap()))(),
                reads=[f"wc_{wn}"], writes=[region], dma=stream)

    rot = [0]
    def nxt():
        rot[0] = (rot[0] + 1) % 4
        return rot[0]

    def plan_routing_dve(c):
        par = c % 2
        LA = sb["logit_all"]
        for tg in range(NTG):
            p.step("vector", (lambda tg=tg, par=par: lambda v: v.tensor_scalar(
                LA[:, par, tg, :], LA[:, par, tg, :],
                sb["rt_istd"][:, par, tg:tg + 1], None, ALU.mult))(),
                reads=[f"pre{par}_{tg}", f"st{par}"], writes=[f"pre{par}_{tg}"], tag="lg_ip")
            p.step("vector", (lambda tg=tg, par=par: lambda v: v.tensor_scalar(
                sb["scr8a"][:, par, tg, :], sb["gwsum_bc"][:],
                sb["rt_negmu"][:, par, tg:tg + 1], sb["rt_istd"][:, par, tg:tg + 1],
                ALU.mult, ALU.mult))(),
                reads=["gwsum_bc", f"nm{par}", f"st{par}"], writes=[f"s8a{par}_{tg}"])
            p.step("vector", (lambda tg=tg, par=par: lambda v: v.tensor_tensor(
                LA[:, par, tg, :], LA[:, par, tg, :], sb["scr8a"][:, par, tg, :],
                ALU.add))(),
                reads=[f"pre{par}_{tg}", f"s8a{par}_{tg}"], writes=[f"pre{par}_{tg}"], tag="lg2_ip")
            p.step("vector", (lambda tg=tg, par=par: lambda v: v.tensor_tensor(
                LA[:, par, tg, :], LA[:, par, tg, :], sb["gb_bc"][:],
                ALU.add))(),
                reads=[f"pre{par}_{tg}", "gb_bc"], writes=[f"pre{par}_{tg}"], tag="lg3_ip")
        p.step("vector", (lambda par=par: lambda v: v.tensor_reduce(
            sb["lmax"][:, par, :], LA[:, par], AX.X, ALU.max))(),
            reads=[f"pre{par}_{tg}" for tg in range(NTG)], writes=[f"lmax{par}"])
        p.step("vector", (lambda par=par: lambda v: v.tensor_scalar_mul(
            sb["nlmax"][:, par, :], sb["lmax"][:, par, :], -0.5))(),
            reads=[f"lmax{par}"], writes=[f"nlmax{par}"])
        for tg in range(NTG):
            gidx = c * NTG + tg
            p.step("vector", (lambda tg=tg, par=par: lambda v: v.tensor_scalar(
                sb["scr8a"][:, par, tg, :], LA[:, par, tg, :],
                sb["lmax"][:, par, tg:tg + 1], None, ALU.is_equal))(),
                reads=[f"pre{par}_{tg}", f"lmax{par}"], writes=[f"s8a{par}_{tg}"])
            p.step("vector", (lambda tg=tg, par=par: lambda v: v.tensor_tensor_scan(
                sb["scr8b"][:, par, tg, :], sb["ones8"][:], sb["scr8a"][:, par, tg, :],
                0.0, ALU.mult, ALU.add))(),
                reads=[f"s8a{par}_{tg}", "ones8"], writes=[f"s8b{par}_{tg}"])
            p.step("vector", (lambda tg=tg, par=par: lambda v: v.tensor_scalar(
                sb["scr8c"][:, par, tg, :], sb["scr8b"][:, par, tg, :], 1.0, None, ALU.is_equal))(),
                reads=[f"s8b{par}_{tg}"], writes=[f"s8c{par}_{tg}"])
            p.step("vector", (lambda tg=tg, par=par, gidx=gidx: lambda v: v.tensor_tensor(
                sb["fm_all"][:, gidx, :], sb["scr8a"][:, par, tg, :], sb["scr8c"][:, par, tg, :],
                ALU.mult))(),
                reads=[f"s8a{par}_{tg}", f"s8c{par}_{tg}"], writes=[f"fm{gidx}"])
            p.step("vector", (lambda tg=tg, par=par, gidx=gidx: lambda v: v.tensor_tensor(
                sb["scr8b"][:, par, tg, :], sb["fm_all"][:, gidx, :], sb["iota8"][:], ALU.mult))(),
                reads=[f"fm{gidx}", "iota8"], writes=[f"s8b{par}_{tg}"])
            p.step("vector", (lambda tg=tg, par=par, gidx=gidx: lambda v: v.tensor_reduce(
                sb["eid_all"][:, gidx:gidx + 1], sb["scr8b"][:, par, tg, :], AX.X, ALU.add))(),
                reads=[f"s8b{par}_{tg}"], writes=[f"eid{gidx}"])
    def plan_routing_pe(c):
        par = c % 2
        LA = sb["logit_all"]
        tbufs = [(psT[0:E, 0:128], "psT"), (ps[4][0:E, 0:128], "ps4"),
                 (ps[5][0:E, 0:128], "ps5")]
        tbufs2 = [(psT[:, 0:E], "psT"), (ps[4][:, 0:E], "ps4"), (ps[5][:, 0:E], "ps5")]
        for tg in range(NTG):
            gidx = c * NTG + tg
            buf, bn = tbufs[tg % 3]
            p.step("tensor", (lambda gidx=gidx, buf=buf: lambda t: t.transpose(
                buf, sb["fm_all"][:, gidx, :], sb["ident_f"][:]))(),
                reads=[f"fm{gidx}", "ident_f"], writes=[bn])
            p.step("vector", (lambda tg=tg, buf=buf: lambda v: v.tensor_copy(
                sb["emaj"][:, tg * 128:(tg + 1) * 128], buf))(),
                reads=[bn], writes=[f"em{tg}"])
        p.step("vector", lambda v: v.tensor_tensor_scan(
            sb["emaj_cum"][:], sb["ones_row8"][:], sb["emaj"][:], sb["carry_c"][:],
            ALU.mult, ALU.add),
            reads=[f"em{tg}" for tg in range(NTG)] + ["ones_row8", "carry_c"],
            writes=["emaj_cum"])
        p.step("vector", lambda v: v.tensor_copy(sb["carry_c"][:], sb["emaj_cum"][:, T - 1:T]),
               reads=["emaj_cum"], writes=["carry_c"])
        for tg in range(NTG):
            gidx = c * NTG + tg
            buf, bn = tbufs2[tg % 3]
            p.step("tensor", (lambda tg=tg, buf=buf: lambda t: t.transpose(
                buf, sb["emaj_cum"][:, tg * 128:(tg + 1) * 128],
                sb["ident_f"][0:E, 0:E]))(),
                reads=["emaj_cum", "ident_f"], writes=[bn])
            p.step("vector", (lambda tg=tg, buf=buf: lambda v: v.tensor_copy(
                sb["cum_tok"][:, tg, :], buf))(),
                reads=[bn], writes=[f"ct{tg}"])
            p.step("vector", (lambda tg=tg, par=par, gidx=gidx: lambda v: v.tensor_tensor(
                sb["scr8b"][:, par, tg, :], sb["fm_all"][:, gidx, :], sb["cum_tok"][:, tg, :],
                ALU.mult))(),
                reads=[f"fm{gidx}", f"ct{tg}"], writes=[f"s8b{par}_{tg}"])
            p.step("vector", (lambda tg=tg, par=par, gidx=gidx: lambda v: v.tensor_reduce(
                sb["pos_all"][:, gidx:gidx + 1], sb["scr8b"][:, par, tg, :], AX.X, ALU.add))(),
                reads=[f"s8b{par}_{tg}"], writes=[f"posr{gidx}"])
        for tg in range(NTG):
            p.step("scalar", (lambda tg=tg, par=par: lambda s: s.activation(
                sb["scr8b"][:, par, tg, :], LA[:, par, tg, :], AF.Tanh,
                bias=sb["nlmax"][:, par, tg:tg + 1], scale=0.5))(),
                reads=[f"pre{par}_{tg}", f"nlmax{par}"], writes=[f"s8b{par}_{tg}"])
            p.step("vector", (lambda tg=tg, par=par: lambda v: v.tensor_scalar(
                sb["scr8c"][:, par, tg, :], sb["scr8b"][:, par, tg, :], -1.0, 1.0,
                ALU.mult, ALU.add))(),
                reads=[f"s8b{par}_{tg}"], writes=[f"s8c{par}_{tg}"])
            p.step("vector", (lambda tg=tg, par=par: lambda v: v.reciprocal(
                sb["scr8c"][:, par, tg, :], sb["scr8c"][:, par, tg, :]))(),
                reads=[f"s8c{par}_{tg}"], writes=[f"s8c{par}_{tg}"], tag="rcp_ip")
            p.step("vector", (lambda tg=tg, par=par: lambda v: v.tensor_scalar(
                sb["scr8b"][:, par, tg, :], sb["scr8b"][:, par, tg, :], 1.0, 1.0,
                ALU.mult, ALU.add))(),
                reads=[f"s8b{par}_{tg}"], writes=[f"s8b{par}_{tg}"], tag="onep_ip")
            p.step("vector", (lambda tg=tg, par=par: lambda v: v.tensor_tensor(
                sb["scr8b"][:, par, tg, :], sb["scr8b"][:, par, tg, :],
                sb["scr8c"][:, par, tg, :], ALU.mult))(),
                reads=[f"s8b{par}_{tg}", f"s8c{par}_{tg}"], writes=[f"s8b{par}_{tg}"], tag="expv_ip")
            p.step("vector", (lambda tg=tg, par=par: lambda v: v.tensor_reduce(
                sb["lsum"][:, par, tg:tg + 1], sb["scr8b"][:, par, tg, :], AX.X, ALU.add))(),
                reads=[f"s8b{par}_{tg}"], writes=[f"ls{par}_{tg}"])
        p.step("vector", (lambda c=c, par=par: lambda v: v.reciprocal(
            sb["gp_all"][:, c * NTG:(c + 1) * NTG], sb["lsum"][:, par, :]))(),
            reads=[f"ls{par}_{tg}" for tg in range(NTG)], writes=["gp_all"])

    for c in range(n_chunks):
        par = c % 2
        bat, l0 = divmod(c * T, L)
        if c > 0:
            plan_routing_dve(c - 1)
        if l0 == 0 and c > 0:
            p.step("vector", lambda v: v.memset(sb["carry_h"][:], 0.0), writes=["carry_h"])
        for q in range(4):
            p.step("gpsimd", (lambda bat=bat, l0=l0, q=q: lambda g: g.dma_start(
                out=sb["x_fm"][:, 2 * q:2 * q + 2, :],
                in_=di["x_fm"].ap()[bat].rearrange("(fc p) l -> p fc l", p=128)
                [:, 2 * q:2 * q + 2, l0:l0 + T]))(),
                writes=[f"x_fm{q}"], dma=f"ldx{q}")
        load_w("waT", "w_a", c); load_w("wbT", "w_b", c)

        p.fence(A_NAMES, ZCQ_NAMES)
        if c == 0:
            p.fence(B_NAMES, ["sc_b0", "sc_b1"])
        for m in range(NFC):
            sl = m % 2
            for which in ("a", "b"):
                wt = "w_a" if which == "a" else "w_b"
                for n in range(2):
                    bank = nxt()
                    def mm(t, m=m, n=n, wt=wt, bank=bank):
                        for kc in range(NFC):
                            last = t.matmul(ps[bank][:], sb[wt][:, m, kc * 128:(kc + 1) * 128],
                                            sb["x_fm"][:, kc, n * 512:(n + 1) * 512],
                                            start=(kc == 0), stop=(kc == NFC - 1))
                        return last
                    p.step("tensor", mm, reads=[wt] + XFM_NAMES, writes=[f"ps{bank}"])
                    if which == "a":
                        p.step("scalar", (lambda n=n, m=m, sl=sl, bank=bank: lambda s: s.activation(
                            sb["a_sl"][:, sl, n * 512:(n + 1) * 512], ps[bank][:], AF.Tanh,
                            bias=sb["c_ab"][:, m:m + 1]))(),
                            reads=[f"ps{bank}", "c_ab"], writes=[f"aw{sl}{n}"])
                    else:
                        p.step("scalar", (lambda n=n, m=m, sl=sl, bank=bank: lambda s: s.activation(
                            sb["b_sl"][:, sl, n * 512:(n + 1) * 512], ps[bank][:], AF.Identity,
                            bias=sb["c_bb"][:, m:m + 1]))(),
                            reads=[f"ps{bank}", "c_bb"], writes=[f"bw{sl}{n}"])
            p.step("vector", (lambda m=m, sl=sl: lambda v: v.tensor_tensor_scan(
                sb["b16"][:, m, :], sb["a_sl"][:, sl, :], sb["b_sl"][:, sl, :],
                sb["carry_h"][:, m:m + 1], ALU.mult, ALU.add))(),
                reads=[f"aw{sl}0", f"aw{sl}1", f"bw{sl}0", f"bw{sl}1", "carry_h"],
                writes=[f"b16_{m}_0", f"b16_{m}_1"], tag="scan")
            p.step("vector", (lambda m=m: lambda v: v.tensor_copy(
                sb["carry_h"][:, m:m + 1], sb["b16"][:, m, T - 1:T]))(),
                reads=[f"b16_{m}_1"], writes=["carry_h"])

        dump("h", "b16", B16_NAMES, cast=True, c=c)
        dump("xfm", "x_fm", XFM_NAMES, cast=True, c=c)
        if c > 0:
            plan_routing_pe(c - 1)
        load_w("cT", "w_a", c); load_w("woT", "w_b", c)

        p.fence(ZCQ_NAMES, A_NAMES)
        if c > 0:
            p.fence(SCR_NAMES, XT_NAMES)
        hreads = [f"b16_{k}_{n}" for k in range(NFC) for n in range(2)]
        smm_q = []
        for m in range(NFC):
            sl = m % 2
            for n in range(2):
                bank = (m * 2 + n) % 2
                def mm(t, m=m, n=n, bank=bank):
                    for kc in range(NFC):
                        t.matmul(ps[bank][:], sb["w_a"][:, m, kc * 128:(kc + 1) * 128],
                                 sb["b16"][:, kc, n * 512:(n + 1) * 512],
                                 start=(kc == 0), stop=False)
                    for kc in range(NFC):
                        last = t.matmul(ps[bank][:], sb["w_b"][:, m, kc * 128:(kc + 1) * 128],
                                        sb["x_fm"][:, kc, n * 512:(n + 1) * 512],
                                        start=False, stop=(kc == NFC - 1))
                    return last
                p.step("tensor", mm, reads=["w_a", "w_b"] + XFM_NAMES
                       + [f"b16_{k}_{n}" for k in range(NFC)],
                       writes=[f"ps{bank}"])
                p.step("vector", (lambda m=m, n=n, bank=bank: lambda v: v.tensor_scalar(
                    sb["slab"][:, m, n * 512:(n + 1) * 512], ps[bank][:],
                    sb["c_cwob"][:, m:m + 1], sb["c_svec"][:, m:m + 1], ALU.add, ALU.mult))(),
                    reads=[f"ps{bank}", "c_cwob", "c_svec"], writes=[f"slab{m}_{n}"])
            p.step("scalar", (lambda m=m, sl=sl: lambda s: s.activation(
                sb["zc_sl"][:, sl, :], sb["slab"][:, m, :], AF.Identity))(),
                reads=[f"slab{m}_0", f"slab{m}_1"], writes=[f"zc{sl}"])
            p.step("scalar", (lambda m=m, sl=sl: lambda s: s.activation(
                sb["zq_sl"][:, sl, :], sb["slab"][:, m, :], AF.Square))(),
                reads=[f"slab{m}_0", f"slab{m}_1"], writes=[f"zq{sl}"])
            def smm(t, m=m, sl=sl):
                t.matmul(ps[2][0:1, :], sb["ones_col_b"][:], sb["zc_sl"][:, sl, 0:512],
                         start=(m == 0), stop=(m == NFC - 1))
                t.matmul(ps[3][0:1, :], sb["ones_col_b"][:], sb["zc_sl"][:, sl, 512:1024],
                         start=(m == 0), stop=(m == NFC - 1))
                t.matmul(ps[4][0:1, :], sb["ones_col_b"][:], sb["zq_sl"][:, sl, 0:512],
                         start=(m == 0), stop=(m == NFC - 1))
                return t.matmul(ps[5][0:1, :], sb["ones_col_b"][:], sb["zq_sl"][:, sl, 512:1024],
                                start=(m == 0), stop=(m == NFC - 1))
            smm_q.append((smm, sl))
            if len(smm_q) == 2:
                smm0, sl0 = smm_q.pop(0)
                p.step("tensor", smm0, reads=[f"zc{sl0}", f"zq{sl0}", "ones_col_b"],
                       writes=["ps2", "ps3", "ps4", "ps5"])
        smm7, sl7 = smm_q.pop(0)
        p.step("tensor", smm7, reads=[f"zc{sl7}", f"zq{sl7}", "ones_col_b"],
               writes=["ps2", "ps3", "ps4", "ps5"])
        dump("z", "slab", SLAB_NAMES, c=c)
        p.step("vector", lambda v: v.tensor_scalar(sb["rowB"][0:1, 0:512], ps[2][0:1, :],
                                                   1.0 / D, None, ALU.mult),
               reads=["ps2"], writes=["rowB"])
        p.step("vector", lambda v: v.tensor_scalar(sb["rowB"][0:1, 512:1024], ps[3][0:1, :],
                                                   1.0 / D, None, ALU.mult),
               reads=["ps3"], writes=["rowB"])
        p.step("vector", lambda v: v.tensor_scalar(sb["rowA"][0:1, 0:512], ps[4][0:1, :],
                                                   1.0 / D, None, ALU.mult),
               reads=["ps4"], writes=["rowA"])
        p.step("vector", lambda v: v.tensor_scalar(sb["rowA"][0:1, 512:1024], ps[5][0:1, :],
                                                   1.0 / D, None, ALU.mult),
               reads=["ps5"], writes=["rowA"])
        p.step("vector", lambda v: v.tensor_mul(ps[2][0:1, 0:512], sb["rowB"][0:1, 0:512],
                                                sb["rowB"][0:1, 0:512]),
               reads=["rowB"], writes=["ps2"])
        p.step("vector", lambda v: v.tensor_mul(ps[3][0:1, 0:512], sb["rowB"][0:1, 512:1024],
                                                sb["rowB"][0:1, 512:1024]),
               reads=["rowB"], writes=["ps3"])
        p.step("vector", lambda v: v.tensor_sub(sb["rowA"][0:1, 0:512], sb["rowA"][0:1, 0:512],
                                                ps[2][0:1, 0:512]),
               reads=["rowA", "ps2"], writes=["rowA"])
        p.step("vector", lambda v: v.tensor_sub(sb["rowA"][0:1, 512:1024], sb["rowA"][0:1, 512:1024],
                                                ps[3][0:1, 0:512]),
               reads=["rowA", "ps3"], writes=["rowA"])
        p.step("vector", lambda v: v.tensor_scalar(sb["rowA"][:], sb["rowA"][:], 1.0, EPS,
                                                   ALU.mult, ALU.add),
               reads=["rowA"], writes=["rowA"], tag="epsadd_ip")
        p.step("scalar", lambda s: s.activation(sb["rowA"][:], sb["rowA"][:], AF.Sqrt),
               reads=["rowA"], writes=["rowA"], tag="sqrt_ip")
        p.step("vector", lambda v: v.reciprocal(sb["rowA"][:], sb["rowA"][:]),
               reads=["rowA"], writes=["rowA"], tag="rcp_ip")
        p.step("vector", lambda v: v.scalar_tensor_tensor(sb["rowB"][:], sb["rowB"][:], -1.0,
                                                          sb["rowA"][:], ALU.mult, ALU.mult),
               reads=["rowB", "rowA"], writes=["rowB"], tag="negmuistd_ip")
        for n in range(2):
            bA, bB = (4, 5) if n == 0 else (2, 3)
            p.step("tensor", (lambda n=n, bA=bA: lambda t: t.matmul(
                ps[bA][:], sb["ones_row"][:], sb["rowA"][0:1, n * 512:(n + 1) * 512],
                start=True, stop=True))(),
                reads=["ones_row", "rowA"], writes=[f"ps{bA}"])
            p.step("tensor", (lambda n=n, bB=bB: lambda t: t.matmul(
                ps[bB][:], sb["ones_row"][:], sb["rowB"][0:1, n * 512:(n + 1) * 512],
                start=True, stop=True))(),
                reads=["ones_row", "rowB"], writes=[f"ps{bB}"])
            for m in range(NFC):
                s0, s1 = (0, 1) if m % 2 == 0 else (2, 3)
                p.step("vector", (lambda m=m, n=n, bA=bA, s0=s0: lambda v: v.tensor_tensor(
                    sb["scr"][:, s0, :], sb["slab"][:, m, n * 512:(n + 1) * 512],
                    ps[bA][:], ALU.mult))(),
                    reads=[f"slab{m}_{n}", f"ps{bA}"], writes=[f"scr{s0}"])
                p.step("vector", (lambda m=m, n=n, bB=bB, s0=s0, s1=s1: lambda v: v.tensor_tensor(
                    sb["scr"][:, s1, :], sb["scr"][:, s0, :], ps[bB][:], ALU.add))(),
                    reads=[f"scr{s0}", f"ps{bB}"], writes=[f"scr{s1}"])
                p.step("gpsimd", (lambda m=m, s0=s0, s1=s1: lambda g: g.tensor_scalar(
                    sb["scr"][:, s0, :], sb["scr"][:, s1, :], sb["c_lng"][:, m:m + 1],
                    sb["c_lnb"][:, m:m + 1], ALU.mult, ALU.add))(),
                    reads=[f"scr{s1}", "c_lng", "c_lnb"], writes=[f"scr{s0}"])
                p.step("gpsimd", (lambda m=m, n=n, s0=s0: lambda g: g.tensor_tensor(
                    sb["slab"][:, m, n * 512:(n + 1) * 512],
                    sb["slab"][:, m, n * 512:(n + 1) * 512], sb["scr"][:, s0, :], ALU.add))(),
                    reads=[f"scr{s0}", f"slab{m}_{n}"], writes=[f"slab{m}_{n}"], tag="u_ip")
                p.step("scalar", (lambda m=m, n=n: lambda s: s.activation(
                    sb["b16"][:, m, n * 512:(n + 1) * 512],
                    sb["slab"][:, m, n * 512:(n + 1) * 512], AF.Identity))(),
                    reads=[f"slab{m}_{n}"], writes=[f"b16_{m}_{n}"], tag="ub16")

        dump("u", "slab", SLAB_NAMES, c=c)
        dump("ub", "b16", B16_NAMES, cast=True, c=c)
        first = (c == 0)
        def f1load(mi, slot, fresh=first):
            if fresh:
                p.step("gpsimd", (lambda mi=mi, slot=slot: lambda g: g.dma_start(
                    out=sb["f1col"][:, slot], in_=di["f1T"].ap()[mi]))(),
                    writes=[f"f1c{slot}"], dma=f"ldf1_{mi % 8}")
                p.step("sync", (lambda mi=mi, slot=slot: lambda e: e.dma_start(
                    out=f1c_d.ap()[mi], in_=sb["f1col"][:, slot]))(),
                    reads=[f"f1c{slot}"], writes=[f"wcf1_{mi}"], dma="wcw")
            else:
                p.step("sync", (lambda mi=mi, slot=slot: lambda e: e.dma_start(
                    out=sb["f1col"][:, slot], in_=f1c_d.ap()[mi]))(),
                    reads=[f"wcf1_{mi}"], writes=[f"f1c{slot}"], dma=f"ldf1_{mi % 8}")
        def f2load(mi, slot, fresh=first):
            if fresh:
                p.step("gpsimd", (lambda mi=mi, slot=slot: lambda g: g.dma_start(
                    out=sb["f2col"][:, slot], in_=di["f2T"].ap()[mi]))(),
                    writes=[f"f2c{slot}"], dma=f"ldf2_{mi % 4}")
                p.step("sync", (lambda mi=mi, slot=slot: lambda e: e.dma_start(
                    out=f2c_d.ap()[mi], in_=sb["f2col"][:, slot]))(),
                    reads=[f"f2c{slot}"], writes=[f"wcf2_{mi}"], dma="wcw")
            else:
                p.step("sync", (lambda mi=mi, slot=slot: lambda e: e.dma_start(
                    out=sb["f2col"][:, slot], in_=f2c_d.ap()[mi]))(),
                    reads=[f"wcf2_{mi}"], writes=[f"f2c{slot}"], dma=f"ldf2_{slot}")
        for sl2 in range(2):
            fresh = (c == 0 and sl2 == 0)
            to = sl2 * 512
            f1load(0, 0, fresh); f1load(1, 1, fresh)
            for m in range(NMF):
                slot = m % 4
                if m + 2 < NMF:
                    f1load(m + 2, (m + 2) % 4, fresh)
                bank = nxt()
                def mm(t, m=m, to=to, slot=slot, bank=bank):
                    for kc in range(NFC):
                        last = t.matmul(ps[bank][:], sb["f1col"][:, slot, kc, :],
                                        sb["b16"][:, kc, to:to + 512],
                                        start=(kc == 0), stop=(kc == NFC - 1))
                    return last
                p.step("tensor", mm, reads=[f"f1c{slot}"]
                       + [f"b16_{k}_{sl2}" for k in range(NFC)],
                       writes=[f"ps{bank}"])
                sc = 2 * (m % 2)
                p.step("scalar", (lambda m=m, bank=bank, sc=sc: lambda s: s.activation(
                    sb["scr"][:, sc, :], ps[bank][:], AF.Erf,
                    bias=sb["c_f1be"][:, m:m + 1], scale=1.0 / math.sqrt(2.0)))(),
                    reads=[f"ps{bank}", "c_f1be"], writes=[f"scr{sc}"])
                p.step("scalar", (lambda m=m, bank=bank, sc=sc: lambda s: s.activation(
                    sb["scr"][:, sc + 1, :], ps[bank][:], AF.Identity,
                    bias=sb["c_f1bh"][:, m:m + 1], scale=0.5))(),
                    reads=[f"ps{bank}", "c_f1bh"], writes=[f"scr{sc + 1}"])
                p.step("vector", (lambda m=m, sc=sc: lambda v: v.scalar_tensor_tensor(
                    sb["hid"][:, m, :], sb["scr"][:, sc, :], 1.0, sb["scr"][:, sc + 1, :],
                    ALU.add, ALU.mult))(),
                    reads=[f"scr{sc}", f"scr{sc + 1}"], writes=[f"hid{m}"])
            f2load(0, 0, fresh); f2load(1, 1, fresh)
            for m2 in range(NFC):
                slot = m2 % 2
                bank = nxt()
                def mm2(t, m2=m2, to=to, slot=slot, bank=bank):
                    for kc in range(NMF):
                        last = t.matmul(ps[bank][:], sb["f2col"][:, slot, kc, :],
                                        sb["hid"][:, kc, :],
                                        start=(kc == 0), stop=(kc == NMF - 1))
                    return last
                p.step("tensor", mm2, reads=[f"f2c{slot}"] + HID_NAMES, writes=[f"ps{bank}"])
                if m2 + 2 < NFC:
                    f2load(m2 + 2, (m2 + 2) % 2, fresh)
                p.step("vector", (lambda m2=m2, to=to, bank=bank, sl2=sl2: lambda v: v.scalar_tensor_tensor(
                    sb["slab"][:, m2, to:to + 512], ps[bank][:], sb["c_f2b"][:, m2:m2 + 1],
                    sb["slab"][:, m2, to:to + 512], ALU.add, ALU.add))(),
                    reads=[f"ps{bank}", "c_f2b", f"slab{m2}_{sl2}"],
                    writes=[f"slab{m2}_{sl2}"], tag="y_ip")
                p.step("vector", (lambda m2=m2, to=to, sl2=sl2: lambda v: v.tensor_copy(
                    sb["b16"][:, m2, to:to + 512], sb["slab"][:, m2, to:to + 512]))(),
                    reads=[f"slab{m2}_{sl2}"], writes=[f"b16_{m2}_{sl2}"], tag="yb16")

        dump("y", "slab", SLAB_NAMES, c=c)
        dump("hid", "hid", HID_NAMES, cast=True, c=c)
        p.fence(XT_NAMES, SCR_NAMES)
        p.fence(["w_p"], HID_NAMES)
        load_w("pT", "w_p", c)
        xg = di["x_tok"].ap()[bat].rearrange("(g p) d -> g p d", p=128)
        for tg in range(NTG):
            gbank = 4 + (tg % 2)
            def mmg(t, tg=tg, gbank=gbank):
                for kc in range(NFC):
                    t.matmul(ps[gbank][:, 0:E], sb["x_fm"][:, kc, tg * 128:(tg + 1) * 128],
                             sb["gw_b"][:, kc, :], start=(kc == 0), stop=False)
                for kc in range(NFC):
                    t.matmul(ps[gbank][:, 0:E], sb["b16"][:, kc, tg * 128:(tg + 1) * 128],
                             sb["G_b"][:, kc, :], start=False, stop=False)
                return t.matmul(ps[gbank][:, 0:E], sb["ones_row"][:], sb["pbgw_row"][:],
                                start=False, stop=True)
            gbank = 4 + (tg % 2)
            p.step("tensor", (lambda mmg=mmg, gbank=gbank, tg=tg: lambda t: mmg(t))(),
                   reads=[f"b16_{k}_{tg // 4}" for k in range(NFC)] + XFM_NAMES
                   + ["gw_b", "G_b", "ones_row", "pbgw_row"],
                   writes=[f"ps{gbank}"], tag="ggate")
            p.step("vector", (lambda tg=tg, par=par, gbank=gbank: lambda v: v.tensor_copy(
                sb["logit_all"][:, par, tg, :], ps[gbank][:, 0:E]))(),
                reads=[f"ps{gbank}"], writes=[f"pre{par}_{tg}"])
        for tg in range(NTG):
            xs = tg % 2
            p.step("sync", (lambda l0=l0, tg=tg, xs=xs, xg=xg: lambda e: e.dma_start(
                out=sb["xt_sl"][:, xs], in_=xg[l0 // 128 + tg]))(),
                writes=[f"xt{xs}"], dma=f"ldxt{xs}")
            for n in range(2):
                bank = nxt()
                def mmp(t, tg=tg, n=n, bank=bank):
                    for kc in range(NFC):
                        t.matmul(ps[bank][:], sb["b16"][:, kc, tg * 128:(tg + 1) * 128],
                                 sb["w_p"][:, kc, n * 512:(n + 1) * 512],
                                 start=(kc == 0), stop=False)
                    return t.matmul(ps[bank][:], sb["ones_row"][:],
                                    sb["pb_row"][0:1, n * 512:(n + 1) * 512],
                                    start=False, stop=True)
                p.step("tensor", mmp,
                       reads=[f"b16_{k}_{tg // 4}" for k in range(NFC)]
                       + ["w_p", "ones_row", "pb_row"],
                       writes=[f"ps{bank}"])
                p.step("vector", (lambda tg=tg, n=n, xs=xs, bank=bank: lambda v: v.tensor_tensor(
                    sb["slab"][:, tg, n * 512:(n + 1) * 512], ps[bank][:],
                    sb["xt_sl"][:, xs, n * 512:(n + 1) * 512], ALU.add))(),
                    reads=[f"ps{bank}", f"xt{xs}"], writes=[f"slab{tg}_{n}"])
        for tg in range(NTG):
            p.step("vector", (lambda tg=tg, par=par: lambda v: v.tensor_reduce(
                sb["rt_negmu"][:, par, tg:tg + 1], sb["slab"][:, tg, :], AX.X, ALU.add,
                negate=True))(),
                reads=[f"slab{tg}_0", f"slab{tg}_1"], writes=[f"negsum{par}_{tg}"])
        p.step("vector", (lambda par=par: lambda v: v.tensor_scalar(
            sb["rt_negmu"][:, par, :], sb["rt_negmu"][:, par, :],
            1.0 / D, None, ALU.mult))(),
            reads=[f"negsum{par}_{tg}" for tg in range(NTG)], writes=[f"nm{par}"],
            tag="negmu_ip")
        for tg in range(NTG):
            bank = 4 if tg % 2 == 0 else 5
            p.step("scalar", (lambda tg=tg, par=par, bank=bank: lambda s: s.activation(
                ps[bank][:], sb["slab"][:, tg, 0:512], AF.Square,
                bias=sb["rt_negmu"][:, par, tg:tg + 1], accum_out=sb["lnvar"][:, tg:tg + 1]))(),
                reads=[f"slab{tg}_0", f"nm{par}"], writes=[f"ps{bank}", f"lnv{tg}"])
            p.step("scalar", (lambda tg=tg, par=par, bank=bank: lambda s: s.activation(
                ps[bank][:], sb["slab"][:, tg, 512:1024], AF.Square,
                bias=sb["rt_negmu"][:, par, tg:tg + 1], accum_out=sb["lnvar2"][:, tg:tg + 1]))(),
                reads=[f"slab{tg}_1", f"nm{par}"], writes=[f"ps{bank}", f"lnw{tg}"])
        p.step("vector", lambda v: v.tensor_add(sb["lnvar"][:], sb["lnvar"][:], sb["lnvar2"][:]),
               reads=[f"lnv{tg}" for tg in range(NTG)] + [f"lnw{tg}" for tg in range(NTG)],
               writes=["lnvar"], tag="lnvar_ip")
        p.step("vector", lambda v: v.tensor_scalar(sb["lnvar"][:], sb["lnvar"][:],
                                                   1.0 / D, EPS, ALU.mult, ALU.add),
               reads=["lnvar"], writes=["lnvar"], tag="lnvar_ip2")
        p.step("scalar", (lambda par=par: lambda s: s.activation(
            sb["rt_istd"][:, par, :], sb["lnvar"][:], AF.Sqrt))(),
            reads=["lnvar"], writes=[f"st{par}"])
        p.step("vector", (lambda par=par: lambda v: v.reciprocal(
            sb["rt_istd"][:, par, :], sb["rt_istd"][:, par, :]))(),
            reads=[f"st{par}"], writes=[f"st{par}"], tag="istd_ip")
        for tg in range(NTG):
            p.step("vector", (lambda tg=tg, par=par: lambda v: v.tensor_scalar(
                sb["slab"][:, tg, :], sb["slab"][:, tg, :],
                sb["rt_negmu"][:, par, tg:tg + 1], sb["rt_istd"][:, par, tg:tg + 1],
                ALU.add, ALU.mult))(),
                reads=[f"slab{tg}_0", f"slab{tg}_1", f"nm{par}", f"st{par}"],
                writes=[f"slab{tg}_0", f"slab{tg}_1"], tag="x1_ip")
        p.step("sync", (lambda c=c: lambda e: e.dma_start(
            out=x1sp.ap().rearrange("(cc g p) d -> cc p g d", cc=NCHUNK, p=128)[c],
            in_=sb["slab"][:]))(),
            reads=SLAB_NAMES, writes=["x1sp"], dma="spw")
        for m in range(NFC):
            p.step("scalar", (lambda m=m: lambda s: s.activation(
                sb["b16"][:, m, :], sb["slab"][:, m, :], AF.Identity))(),
                reads=[f"slab{m}_0", f"slab{m}_1"],
                writes=[f"b16_{m}_0", f"b16_{m}_1"])
        p.step("sync", (lambda c=c: lambda e: e.dma_start(
            out=x1spb.ap().rearrange("(cc g p) d -> cc p g d", cc=NCHUNK, p=128)[c],
            in_=sb["b16"][:]))(),
            reads=B16_NAMES, writes=["x1spb"], dma="spwb")

    plan_routing_dve(n_chunks - 1)
    plan_routing_pe(n_chunks - 1)
    p.step("vector", lambda v: v.tensor_scalar(sb["pos_all"][:], sb["pos_all"][:],
                                               -1.0, None, ALU.add),
           reads=[f"posr{g}" for g in range(n_chunks * NTG)], writes=["pos_all"],
           tag="pos_ip")

    if dbg:
        dbgs = {}
        for dn, tile in [("dbg_pos", "pos_all"), ("dbg_eid", "eid_all"),
                         ("dbg_gp", "gp_all"), ("dbg_fm", "fm_all")]:
            shp = list(sb[tile].shape)
            dbgs[dn] = nc.dram_tensor(dn, shp, F32, kind="ExternalOutput")
            p.step("sync", (lambda dn=dn, tile=tile: lambda e: e.dma_start(
                out=dbgs[dn].ap(), in_=sb[tile][:]))(),
                reads=[tile], dma="sto")
        dbgs["dbg_cnt"] = nc.dram_tensor("dbg_cnt", [E, 1], F32, kind="ExternalOutput")
        p.step("sync", lambda e: e.dma_start(out=dbgs["dbg_cnt"].ap(), in_=sb["carry_c"][:]),
               reads=["carry_c"], dma="sto")
        if n_chunks < NCHUNK:
            p.wait_tokens("sync", [("spw", p.dma_counts.get("spw", 0)),
                                   ("sto", p.dma_counts.get("sto", 0)),
                                   ("dmp", p.dma_counts.get("dmp", 0))])
            with nc.Block() as block:
                p.emit(block)
            return nc
    elif n_chunks < NCHUNK:
        p.wait_tokens("sync", [("spw", p.dma_counts.get("spw", 0))])
        with nc.Block() as block:
            p.emit(block)
        return nc

    p.step("sync", lambda e: e.dma_start(out=cnt_in.ap().rearrange("a b -> b a"), in_=sb["carry_c"][:]),
           reads=["carry_c"], writes=["cnt_in"], dma="cnt")
    p.step("gpsimd", lambda g: g.collective_compute(
        "AllGather", ALU.bypass, replica_groups=RG,
        ins=[flatten_dims_for_collective(cnt_in.ap())],
        outs=[flatten_dims_for_collective(cnt_out.ap())]),
        reads=["cnt_in"], writes=["cnt_out"], tag="ag_cnt")
    p.step("sync", lambda e: e.dma_start(out=sb["cnts"][:], in_=cnt_out.ap()),
           reads=["cnt_out"], writes=["cnts"], dma="cnt")
    p.step("tensor", lambda t: t.matmul(psT[0:1, 0:E], sb["trimask"][:], sb["cnts"][:],
                                        start=True, stop=True),
           reads=["trimask", "cnts"], writes=["psT"])
    p.step("vector", lambda v: v.tensor_copy(sb["orow"][:], psT[0:1, 0:E]),
           reads=["psT"], writes=["orow"])
    p.step("tensor", lambda t: t.matmul(psT[:, 0:E], sb["ones_row"][:], sb["orow"][:],
                                        start=True, stop=True),
           reads=["ones_row", "orow"], writes=["psT"])
    p.step("vector", lambda v: v.tensor_scalar(sb["oadj_bc"][:], sb["iota8"][:],
                                               float(CAP), None, ALU.mult),
           reads=["iota8"], writes=["oadj_bc"])
    p.step("vector", lambda v: v.tensor_tensor(sb["oadj_bc"][:], sb["oadj_bc"][:],
                                               psT[:, 0:E], ALU.add),
           reads=["psT", "oadj_bc"], writes=["oadj_bc"], tag="oadj_ip")
    for g in range(NG):
        j = g % NTG
        p.step("vector", (lambda g=g, j=j: lambda v: v.tensor_tensor(
            sb["scr8b"][:, 0, j, :], sb["fm_all"][:, g, :], sb["oadj_bc"][:], ALU.mult))(),
            reads=[f"fm{g}", "oadj_bc"], writes=[f"s8b0_{j}"])
        p.step("vector", (lambda g=g, j=j: lambda v: v.tensor_reduce(
            sb["o_all"][:, g:g + 1], sb["scr8b"][:, 0, j, :], AX.X, ALU.add))(),
            reads=[f"s8b0_{j}"], writes=[f"oa{g}"])
    oreads = [f"oa{g}" for g in range(NG)]
    p.step("vector", lambda v: v.tensor_tensor(sb["slot_f"][:], sb["pos_all"][:],
                                               sb["o_all"][:], ALU.add),
           reads=["pos_all"] + oreads, writes=["slot_f"])
    p.step("vector", lambda v: v.tensor_scalar(sb["o_all"][:], sb["eid_all"][:],
                                               float(CAP), None, ALU.mult),
           reads=["eid_all"], writes=["o_all"])
    p.step("vector", lambda v: v.tensor_tensor(sb["o_all"][:], sb["slot_f"][:],
                                               sb["o_all"][:], ALU.subtract),
           reads=["slot_f", "o_all"], writes=["o_all"], tag="posg_ip")
    p.step("vector", lambda v: v.tensor_scalar(sb["o_all"][:], sb["o_all"][:],
                                               float(CAP), None, ALU.is_lt),
           reads=["o_all"], writes=["o_all"], tag="keep_ip")
    p.step("vector", lambda v: v.tensor_tensor(sb["gpk_all"][:], sb["gp_all"][:],
                                               sb["o_all"][:], ALU.mult),
           reads=["gp_all", "o_all"], writes=["gpk_all"])
    p.step("vector", lambda v: v.tensor_scalar(sb["pos_all"][:], sb["o_all"][:],
                                               -float(NSLOT), float(NSLOT), ALU.mult, ALU.add),
           reads=["o_all"], writes=["pos_all"], tag="trash_term")
    p.step("vector", lambda v: v.tensor_tensor(sb["slot_f"][:], sb["slot_f"][:],
                                               sb["o_all"][:], ALU.mult),
           reads=["slot_f", "o_all"], writes=["slot_f"], tag="slot_ip")
    p.step("vector", lambda v: v.tensor_tensor(sb["slot_f"][:], sb["slot_f"][:],
                                               sb["pos_all"][:], ALU.add),
           reads=["slot_f", "pos_all"], writes=["slot_f"], tag="slot_ip2")
    p.step("vector", lambda v: v.tensor_copy(sb["slot_i"][:], sb["slot_f"][:]),
           reads=["slot_f"], writes=["slot_i"])
    x1r = x1sp.ap().rearrange("(g p) d -> g p d", p=128)
    x1rb = x1spb.ap().rearrange("(g p) d -> g p d", p=128)
    p.fence(["sc_b0", "sc_b1"], B_NAMES)
    p.fence(["sc_b2", "sc_b3", "sc_b4", "sc_b5"],
            A_NAMES + ["zc0", "zc1", "zq0", "zq1"])
    p.wait_tokens("gpsimd", [disp_zeroed])
    for g in range(NG):
        slot = g % 6
        sct = (sb["sc_b"][:, slot] if slot < 2 else
               sb["sc_bB"][:, slot - 2] if slot < 4 else sb["sc_bC"][:, slot - 4])
        p.step("sync", (lambda g=g, sct=sct: lambda e: e.dma_start(
            out=sct, in_=x1rb[g]))(),
            reads=["x1spb"], writes=[f"sc_b{slot}"], dma=f"scb{slot}")
        p.step("gpsimd", (lambda g=g, sct=sct: lambda e: e.indirect_dma_start(
            out=disp.ap(), out_offset=IndirectOffsetOnAxis(sb["slot_i"][:, g:g + 1], 0),
            in_=sct, in_offset=None))(),
            reads=[f"sc_b{slot}", "slot_i"], writes=[], dma=f"scbi{slot}")

    p.wait_tokens("gpsimd", [(f"scbi{s}", p.dma_counts.get(f"scbi{s}", 0))
                             for s in range(6)])
    p.step("gpsimd", lambda g: g.collective_compute(
        "ReduceScatter", ALU.add, replica_groups=RG,
        ins=[flatten_dims_for_collective(disp.ap()[0:NSLOT, :])],
        outs=[flatten_dims_for_collective(disp_my.ap())]),
        reads=["disp"], writes=["disp_my"], tag="rs")
    if dbg:
        d_dm = nc.dram_tensor("d_dispmy", [CAP, D], F32, kind="ExternalOutput")
        p.step("gpsimd", lambda g: g.dma_start(out=d_dm.ap(), in_=disp_my.ap()),
               reads=["disp_my"], dma="dmp2")
        d_sf = nc.dram_tensor("d_slotf", [128, NG], F32, kind="ExternalOutput")
        p.step("sync", lambda e: e.dma_start(out=d_sf.ap(), in_=sb["slot_f"][:]),
               reads=["slot_f"], dma="dmp2")
        d_gk = nc.dram_tensor("d_gpk", [128, NG], F32, kind="ExternalOutput")
        p.step("sync", lambda e: e.dma_start(out=d_gk.ap(), in_=sb["gpk_all"][:]),
               reads=["gpk_all"], dma="dmp2")
        d_or = nc.dram_tensor("d_orow", [1, E], F32, kind="ExternalOutput")
        p.step("sync", lambda e: e.dma_start(out=d_or.ap(), in_=sb["orow"][:]),
               reads=["orow"], dma="dmp2")
    p.fence(MFM_NAMES + EOFM_NAMES + ["eot0", "eot1"],
            XFM_NAMES + ["w_a", "w_b"])
    p.fence(HE_NAMES, SLAB_NAMES + B16_NAMES)
    for fc in range(NFC):
        p.step("sync", (lambda fc=fc: lambda e: e.dma_start(
            out=sb["moe_fm"][:, fc, :], in_=disp_my.ap()[:, fc * 128:(fc + 1) * 128],
            transpose=True))(),
            reads=["disp_my"], writes=[f"mfm{fc}"], dma=f"mtk{fc}")
    def w1load(mi, slot):
        p.step("gpsimd", (lambda mi=mi, slot=slot: lambda g: g.dma_start(
            out=sb["f1e"][:, slot], in_=di["w1r"].ap()[mi]))(),
            writes=[f"f1e{slot}"], dma=f"lde{slot % 4}")
    def w2load(mi, slot):
        p.step("gpsimd", (lambda mi=mi, slot=slot: lambda g: g.dma_start(
            out=sb["f2col"][:, slot], in_=di["w2r"].ap()[mi]))(),
            writes=[f"f2c{slot}"], dma=f"ldf2_{slot}")
    p.fence([f"f1e{s}" for s in range(16)], HID_NAMES + ["w_p"])
    for mi in range(16):
        w1load(mi, mi)
    for m in range(NMF):
        slot = m % 16
        for n0, nw in ((0, 512), (512, 128)):
            bank = nxt()
            def mm(t, m=m, slot=slot, n0=n0, nw=nw, bank=bank):
                for kc in range(NFC):
                    last = t.matmul(ps[bank][:, 0:nw], sb["f1e"][:, slot, kc, :],
                                    sb["moe_fm"][:, kc, n0:n0 + nw],
                                    start=(kc == 0), stop=(kc == NFC - 1))
                return last
            p.step("tensor", mm, reads=[f"f1e{slot}"] + MFM_NAMES, writes=[f"ps{bank}"])
            p.step("scalar", (lambda m=m, bank=bank, nw=nw: lambda s: s.activation(
                sb["scr"][:, 0, 0:nw], ps[bank][:, 0:nw], AF.Square,
                bias=sb["c_b1r"][:, m:m + 1]))(),
                reads=[f"ps{bank}", "c_b1r"], writes=["scr0"])
            p.step("scalar", (lambda m=m, bank=bank, nw=nw: lambda s: s.activation(
                sb["scr"][:, 1, 0:nw], ps[bank][:, 0:nw], AF.Identity,
                bias=sb["c_b1r"][:, m:m + 1]))(),
                reads=[f"ps{bank}", "c_b1r"], writes=["scr1"])
            p.step("vector", (lambda nw=nw: lambda v: v.tensor_scalar(
                sb["scr"][:, 2, 0:nw], sb["scr"][:, 0, 0:nw], GC2, GC1, ALU.mult, ALU.add))(),
                reads=["scr0"], writes=["scr2"])
            p.step("vector", (lambda nw=nw: lambda v: v.tensor_tensor(
                sb["scr"][:, 0, 0:nw], sb["scr"][:, 2, 0:nw], sb["scr"][:, 1, 0:nw], ALU.mult))(),
                reads=["scr1", "scr2"], writes=["scr0"])
            p.step("scalar", (lambda nw=nw: lambda s: s.activation(
                sb["scr"][:, 2, 0:nw], sb["scr"][:, 0, 0:nw], AF.Tanh))(),
                reads=["scr0"], writes=["scr2"])
            p.step("vector", (lambda nw=nw: lambda v: v.tensor_scalar(
                sb["scr"][:, 3, 0:nw], sb["scr"][:, 2, 0:nw], 0.5, 0.5, ALU.mult, ALU.add))(),
                reads=["scr2"], writes=["scr3"])
            p.step("vector", (lambda m=m, n0=n0, nw=nw: lambda v: v.tensor_tensor(
                sb["he"][:, m, n0:n0 + nw], sb["scr"][:, 3, 0:nw], sb["scr"][:, 1, 0:nw],
                ALU.mult))(),
                reads=["scr1", "scr3"], writes=[f"he{m}"])
        if m + 16 < NMF:
            w1load(m + 16, slot)
    w2load(0, 0); w2load(1, 1)
    for m2 in range(NFC):
        slot = m2 % 2
        for n0, nw in ((0, 512), (512, 128)):
            bank = nxt()
            def mm2(t, m2=m2, slot=slot, n0=n0, nw=nw, bank=bank):
                for kc in range(NMF):
                    last = t.matmul(ps[bank][:, 0:nw], sb["f2col"][:, slot, kc, :],
                                    sb["he"][:, kc, n0:n0 + nw],
                                    start=(kc == 0), stop=(kc == NMF - 1))
                return last
            p.step("tensor", mm2, reads=[f"f2c{slot}"] + HE_NAMES, writes=[f"ps{bank}"])
            p.step("vector", (lambda m2=m2, n0=n0, nw=nw, bank=bank: lambda v: v.tensor_scalar(
                sb["eo_fm"][:, m2, n0:n0 + nw], ps[bank][:, 0:nw],
                sb["c_b2r"][:, m2:m2 + 1], None, ALU.add))(),
                reads=[f"ps{bank}", "c_b2r"], writes=[f"eo{m2}"])
        if m2 + 2 < NFC:
            w2load(m2 + 2, (m2 + 2) % 2)
    etb = [(psT[:, 0:128], "psT"), (ps[0][:, 0:128], "ps0"), (ps[1][:, 0:128], "ps1")]
    for s5 in range(5):
        slot = s5 % 2
        for fc in range(NFC):
            buf, bn = etb[(s5 * NFC + fc) % 3]
            p.step("tensor", (lambda s5=s5, fc=fc, buf=buf: lambda t: t.transpose(
                buf, sb["eo_fm"][:, fc, s5 * 128:(s5 + 1) * 128], sb["ident_f"][:]))(),
                reads=[f"eo{fc}", "ident_f"], writes=[bn])
            p.step("vector", (lambda fc=fc, slot=slot, buf=buf: lambda v: v.tensor_copy(
                sb["eo_tok"][:, slot, fc * 128:(fc + 1) * 128], buf))(),
                reads=[bn], writes=[f"eot{slot}"])
        p.step("sync", (lambda s5=s5, slot=slot: lambda e: e.dma_start(
            out=eo_my.ap().rearrange("(s p) d -> s p d", p=128)[s5], in_=sb["eo_tok"][:, slot]))(),
            reads=[f"eot{slot}"], writes=[], dma=f"eow{slot}")

    p.wait_tokens("gpsimd", [("eow0", p.dma_counts.get("eow0", 0)),
                             ("eow1", p.dma_counts.get("eow1", 0))])
    p.step("gpsimd", lambda g: g.collective_compute(
        "AllGather", ALU.bypass, replica_groups=RG,
        ins=[flatten_dims_for_collective(eo_my.ap())],
        outs=[flatten_dims_for_collective(eo_full.ap()[0:NSLOT, :])]),
        writes=["eo_full"], tag="ag_eo")

    if dbg:
        p.wait_tokens("sync", [("eow0", p.dma_counts.get("eow0", 0)),
                               ("eow1", p.dma_counts.get("eow1", 0))])
        d_eo = nc.dram_tensor("d_eomy", [CAP, D], F32, kind="ExternalOutput")
        p.step("sync", lambda e: e.dma_start(out=d_eo.ap(), in_=eo_my.ap()),
               reads=[], dma="dmp2")
        d_ef = nc.dram_tensor("d_eofull", [NSLOT, D], F32, kind="ExternalOutput")
        p.step("sync", lambda e: e.dma_start(out=d_ef.ap(), in_=eo_full.ap()[0:NSLOT, :]),
               reads=["eo_full"], dma="dmp2")
    p.fence(Y2_NAMES, [f"f1e{s}" for s in range(16)])
    p.fence([f"x1q{s}" for s in range(8)],
            A_NAMES + ZCQ_NAMES + B_NAMES + SCR_NAMES + XT_NAMES + ["w_b"]
            + [f"sc_b{s}" for s in range(6)])
    p.fence([f"egq{s}" for s in range(4)], ["w_a"] + MFM_NAMES + EOFM_NAMES)
    p.step("vector", lambda v: v.memset(sb["eogq"][:, 0], 0.0), writes=["egq0"])
    p.step("sync", lambda e: e.dma_start(out=eo_full.ap()[NSLOT:NSLOT + 1, :],
                                         in_=sb["eogq"][0:1, 0, :]),
           reads=["egq0"], writes=["eo_trash"], dma="etz")
    p.wait_tokens("gpsimd", [("etz", p.dma_counts["etz"])])
    outr = out_d.ap().rearrange("b (g p) d -> (b g) p d", p=128)
    for c in range(NCHUNK):
        for tg in range(NTG):
            g = c * NTG + tg
            slot = g % 8
            x1t = sb["x1q"][:, slot] if slot < 4 else sb["x1q2"][:, slot - 4]
            p.step("sync", (lambda g=g, x1t=x1t: lambda e: e.dma_start(
                out=x1t, in_=x1r[g]))(),
                reads=["x1sp"], writes=[f"x1q{slot}"], dma=f"x1r{slot % 4}")
            eslot = g % 4
            p.step("gpsimd", (lambda g=g, eslot=eslot: lambda e: e.indirect_dma_start(
                out=sb["eogq"][:, eslot], out_offset=None,
                in_=eo_full.ap(), in_offset=IndirectOffsetOnAxis(sb["slot_i"][:, g:g + 1], 0)))(),
                reads=["eo_full", "slot_i", f"egq{eslot}"], writes=[f"egq{eslot}"],
                dma=f"eg{eslot}")
            p.step("vector", (lambda g=g, tg=tg, eslot=eslot, x1t=x1t: lambda v: v.scalar_tensor_tensor(
                sb["y2"][:, tg, :], sb["eogq"][:, eslot, :], sb["gpk_all"][:, g:g + 1],
                x1t, ALU.mult, ALU.add))(),
                reads=[f"egq{eslot}", f"x1q{slot}", "gpk_all"], writes=[f"y2_{tg}"])
            p.step("vector", (lambda tg=tg: lambda v: v.tensor_reduce(
                sb["rt_negmu"][:, 0, tg:tg + 1], sb["y2"][:, tg, :], AX.X, ALU.add,
                negate=True))(),
                reads=[f"y2_{tg}"], writes=[f"ns2{tg}"])
        p.step("vector", lambda v: v.tensor_scalar(sb["rt_negmu"][:, 0, :], sb["rt_negmu"][:, 0, :],
                                                   1.0 / D, None, ALU.mult),
               reads=[f"ns2{tg}" for tg in range(NTG)], writes=["nm0"], tag="nm2_ip")
        for tg in range(NTG):
            bank = 4 if tg % 2 == 0 else 5
            p.step("scalar", (lambda tg=tg, bank=bank: lambda s: s.activation(
                ps[bank][:], sb["y2"][:, tg, 0:512], AF.Square,
                bias=sb["rt_negmu"][:, 0, tg:tg + 1], accum_out=sb["lnvar"][:, tg:tg + 1]))(),
                reads=[f"y2_{tg}", "nm0"], writes=[f"ps{bank}", f"lv2{tg}"])
            p.step("scalar", (lambda tg=tg, bank=bank: lambda s: s.activation(
                ps[bank][:], sb["y2"][:, tg, 512:1024], AF.Square,
                bias=sb["rt_negmu"][:, 0, tg:tg + 1], accum_out=sb["lnvar2"][:, tg:tg + 1]))(),
                reads=[f"y2_{tg}", "nm0"], writes=[f"ps{bank}", f"lw2{tg}"])
        p.step("vector", lambda v: v.tensor_add(sb["lnvar"][:], sb["lnvar"][:], sb["lnvar2"][:]),
               reads=[f"lv2{tg}" for tg in range(NTG)] + [f"lw2{tg}" for tg in range(NTG)],
               writes=["lnvar"], tag="lv2_ip")
        p.step("vector", lambda v: v.tensor_scalar(sb["lnvar"][:], sb["lnvar"][:],
                                                   1.0 / D, EPS, ALU.mult, ALU.add),
               reads=["lnvar"], writes=["lnvar"], tag="lv2_ip2")
        p.step("scalar", lambda s: s.activation(sb["rt_istd"][:, 0, :], sb["lnvar"][:], AF.Sqrt),
               reads=["lnvar"], writes=["st0"])
        p.step("vector", lambda v: v.reciprocal(sb["rt_istd"][:, 0, :], sb["rt_istd"][:, 0, :]),
               reads=["st0"], writes=["st0"], tag="istd2_ip")
        for tg in range(NTG):
            g = c * NTG + tg
            p.step("vector", (lambda tg=tg: lambda v: v.tensor_scalar(
                sb["y2"][:, tg, :], sb["y2"][:, tg, :],
                sb["rt_negmu"][:, 0, tg:tg + 1], sb["rt_istd"][:, 0, tg:tg + 1],
                ALU.add, ALU.mult))(),
                reads=[f"y2_{tg}", "nm0", "st0"], writes=[f"y2_{tg}"], tag="ln2_ap")
            p.step("sync", (lambda g=g, tg=tg: lambda e: e.dma_start(
                out=outr[g], in_=sb["y2"][:, tg, :]))(),
                reads=[f"y2_{tg}"], writes=[], dma=f"sto{tg % 2}")

    toks = [(f"sto{s}", p.dma_counts[f"sto{s}"]) for s in range(2)]
    if "dmp2" in p.dma_counts:
        toks.append(("dmp2", p.dma_counts["dmp2"]))
    p.wait_tokens("sync", toks)
    with nc.Block() as block:
        p.emit(block)
    return nc


def kernel(**inputs):
    per_core = host_prepare(inputs)
    nc = build_nc()
    res = run_bass_kernel_spmd(nc, per_core, core_ids=list(range(NCORES)))
    out = np.concatenate([np.asarray(r["out"]) for r in res.results], axis=0)
    return out.reshape(B, L, D)
